# version 04
# baseline (speedup 1.0000x reference)
import contextlib
import numpy as np

import concourse.bass as bass
import concourse.mybir as mybir
from concourse.bass_utils import run_bass_kernel_spmd

F32 = mybir.dt.float32
BF16 = mybir.dt.bfloat16
I32 = mybir.dt.int32
AF = mybir.ActivationFunctionType
ALU = mybir.AluOpType
AX = mybir.AxisListType

NEG = -1.0e30
EPS = 1e-5
BASE = 10000.0
SIZES = dict(T=4096, D=1024, N=64, NH=16, HD=64, RD=32, L=256, NE=7, I=2048)


class Buf:
    __slots__ = ("w", "r")

    def __init__(self):
        self.w = None
        self.r = {}


class Prog:
    ENGS = ("tensor", "vector", "scalar", "gpsimd", "sync")
    QS = ("sync", "scalar", "gpsimd")
    NDS = 6

    def __init__(self, nc, es):
        self.nc = nc
        self.sem = {}
        self.cnt = {}
        self.waited = {e: {} for e in self.ENGS}
        for name in self.ENGS:
            self.sem[name] = es.enter_context(nc.semaphore(f"sm_{name}"))
            self.cnt[name] = 0
        self.drr = {}
        for q in self.QS:
            for i in range(self.NDS):
                key = f"d_{q}{i}"
                self.sem[key] = es.enter_context(nc.semaphore(key))
                self.cnt[key] = 0
            self.drr[q] = 0
        self.eng = {"tensor": nc.tensor, "vector": nc.vector,
                    "scalar": nc.scalar, "gpsimd": nc.gpsimd, "sync": nc.sync}

    def _wait(self, ename, toks):
        w = self.waited[ename]
        need = {}
        for t in toks:
            if t is None:
                continue
            s, v = t
            if v > need.get(s, -1):
                need[s] = v
        for s, v in need.items():
            if w.get(s, 0) < v:
                self.eng[ename].wait_ge(self.sem[s], v)
                w[s] = v

    @staticmethod
    def _toks(ins, outs):
        toks = []
        for b in ins:
            if b is not None and b.w is not None:
                toks.append(b.w)
        for b in outs:
            if b is None:
                continue
            if b.w is not None:
                toks.append(b.w)
            toks.extend(b.r.items())
        return toks

    def _finish(self, ins, outs, tok):
        s, v = tok
        for b in ins:
            if b is not None and b.r.get(s, 0) < v:
                b.r[s] = v
        for b in outs:
            if b is not None:
                b.w = tok
                b.r = {}

    def op(self, ename, method, outs, ins, *args, waits=(), **kw):
        self._wait(ename, self._toks(ins, outs) + list(waits))
        inst = getattr(self.eng[ename], method)(*args, **kw)
        self.cnt[ename] += 1
        tok = (ename, self.cnt[ename])
        inst.then_inc(self.sem[ename], 1)
        self._finish(ins, outs, tok)
        return tok

    def _dma_sem(self, q):
        i = self.drr[q]
        self.drr[q] = (i + 1) % self.NDS
        key = f"d_{q}{i}"
        if self.cnt[key]:
            self._wait(q, [(key, self.cnt[key])])
        return key

    def dma(self, q, out_ap, in_ap, outs, ins, waits=(), **kw):
        self._wait(q, self._toks(ins, outs) + list(waits))
        key = self._dma_sem(q)
        inst = self.eng[q].dma_start(out=out_ap, in_=in_ap, **kw)
        inst.then_inc(self.sem[key], 16)
        self.cnt[key] += 16
        tok = (key, self.cnt[key])
        self._finish(ins, outs, tok)
        return tok

    def barrier(self):
        toks = [(e, self.cnt[e]) for e in self.ENGS]
        toks += [(f"d_{q}{i}", self.cnt[f"d_{q}{i}"])
                 for q in self.QS for i in range(self.NDS)]
        for e in self.ENGS:
            self._wait(e, toks)


def build_program(S, debug=False):
    T, D, N = S["T"], S["D"], S["N"]
    NH, HD, RD, L = S["NH"], S["HD"], S["RD"], S["L"]
    NE, I = S["NE"], S["I"]
    DH = D // 2
    NB = T // 128
    NOB = NB // 2
    TO = NOB * 128
    CH0 = min(512, T)
    NCH0 = T // CH0
    TPC = CH0 // 128
    CH1 = min(1024, T)
    NCH1 = T // CH1
    QG = min(4, NOB)
    NQG = NOB // QG
    KD = D // 128
    KL = L // 128
    NOPE = HD - RD
    RH = RD // 2
    ICH = 512
    NICH = I // ICH
    TF = min(512, TO)
    NTF = TO // TF
    NTT = TO // 128
    KDH = DH // 128

    nc = bass.Bass("TRN2", target_bir_lowering=False)

    def din(name, shape, dt=F32):
        return nc.dram_tensor(name, shape, dt, kind="ExternalInput").ap()

    kscr = "ExternalOutput" if debug else "Internal"

    def dsc(name, shape, dt):
        return nc.dram_tensor(name, shape, dt, kind=kscr).ap()

    x_full = din("x_full", [T, D])
    par = din("par", [1, 1], I32)
    pselin = din("pselin", [128, 1])
    pinvin = din("pinvin", [128, 1])
    tokin = din("tok", [1, 1], I32)
    win_km = din("win_km", [D, D + DH])
    convw_t = din("convw_t", [128, KD * 4])
    convb_t = din("convb_t", [128, KD])
    wdt_km = din("wdt_km", [D, DH])
    bdt_t = din("bdt_t", [128, KDH])
    a_log = din("a_log", [1, N])
    dp_t = din("dp_t", [128, KDH])
    wb_km = din("wb_km", [D, N])
    wc_km = din("wc_km", [D, N])
    wout_km = din("wout_km", [D, D])
    ln_g = din("ln_g", [1, D])
    ln_b = din("ln_b", [1, D])
    ln1_w = din("ln1_w", [1, D])
    ln2_w = din("ln2_w", [1, D])
    kvd_km = din("kvd_km", [D, L])
    qd_km = din("qd_km", [D, L])
    kun_km = din("kun_km", [L, NH * NOPE])
    qun_km = din("qun_km", [L, NH * NOPE])
    krot_km = din("krot_km", [D, NH * RD])
    qrot_km = din("qrot_km", [L, NH * RD])
    vu_km = din("vu_km", [L, NH * HD])
    ow_km = din("ow_km", [D, D])
    cos_rep = din("cos_rep", [128, T])
    sin_rep = din("sin_rep", [128, T])
    amask = din("amask", [QG * 2 * 128, 128])
    router_km = din("router_km", [D, NE])
    rbias = din("rbias", [1, NE])
    gate_km = din("gate_km", [D, (NE + 1) * I])
    up_km = din("up_km", [D, (NE + 1) * I])
    down_km = din("down_km", [(NE + 1) * I, D])
    ident_in = din("ident_in", [128, 128], BF16)

    out_own = nc.dram_tensor("out_own", [TO, D], F32, kind="ExternalOutput").ap()

    z_lin = dsc("z_lin", [DH, T], BF16)
    dt_lin = dsc("dt_lin", [DH, T], F32)
    xc_lin = dsc("xc_lin", [DH, T], F32)
    b_lin = dsc("b_lin", [N, T], F32)
    c_lin = dsc("c_lin", [N, T], F32)
    k_lin = dsc("k_lin", [NH * HD, T], BF16)
    vt_lin = dsc("vt_lin", [T, NH * HD], BF16)
    q_all = dsc("q_all", [NH * HD, T], BF16)
    arow_d = nc.dram_tensor("arow_d", [1, N], F32).ap()
    dbg = {}
    if debug:
        for nm, shape in [("d_y", [DH, T]), ("d_x1", [TO, D]),
                          ("d_x2", [TO, D]), ("d_wgt", [TO, NE]),
                          ("d_s", [TO, D]), ("d_g", [TO, D]),
                          ("d_yg", [D, TO])]:
            dbg[nm] = nc.dram_tensor(nm, shape, F32, kind="ExternalOutput").ap()
    _shr = {} if S.get("local_shared") else {"addr_space": "Shared"}
    ygA = nc.dram_tensor("ygA", [DH, T], BF16, **_shr).ap()
    ygB = nc.dram_tensor("ygB", [DH, T], BF16, **_shr).ap()
    flA = nc.dram_tensor("flA", [1, 1], I32, **_shr).ap()
    flB = nc.dram_tensor("flB", [1, 1], I32, **_shr).ap()

    dram_bufs = {}

    def db(ap):
        key = ap.tensor.name
        if key not in dram_bufs:
            dram_bufs[key] = Buf()
        return dram_bufs[key]

    def mk_sb(stack, name, shape, dt):
        t = stack.enter_context(nc.sbuf_tensor(name, shape, dt))
        return t, Buf()

    def mk_ps(stack, name, shape, dt=F32):
        t = stack.enter_context(nc.psum_tensor(name, shape, dt))
        return t, Buf()

    es = contextlib.ExitStack()
    with es:
        p = Prog(nc, es)

        def wload(stack, name, dram_ap, C_, q="gpsimd", dt=BF16):
            K_ = dram_ap.shape[0]
            nk = K_ // 128
            t = stack.enter_context(nc.sbuf_tensor(name, [128, nk * C_], dt))
            b = Buf()
            src = dram_ap.rearrange("(k p) c -> p k c", p=128)
            dst = t[:, :].rearrange("p (k c) -> p k c", c=C_)
            p.dma(q, dst, src, [b], [db(dram_ap)])

            def sl(kt, c0, c1):
                return t[:, kt * C_ + c0:kt * C_ + c1]
            return t, b, sl
        pl_sem = es.enter_context(nc.semaphore("pl_sem"))
        blk = es.enter_context(nc.Block())

        def sb(name, shape, dt):
            return mk_sb(es, name, shape, dt)

        ident_sb, ident_b = sb("ident", [128, 128], BF16)
        p.dma("sync", ident_sb[:, :], ident_in[:, :], [ident_b], [db(ident_in)])
        par_sb, par_b = sb("par_t", [1, 1], I32)
        p.dma("sync", par_sb[:, :], par[:, :], [par_b], [db(par)])
        tok_sb, tok_b = sb("tok_t", [1, 1], I32)
        p.dma("sync", tok_sb[:, :], tokin[:, :], [tok_b], [db(tokin)])
        fl_sb, fl_b = sb("fl_t", [1, 2], I32)
        acol_sb, acol_b = sb("acol", [128, N], F32)
        arow_sb, arow_b = sb("arow", [1, N], F32)
        p.dma("sync", arow_sb[:, :], a_log[:, :], [arow_b], [db(a_log)])
        p.op("scalar", "activation", [arow_b], [arow_b],
             arow_sb[:, :], arow_sb[:, :], AF.Exp)
        p.op("vector", "tensor_scalar_mul", [arow_b], [arow_b],
             arow_sb[:, :], arow_sb[:, :], -1.0)
        p.dma("sync", arow_d[:, :], arow_sb[:, :], [db(arow_d)], [arow_b])
        p.dma("sync", acol_sb[:, :], arow_d.partition_broadcast(128),
              [acol_b], [db(arow_d)])
        dp_sb, dp_b = sb("dpc", [128, KDH], F32)
        p.dma("sync", dp_sb[:, :], dp_t[:, :], [dp_b], [db(dp_t)])
        rg_par = es.enter_context(nc.sync.register("rg_par"))
        p.op("sync", "load", [], [par_b], rg_par, par_sb[0:1, 0:1])
        psel_sb, psel_b = sb("psel", [128, 1], F32)
        pinv_sb, pinv_b = sb("pinv", [128, 1], F32)
        p.dma("sync", psel_sb[:, :], pselin[:, :], [psel_b], [db(pselin)])
        p.dma("sync", pinv_sb[:, :], pinvin[:, :], [pinv_b], [db(pinvin)])
        _selholder = {}

        def par_select(out_ap, out_b, a_ap, a_b, bb_ap, bb_b, tmp_ap, tmp_b,
                       rows=128):
            p.op("vector", "tensor_scalar", [tmp_b], [a_b, pinv_b],
                 tmp_ap, a_ap, pinv_sb[0:rows, :], None, op0=ALU.mult)
            p.op("vector", "scalar_tensor_tensor", [out_b],
                 [bb_b, psel_b, tmp_b],
                 out_ap, bb_ap, psel_sb[0:rows, :], tmp_ap,
                 op0=ALU.mult, op1=ALU.add)

        def load_x_block(j):
            selA_sb, selA_b = _selholder["A"]
            selB_sb, selB_b = _selholder["B"]
            p.dma("sync", selA_sb[:, :],
                  x_full[2 * j * 128:(2 * j + 1) * 128, :],
                  [selA_b], [db(x_full)])
            p.dma("sync", selB_sb[:, :],
                  x_full[(2 * j + 1) * 128:(2 * j + 2) * 128, :],
                  [selB_b], [db(x_full)])
            par_select(xt_sb[:, :], xt_b, selA_sb[:, :], selA_b,
                       selB_sb[:, :], selB_b, sq_sb[:, :], sq_b)
        mu_sb, mu_b = sb("mu", [128, 1], F32)
        var_sb, var_b = sb("var", [128, 1], F32)
        sq_sb, sq_b = sb("sq", [128, D], F32)
        xh_sb, xh_b = sb("xh", [128, D], F32)
        xt_sb, xt_b = sb("xtile", [128, D], F32)

        def ln_norm(src_ap, src_b):
            p.op("vector", "reduce_sum", [mu_b], [src_b],
                 mu_sb[:, :], src_ap, axis=AX.X)
            p.op("vector", "tensor_scalar_mul", [mu_b], [mu_b],
                 mu_sb[:, :], mu_sb[:, :], 1.0 / D)
            p.op("vector", "tensor_scalar", [xh_b], [src_b, mu_b],
                 xh_sb[:, :], src_ap, mu_sb[:, :], None, op0=ALU.subtract)
            p.op("scalar", "activation", [sq_b], [xh_b],
                 sq_sb[:, :], xh_sb[:, :], AF.Square)
            p.op("vector", "reduce_sum", [var_b], [sq_b],
                 var_sb[:, :], sq_sb[:, :], axis=AX.X)
            p.op("vector", "tensor_scalar_mul", [var_b], [var_b],
                 var_sb[:, :], var_sb[:, :], 1.0 / D)
            p.op("scalar", "activation", [var_b], [var_b],
                 var_sb[:, :], var_sb[:, :], AF.Ln, bias=np.float32(EPS))
            p.op("scalar", "activation", [var_b], [var_b],
                 var_sb[:, :], var_sb[:, :], AF.Exp, scale=np.float32(-0.5))
            p.op("vector", "tensor_scalar", [xh_b], [xh_b, var_b],
                 xh_sb[:, :], xh_sb[:, :], var_sb[:, :], None, op0=ALU.mult)

        with contextlib.ExitStack() as s0:
            def sb0(name, shape, dt):
                return mk_sb(s0, name, shape, dt)

            mmp = [mk_ps(s0, f"mm_ps{i}", [128, 512]) for i in range(4)]
            tpp = [mk_ps(s0, f"tp_ps{i}", [128, 128], BF16) for i in range(2)]
            _mmi = [0]
            def next_mm():
                t = mmp[_mmi[0] % 4]
                _mmi[0] += 1
                return t
            _tpi = [0]
            def next_tp():
                t = tpp[_tpi[0] % 2]
                _tpi[0] += 1
                return t

            gln_g, gln_gb = sb0("g_lng", [128, D], F32)
            gln_bt, gln_bb = sb0("g_lnb", [128, D], F32)
            gln1, gln1b = sb0("g_ln1", [128, D], F32)
            p.dma("sync", gln_g[:, :], ln_g.partition_broadcast(128),
                  [gln_gb], [db(ln_g)])
            p.dma("sync", gln_bt[:, :], ln_b.partition_broadcast(128),
                  [gln_bb], [db(ln_b)])
            p.dma("sync", gln1[:, :], ln1_w.partition_broadcast(128),
                  [gln1b], [db(ln1_w)])
            convw_sb, convw_b = sb0("convw", [128, KD * 4], F32)
            p.dma("sync", convw_sb[:, :], convw_t[:, :], [convw_b],
                  [db(convw_t)])
            convb_sb, convb_b = sb0("convb", [128, KD], F32)
            p.dma("sync", convb_sb[:, :], convb_t[:, :], [convb_b],
                  [db(convb_t)])
            bdt_sb, bdt_b = sb0("bdtc", [128, KDH], F32)
            p.dma("sync", bdt_sb[:, :], bdt_t[:, :], [bdt_b], [db(bdt_t)])

            win_sb, win_b, win_sl = wload(s0, "win", win_km, D + DH)
            wdt_sb, wdt_b, wdt_sl = wload(s0, "wdt", wdt_km, DH)
            wb_sb, wb_b, wb_sl = wload(s0, "wbm", wb_km, N)
            wc_sb, wc_b, wc_sl = wload(s0, "wcm", wc_km, N)
            kvd_sb, kvd_b, kvd_sl = wload(s0, "kvdm", kvd_km, L)
            qd_sb, qd_b, qd_sl = wload(s0, "qdm", qd_km, L)
            kun_sb, kun_b, kun_sl = wload(s0, "kunm", kun_km, NH * NOPE)
            qun_sb, qun_b, qun_sl = wload(s0, "qunm", qun_km, NH * NOPE)
            krot_sb, krot_b, krot_sl = wload(s0, "krotw", krot_km, NH * RD)
            qrot_sb, qrot_b, qrot_sl = wload(s0, "qrotw", qrot_km, NH * RD)
            vu_sb, vu_b, vu_sl = wload(s0, "vum", vu_km, NH * HD)

            tdB_sb, tdB_b = sb0("tdB", [128, D], BF16)
            x1m_dt = [sb0(f"x1mdt{j}", [128, CH0], BF16) for j in range(KD)]
            ln1_dt = [sb0(f"ln1dt{j}", [128, CH0], BF16) for j in range(KD)]
            xmain = [sb0(f"xmain{j}", [128, CH0], F32) for j in range(KD)]
            halo = [sb0(f"halo{j}", [128, 3], F32) for j in range(KD)]
            xmc3 = [sb0(f"xmc3{j}", [128, 6], F32) for j in range(KD)]
            xcf = [sb0(f"xcf{j}", [128, CH0], F32) for j in range(KD)]
            xcb = [sb0(f"xcb{j}", [128, CH0], BF16) for j in range(KD)]
            nt_sb, nt_b = sb0("ntile", [N, CH0], F32)
            kvl = [sb0(f"kvl{j}", [128, CH0], BF16) for j in range(KL)]
            ev_sb, ev_b = sb0("ev", [128, max(CH0, 512)], BF16)
            kr = [sb0(f"kr{j}", [128, CH0], F32)
                  for j in range(NH * RD // 128)]
            rp1_sb, rp1_b = sb0("rp1", [128, CH0], F32)
            rp2_sb, rp2_b = sb0("rp2", [128, CH0], F32)
            spA_sb, spA_b = rp1_sb, rp1_b
            spB_sb, spB_b = rp2_sb, rp2_b
            rpo_sb, rpo_b = sb0("rpo", [128, CH0], BF16)
            cosc_sb, cosc_b = sb0("cosc", [128, CH0], F32)
            sinc_sb, sinc_b = sb0("sinc", [128, CH0], F32)

            dtc = [sb0(f"dtc{j}", [128, CH0], F32) for j in range(KDH)]
            zsil = [sb0(f"zsil{j}", [128, CH0], BF16) for j in range(KDH)]
            brep = [sb0(f"brep{i}", [128, CH0], F32) for i in range(2)]
            crep = [sb0(f"crep{i}", [128, CH0], F32) for i in range(2)]
            sc_a = [sb0(f"sc_a{i}", [128, CH0], F32) for i in range(2)]
            sc_b = [sb0(f"sc_b{i}", [128, CH0], F32) for i in range(2)]
            h_s, h_b = sb0("h_s", [128, CH0], F32)
            hc_s, hc_b = sb0("hc_s", [128, CH0], F32)
            yac = [sb0(f"yac{j}", [128, CH0], F32) for j in range(KDH)]
            ygo_sb, ygo_b = sb0("ygo", [128, CH0], BF16)
            hst = [sb0(f"hst{j}", [128, N], F32) for j in range(KDH)]
            for j in range(KDH):
                p.op("gpsimd", "memset", [hst[j][1]], [], hst[j][0][:, :], 0.0)
            for j in range(KD):
                p.op("gpsimd", "memset", [halo[j][1]], [],
                     halo[j][0][:, :], 0.0)

            def transpose_128(dst_ap, dst_b, src_ap, src_b):
                tp_ps, tp_pb = next_tp()
                p.op("tensor", "transpose", [tp_pb], [src_b, ident_b],
                     tp_ps[:, :], src_ap, ident_sb[:, :])
                p.op("vector", "tensor_copy", [dst_b], [tp_pb], dst_ap,
                     tp_ps[:, :])

            def rope_rot(dst_lin, src_tiles):
                ng = NH * RH // 128
                hpt2 = 128 // RH
                for gti in range(ng):
                    g1, g1b = src_tiles[gti]
                    g2, g2b = src_tiles[ng + gti]
                    p.op("vector", "tensor_tensor", [rp1_b], [g1b, cosc_b],
                         rp1_sb[:, :], g1[:, :], cosc_sb[:, :], op=ALU.mult)
                    p.op("vector", "tensor_tensor", [rp2_b], [g2b, sinc_b],
                         rp2_sb[:, :], g2[:, :], sinc_sb[:, :], op=ALU.mult)
                    p.op("vector", "tensor_tensor", [rpo_b], [rp1_b, rp2_b],
                         rpo_sb[:, :], rp1_sb[:, :], rp2_sb[:, :],
                         op=ALU.subtract)
                    for i in range(hpt2):
                        h = gti * hpt2 + i
                        p.dma("sync",
                              dst_lin[h * HD:h * HD + RH, t0:t0 + CH0],
                              rpo_sb[i * RH:(i + 1) * RH, :],
                              [db(dst_lin)], [rpo_b])
                    p.op("vector", "tensor_tensor", [rp1_b], [g2b, cosc_b],
                         rp1_sb[:, :], g2[:, :], cosc_sb[:, :], op=ALU.mult)
                    p.op("vector", "tensor_tensor", [rp2_b], [g1b, sinc_b],
                         rp2_sb[:, :], g1[:, :], sinc_sb[:, :], op=ALU.mult)
                    p.op("vector", "tensor_tensor", [rpo_b], [rp1_b, rp2_b],
                         rpo_sb[:, :], rp1_sb[:, :], rp2_sb[:, :], op=ALU.add)
                    for i in range(hpt2):
                        h = gti * hpt2 + i
                        p.dma("sync",
                              dst_lin[h * HD + RH:h * HD + RD, t0:t0 + CH0],
                              rpo_sb[i * RH:(i + 1) * RH, :],
                              [db(dst_lin)], [rpo_b])

            for ch in range(NCH0):
                t0 = ch * CH0
                p.dma("sync", cosc_sb[:, :], cos_rep[:, t0:t0 + CH0],
                      [cosc_b], [db(cos_rep)])
                p.dma("sync", sinc_sb[:, :], sin_rep[:, t0:t0 + CH0],
                      [sinc_b], [db(sin_rep)])
                for tt in range(TPC):
                    r0 = t0 + tt * 128
                    p.dma("sync", xt_sb[:, :], x_full[r0:r0 + 128, :],
                          [xt_b], [db(x_full)])
                    ln_norm(xt_sb[:, :], xt_b)
                    ln_norm(xh_sb[:, :], xh_b)
                    p.op("vector", "tensor_tensor", [tdB_b], [xh_b, gln_gb],
                         tdB_sb[:, :], xh_sb[:, :], gln_g[:, :], op=ALU.mult)
                    p.op("vector", "tensor_tensor", [tdB_b], [tdB_b, gln_bb],
                         tdB_sb[:, :], tdB_sb[:, :], gln_bt[:, :], op=ALU.add)
                    for j in range(KD):
                        transpose_128(
                            x1m_dt[j][0][:, tt * 128:(tt + 1) * 128],
                            x1m_dt[j][1],
                            tdB_sb[:, j * 128:(j + 1) * 128], tdB_b)
                    p.op("vector", "tensor_tensor", [tdB_b], [xh_b, gln1b],
                         tdB_sb[:, :], xh_sb[:, :], gln1[:, :], op=ALU.mult)
                    for j in range(KD):
                        transpose_128(
                            ln1_dt[j][0][:, tt * 128:(tt + 1) * 128],
                            ln1_dt[j][1],
                            tdB_sb[:, j * 128:(j + 1) * 128], tdB_b)

                for o in range((D + DH) // 128):
                    mm_ps, mm_pb = next_mm()
                    for kt in range(KD):
                        p.op("tensor", "matmul", [mm_pb],
                             [win_b, x1m_dt[kt][1]],
                             mm_ps[:, 0:CH0],
                             win_sl(kt, o * 128, (o + 1) * 128),
                             x1m_dt[kt][0][:, :],
                             start=(kt == 0), stop=(kt == KD - 1))
                    if o < KD:
                        p.op("scalar", "copy", [xmain[o][1]], [mm_pb],
                             xmain[o][0][:, :], mm_ps[:, 0:CH0])
                    else:
                        oz = o - KD
                        p.op("scalar", "activation", [zsil[oz][1]], [mm_pb],
                             zsil[oz][0][:, :], mm_ps[:, 0:CH0], AF.Silu)

                for j in range(KD):
                    wj = [convw_sb[:, j * 4 + i:j * 4 + i + 1]
                          for i in range(4)]
                    xm, xmb = xmain[j]
                    xc, xcfb = xcf[j]
                    p.op("vector", "tensor_scalar", [xcfb], [xmb, convw_b],
                         xc[:, 3:CH0], xm[:, 0:CH0 - 3], wj[0], None,
                         op0=ALU.mult)
                    for i in range(1, 4):
                        p.op("vector", "scalar_tensor_tensor", [xcfb],
                             [xmb, convw_b, xcfb],
                             xc[:, 3:CH0], xm[:, i:CH0 - 3 + i], wj[i],
                             xc[:, 3:CH0], op0=ALU.mult, op1=ALU.add)
                    m3, m3b = xmc3[j]
                    p.op("vector", "tensor_copy", [m3b], [halo[j][1]],
                         m3[:, 0:3], halo[j][0][:, :])
                    p.op("vector", "tensor_copy", [m3b], [xmb],
                         m3[:, 3:6], xm[:, 0:3])
                    p.op("vector", "tensor_scalar", [xcfb], [m3b, convw_b],
                         xc[:, 0:3], m3[:, 0:3], wj[0], None, op0=ALU.mult)
                    for i in range(1, 4):
                        p.op("vector", "scalar_tensor_tensor", [xcfb],
                             [m3b, convw_b, xcfb],
                             xc[:, 0:3], m3[:, i:3 + i], wj[i],
                             xc[:, 0:3], op0=ALU.mult, op1=ALU.add)
                    p.op("vector", "tensor_copy", [halo[j][1]], [xmb],
                         halo[j][0][:, :], xm[:, CH0 - 3:CH0])
                    p.op("scalar", "activation", [xcfb], [xcfb, convb_b],
                         xc[:, :], xc[:, :], AF.Silu,
                         bias=convb_sb[:, j:j + 1])
                    p.op("vector", "tensor_copy", [xcb[j][1]], [xcfb],
                         xcb[j][0][:, :], xc[:, :])
                    if debug and j < KDH:
                        p.dma("scalar",
                              xc_lin[j * 128:(j + 1) * 128, t0:t0 + CH0],
                              xc[:, :], [db(xc_lin)], [xcfb])

                for o in range(KDH):
                    mm_ps, mm_pb = next_mm()
                    for kt in range(KD):
                        p.op("tensor", "matmul", [mm_pb], [wdt_b, xcb[kt][1]],
                             mm_ps[:, 0:CH0],
                             wdt_sl(kt, o * 128, (o + 1) * 128),
                             xcb[kt][0][:, :],
                             start=(kt == 0), stop=(kt == KD - 1))
                    bcol = bdt_sb[:, o:o + 1]
                    p.op("scalar", "activation", [spA_b], [mm_pb, bdt_b],
                         spA_sb[:, :], mm_ps[:, 0:CH0], AF.Abs, bias=bcol)
                    p.op("scalar", "activation", [spA_b], [spA_b],
                         spA_sb[:, :], spA_sb[:, :], AF.Exp, scale=np.float32(-1.0))
                    p.op("scalar", "activation", [spA_b], [spA_b],
                         spA_sb[:, :], spA_sb[:, :], AF.Ln, bias=np.float32(1.0))
                    p.op("scalar", "activation", [spB_b], [mm_pb, bdt_b],
                         spB_sb[:, :], mm_ps[:, 0:CH0], AF.Relu, bias=bcol)
                    p.op("vector", "tensor_tensor", [dtc[o][1]],
                         [spA_b, spB_b],
                         dtc[o][0][:, :], spA_sb[:, :], spB_sb[:, :],
                         op=ALU.add)
                    if debug:
                        p.dma("scalar",
                              dt_lin[o * 128:(o + 1) * 128, t0:t0 + CH0],
                              dtc[o][0][:, :], [db(dt_lin)], [dtc[o][1]])

                for wsl_, wbuf_, dst in ((wb_sl, wb_b, b_lin),
                                         (wc_sl, wc_b, c_lin)):
                    mm_ps, mm_pb = next_mm()
                    for kt in range(KD):
                        p.op("tensor", "matmul", [mm_pb], [wbuf_, xcb[kt][1]],
                             mm_ps[0:N, 0:CH0],
                             wsl_(kt, 0, N),
                             xcb[kt][0][:, :],
                             start=(kt == 0), stop=(kt == KD - 1))
                    p.op("scalar", "copy", [nt_b], [mm_pb],
                         nt_sb[:, :], mm_ps[0:N, 0:CH0])
                    p.dma("scalar", dst[:, t0:t0 + CH0], nt_sb[:, :],
                          [db(dst)], [nt_b])

                for o in range(KL):
                    mm_ps, mm_pb = next_mm()
                    for kt in range(KD):
                        p.op("tensor", "matmul", [mm_pb],
                             [kvd_b, ln1_dt[kt][1]],
                             mm_ps[:, 0:CH0],
                             kvd_sl(kt, o * 128, (o + 1) * 128),
                             ln1_dt[kt][0][:, :],
                             start=(kt == 0), stop=(kt == KD - 1))
                    p.op("scalar", "copy", [kvl[o][1]], [mm_pb],
                         kvl[o][0][:, :], mm_ps[:, 0:CH0])

                for tt in range(TPC):
                    for vh in range(NH * HD // 512):
                        mm2_ps, mm2_pb = next_mm()
                        for kt in range(KL):
                            p.op("tensor", "matmul", [mm2_pb],
                                 [kvl[kt][1], vu_b],
                                 mm2_ps[:, 0:512],
                                 kvl[kt][0][:, tt * 128:(tt + 1) * 128],
                                 vu_sl(kt, vh * 512, (vh + 1) * 512),
                                 start=(kt == 0), stop=(kt == KL - 1))
                        p.op("scalar", "copy", [ev_b], [mm2_pb],
                             ev_sb[:, 0:512], mm2_ps[:, 0:512])
                        p.dma("sync",
                              vt_lin[t0 + tt * 128:t0 + tt * 128 + 128,
                                     vh * 512:(vh + 1) * 512],
                              ev_sb[:, 0:512], [db(vt_lin)], [ev_b])

                hpt = 128 // NOPE
                for o in range(NH * NOPE // 128):
                    mm_ps, mm_pb = next_mm()
                    for kt in range(KL):
                        p.op("tensor", "matmul", [mm_pb], [kun_b, kvl[kt][1]],
                             mm_ps[:, 0:CH0],
                             kun_sl(kt, o * 128, (o + 1) * 128),
                             kvl[kt][0][:, :],
                             start=(kt == 0), stop=(kt == KL - 1))
                    p.op("scalar", "copy", [ev_b], [mm_pb],
                         ev_sb[:, 0:CH0], mm_ps[:, 0:CH0])
                    for i in range(hpt):
                        h = o * hpt + i
                        p.dma("sync",
                              k_lin[h * HD + RD:h * HD + HD, t0:t0 + CH0],
                              ev_sb[i * NOPE:(i + 1) * NOPE, 0:CH0],
                              [db(k_lin)], [ev_b])

                for o in range(NH * RD // 128):
                    mm_ps, mm_pb = next_mm()
                    for kt in range(KD):
                        p.op("tensor", "matmul", [mm_pb],
                             [krot_b, ln1_dt[kt][1]],
                             mm_ps[:, 0:CH0],
                             krot_sl(kt, o * 128, (o + 1) * 128),
                             ln1_dt[kt][0][:, :],
                             start=(kt == 0), stop=(kt == KD - 1))
                    p.op("scalar", "copy", [kr[o][1]], [mm_pb],
                         kr[o][0][:, :], mm_ps[:, 0:CH0])
                rope_rot(k_lin, kr)

                for o in range(KL):
                    mm_ps, mm_pb = next_mm()
                    for kt in range(KD):
                        p.op("tensor", "matmul", [mm_pb],
                             [qd_b, ln1_dt[kt][1]],
                             mm_ps[:, 0:CH0],
                             qd_sl(kt, o * 128, (o + 1) * 128),
                             ln1_dt[kt][0][:, :],
                             start=(kt == 0), stop=(kt == KD - 1))
                    p.op("scalar", "copy", [kvl[o][1]], [mm_pb],
                         kvl[o][0][:, :], mm_ps[:, 0:CH0])
                for o in range(NH * NOPE // 128):
                    mm_ps, mm_pb = next_mm()
                    for kt in range(KL):
                        p.op("tensor", "matmul", [mm_pb], [qun_b, kvl[kt][1]],
                             mm_ps[:, 0:CH0],
                             qun_sl(kt, o * 128, (o + 1) * 128),
                             kvl[kt][0][:, :],
                             start=(kt == 0), stop=(kt == KL - 1))
                    p.op("scalar", "copy", [ev_b], [mm_pb],
                         ev_sb[:, 0:CH0], mm_ps[:, 0:CH0])
                    for i in range(hpt):
                        h = o * hpt + i
                        p.dma("sync",
                              q_all[h * HD + RD:h * HD + HD, t0:t0 + CH0],
                              ev_sb[i * NOPE:(i + 1) * NOPE, 0:CH0],
                              [db(q_all)], [ev_b])
                for o in range(NH * RD // 128):
                    mm_ps, mm_pb = next_mm()
                    for kt in range(KL):
                        p.op("tensor", "matmul", [mm_pb],
                             [qrot_b, kvl[kt][1]],
                             mm_ps[:, 0:CH0],
                             qrot_sl(kt, o * 128, (o + 1) * 128),
                             kvl[kt][0][:, :],
                             start=(kt == 0), stop=(kt == KL - 1))
                    p.op("scalar", "copy", [kr[o][1]], [mm_pb],
                         kr[o][0][:, :], mm_ps[:, 0:CH0])
                rope_rot(q_all, kr)

                for n in range(N):
                    bi = n % 2
                    p.dma("gpsimd", brep[bi][0][:, :],
                          b_lin[n, t0:t0 + CH0].partition_broadcast(128),
                          [brep[bi][1]], [db(b_lin)])
                    p.dma("gpsimd", crep[bi][0][:, :],
                          c_lin[n, t0:t0 + CH0].partition_broadcast(128),
                          [crep[bi][1]], [db(c_lin)])
                    for j in range(KDH):
                        ab = sc_a[(n * KDH + j) % 2]
                        bb = sc_b[(n * KDH + j) % 2]
                        p.op("scalar", "activation", [ab[1]],
                             [dtc[j][1], acol_b],
                             ab[0][:, :], dtc[j][0][:, :], AF.Exp,
                             scale=acol_sb[:, n:n + 1])
                        p.op("vector", "tensor_tensor", [bb[1]],
                             [xcf[j][1], brep[bi][1]],
                             bb[0][:, :], xcf[j][0][:, :],
                             brep[bi][0][:, :], op=ALU.mult)
                        p.op("vector", "tensor_tensor_scan", [h_b],
                             [ab[1], bb[1], hst[j][1]],
                             out=h_s[:, :], data0=ab[0][:, :],
                             data1=bb[0][:, :],
                             initial=hst[j][0][:, n:n + 1],
                             op0=ALU.mult, op1=ALU.add)
                        p.op("scalar", "copy", [hst[j][1]], [h_b],
                             hst[j][0][:, n:n + 1], h_s[:, CH0 - 1:CH0])
                        if n == 0:
                            p.op("vector", "tensor_tensor", [yac[j][1]],
                                 [h_b, crep[bi][1]],
                                 yac[j][0][:, :], h_s[:, :],
                                 crep[bi][0][:, :], op=ALU.mult)
                        else:
                            p.op("vector", "tensor_tensor", [hc_b],
                                 [h_b, crep[bi][1]],
                                 hc_s[:, :], h_s[:, :], crep[bi][0][:, :],
                                 op=ALU.mult)
                            p.op("vector", "tensor_tensor", [yac[j][1]],
                                 [hc_b, yac[j][1]],
                                 yac[j][0][:, :], yac[j][0][:, :],
                                 hc_s[:, :], op=ALU.add)
                for j in range(KDH):
                    p.op("vector", "scalar_tensor_tensor", [yac[j][1]],
                         [xcf[j][1], dp_b, yac[j][1]],
                         yac[j][0][:, :], xcf[j][0][:, :],
                         dp_sb[:, j:j + 1], yac[j][0][:, :],
                         op0=ALU.mult, op1=ALU.add)
                    if debug:
                        p.dma("scalar",
                              dbg["d_y"][j * 128:(j + 1) * 128, t0:t0 + CH0],
                              yac[j][0][:, :], [db(dbg["d_y"])], [yac[j][1]])
                    p.op("vector", "tensor_tensor", [ygo_b],
                         [yac[j][1], zsil[j][1]],
                         ygo_sb[:, :], yac[j][0][:, :], zsil[j][0][:, :],
                         op=ALU.mult)
                    p._wait("sync", p._toks([ygo_b], [db(ygA), db(ygB)]))
                    dkey = p._dma_sem("sync")
                    with nc.sync.If_eq(rg_par, 0):
                        nc.sync.dma_start(
                            out=ygA[j * 128:(j + 1) * 128, t0:t0 + CH0],
                            in_=ygo_sb[:, :]).then_inc(p.sem[dkey], 16)
                    with nc.sync.Else():
                        nc.sync.dma_start(
                            out=ygB[j * 128:(j + 1) * 128, t0:t0 + CH0],
                            in_=ygo_sb[:, :]).then_inc(p.sem[dkey], 16)
                    p.cnt[dkey] += 16
                    tokd = (dkey, p.cnt[dkey])
                    p._finish([ygo_b], [db(ygA), db(ygB)], tokd)

            yg_toks = [(f"d_sync{i}", p.cnt[f"d_sync{i}"])
                       for i in range(p.NDS)]
            p._wait("sync", yg_toks + [tok_b.w])
            dkey = p._dma_sem("sync")
            with nc.sync.If_eq(rg_par, 0):
                nc.sync.dma_start(out=flA[:, :], in_=tok_sb[:, :]).then_inc(
                    p.sem[dkey], 16)
            with nc.sync.Else():
                nc.sync.dma_start(out=flB[:, :], in_=tok_sb[:, :]).then_inc(
                    p.sem[dkey], 16)
            p.cnt[dkey] += 16
            p.barrier()

        x1o = [sb(f"x1o{i}", [128, D], F32) for i in range(NTT)]
        _selholder["A"] = sb("selA", [128, D], F32)
        _selholder["B"] = sb("selB", [128, D], F32)

        with contextlib.ExitStack() as s3:
            def sb3(name, shape, dt):
                return mk_sb(s3, name, shape, dt)

            att_s0, att_s0b = mk_ps(s3, "att_s0", [128, 512])
            att_s1, att_s1b = mk_ps(s3, "att_s1", [128, 512])
            att_om = [mk_ps(s3, f"att_o{m}", [128, 65]) for m in range(QG)]
            tp3 = [mk_ps(s3, f"tp_ps3{i}", [128, 128], BF16) for i in range(2)]
            _t3 = [0]
            def next_tp3():
                t = tp3[_t3[0] % 2]
                _t3[0] += 1
                return t

            ow_sb, ow_b, ow_sl = wload(s3, "owm", ow_km, D)
            NKTmax = min(8 * (NQG - 1) + 2 * QG, NB)
            kh_sb, kh_b = sb3("kh", [HD, NKTmax * 128], BF16)
            vb_sb, vb_b = sb3("vb", [128, 65 * NKTmax], BF16)
            qh_sb, qh_b = sb3("qh", [HD, QG * 128], BF16)
            qsA_sb, qsA_b = sb3("qsA", [HD, 128], BF16)
            qsB_sb, qsB_b = sb3("qsB", [HD, 128], BF16)
            qtmp_sb, qtmp_b = sb3("qtmp", [HD, 128], F32)
            pT = [sb3(f"pT{i}", [128, QG * 128], BF16) for i in range(2)]
            ayg = [sb3(f"ayg{m}", [128, NH * HD], BF16) for m in range(QG)]
            ayT = [sb3(f"ayT{j}", [128, QG * 128], BF16) for j in range(KD)]
            recip_sb, recip_b = sb3("recip", [128, 1], F32)
            msk = [[sb3(f"msk{m}_{j2}", [128, 128], F32) for j2 in range(2)]
                   for m in range(QG)]
            for m in range(QG):
                for j2 in range(2):
                    p.dma("sync", msk[m][j2][0][:, :],
                          amask[(m * 2 + j2) * 128:(m * 2 + j2 + 1) * 128, :],
                          [msk[m][j2][1]], [db(amask)])
            p.op("gpsimd", "memset", [vb_b], [], vb_sb[:, :], 1.0)

            for g in range(NQG):
                NKT = min(8 * g + 2 * QG, NB)
                for h in range(NH):
                    p.dma("sync", kh_sb[:, 0:NKT * 128],
                          k_lin[h * HD:(h + 1) * HD, 0:NKT * 128],
                          [kh_b], [db(k_lin)])
                    for kt in range(NKT):
                        p.dma("sync", vb_sb[:, kt * 65:kt * 65 + 64],
                              vt_lin[kt * 128:(kt + 1) * 128,
                                     h * HD:h * HD + 64],
                              [vb_b], [db(vt_lin)])
                    for m in range(QG):
                        gb = (8 * g + 2 * m) * 128
                        p.dma("sync", qsA_sb[:, :],
                              q_all[h * HD:(h + 1) * HD, gb:gb + 128],
                              [qsA_b], [db(q_all)])
                        p.dma("sync", qsB_sb[:, :],
                              q_all[h * HD:(h + 1) * HD, gb + 128:gb + 256],
                              [qsB_b], [db(q_all)])
                        par_select(qh_sb[:, m * 128:(m + 1) * 128], qh_b,
                                   qsA_sb[:, :], qsA_b, qsB_sb[:, :], qsB_b,
                                   qtmp_sb[:, :], qtmp_b, rows=HD)
                    QW = QG * 128
                    for kt in range(NKT):
                        sps, spb = (att_s0, att_s0b) if kt % 2 == 0 else \
                                   (att_s1, att_s1b)
                        p.op("tensor", "matmul", [spb], [kh_b, qh_b],
                             sps[:, 0:QW],
                             kh_sb[0:HD, kt * 128:(kt + 1) * 128],
                             qh_sb[0:HD, 0:QW], start=True, stop=True)
                        r = kt - 8 * g
                        if 0 <= r < 2 * QG:
                            m, j2 = r // 2, r % 2
                            p.op("vector", "tensor_tensor", [spb],
                                 [spb, msk[m][j2][1]],
                                 sps[:, m * 128:(m + 1) * 128],
                                 sps[:, m * 128:(m + 1) * 128],
                                 msk[m][j2][0][:, :], op=ALU.add)
                        pb = pT[kt % 2]
                        p.op("scalar", "activation", [pb[1]], [spb],
                             pb[0][:, 0:QW], sps[:, 0:QW], AF.Exp,
                             scale=np.float32(1.0 / np.sqrt(HD)))
                        for m in range(QG):
                            lastm = min(8 * g + 2 * m + 2, NB)
                            if kt < lastm:
                                p.op("tensor", "matmul", [att_om[m][1]],
                                     [pb[1], vb_b],
                                     att_om[m][0][:, 0:65],
                                     pb[0][:, m * 128:(m + 1) * 128],
                                     vb_sb[:, kt * 65:kt * 65 + 65],
                                     start=(kt == 0), stop=(kt == lastm - 1))
                    for m in range(QG):
                        p.op("vector", "reciprocal", [recip_b], [att_om[m][1]],
                             recip_sb[:, :], att_om[m][0][:, 64:65])
                        p.op("vector", "tensor_scalar", [ayg[m][1]],
                             [att_om[m][1], recip_b],
                             ayg[m][0][:, h * HD:h * HD + 64],
                             att_om[m][0][:, 0:64],
                             recip_sb[:, :], None, op0=ALU.mult)

                for m in range(QG):
                    for j in range(KD):
                        tp_ps3, tp_pb3 = next_tp3()
                        p.op("tensor", "transpose", [tp_pb3],
                             [ayg[m][1], ident_b],
                             tp_ps3[:, :],
                             ayg[m][0][:, j * 128:(j + 1) * 128],
                             ident_sb[:, :])
                        p.op("vector", "tensor_copy", [ayT[j][1]], [tp_pb3],
                             ayT[j][0][:, m * 128:(m + 1) * 128],
                             tp_ps3[:, :])
                    load_x_block(QG * g + m)
                    ln_norm(xt_sb[:, :], xt_b)
                    ti = g * QG + m
                    for eh in range(D // 512):
                        mps, mpb = (att_s0, att_s0b) if eh % 2 == 0 else \
                                   (att_s1, att_s1b)
                        for kt in range(KD):
                            p.op("tensor", "matmul", [mpb],
                                 [ayT[kt][1], ow_b],
                                 mps[:, 0:512],
                                 ayT[kt][0][:, m * 128:(m + 1) * 128],
                                 ow_sl(kt, eh * 512, (eh + 1) * 512),
                                 start=(kt == 0), stop=(kt == KD - 1))
                        p.op("vector", "tensor_tensor", [x1o[ti][1]],
                             [mpb, xh_b],
                             x1o[ti][0][:, eh * 512:(eh + 1) * 512],
                             mps[:, 0:512],
                             xh_sb[:, eh * 512:(eh + 1) * 512], op=ALU.add)
                    if debug:
                        p.dma("sync", dbg["d_x1"][ti * 128:(ti + 1) * 128, :],
                              x1o[ti][0][:, :], [db(dbg["d_x1"])],
                              [x1o[ti][1]])
            p.barrier()

        with contextlib.ExitStack() as s5:
            def sb5(name, shape, dt):
                return mk_sb(s5, name, shape, dt)

            mm_ps, mm_pb = mk_ps(s5, "m5a", [128, 512])
            mm2_ps, mm2_pb = mk_ps(s5, "m5b", [128, 512])
            tp_ps5, tp_pb5 = mk_ps(s5, "tp5", [128, 128], BF16)
            rt_ps, rt_pb = mk_ps(s5, "rt5", [128, NE])

            gln2, gln2b = sb5("g_ln2", [128, D], F32)
            p.dma("sync", gln2[:, :], ln2_w.partition_broadcast(128),
                  [gln2b], [db(ln2_w)])
            rb_rep, rb_repb = sb5("rbrep", [128, NE], F32)
            p.dma("sync", rb_rep[:, :], rbias.partition_broadcast(128),
                  [rb_repb], [db(rbias)])
            rw_sb, rw_b, rw_sl = wload(s5, "rwm", router_km, NE)
            tdB5_sb, tdB5_b = sb5("tdB5", [128, D], BF16)
            xt_dt = [sb5(f"xtdt{j}", [128, TO], BF16) for j in range(KD)]
            probs_sb, probs_b = sb5("probs", [128, NE], F32)
            mskA_sb, mskA_b = sb5("mskA", [128, NE], F32)
            m1_sb, m1_b = sb5("m1c", [128, 1], F32)
            m2_sb, m2_b = sb5("m2c", [128, 1], F32)
            wgt_sb, wgt_b = sb5("wgt", [128, NE * NTT], F32)
            gch_sb, gch_b = sb5("gch", [128, KD * ICH], BF16)
            uch_sb, uch_b = sb5("uch", [128, KD * ICH], BF16)
            dch_sb, dch_b = sb5("dch", [128, (ICH // 128) * D], BF16)

            def gch_sl(kt, c0, c1):
                return gch_sb[:, kt * ICH + c0:kt * ICH + c1]

            def uch_sl(kt, c0, c1):
                return uch_sb[:, kt * ICH + c0:kt * ICH + c1]

            def dch_sl(kt, c0, c1):
                return dch_sb[:, kt * D + c0:kt * D + c1]
            gsil_sb, gsil_b = sb5("gsil", [128, TF], BF16)
            gub = [sb5(f"gub{i}", [128, TO], BF16) for i in range(ICH // 128)]

            for tt in range(NTT):
                ln_norm(x1o[tt][0][:, :], x1o[tt][1])
                p.op("vector", "tensor_tensor", [tdB5_b], [xh_b, gln2b],
                     tdB5_sb[:, :], xh_sb[:, :], gln2[:, :], op=ALU.mult)
                for j in range(KD):
                    p.op("tensor", "transpose", [tp_pb5], [tdB5_b, ident_b],
                         tp_ps5[:, :], tdB5_sb[:, j * 128:(j + 1) * 128],
                         ident_sb[:, :])
                    p.op("vector", "tensor_copy", [xt_dt[j][1]], [tp_pb5],
                         xt_dt[j][0][:, tt * 128:(tt + 1) * 128],
                         tp_ps5[:, :])

            for tt in range(NTT):
                for kt in range(KD):
                    p.op("tensor", "matmul", [rt_pb], [xt_dt[kt][1], rw_b],
                         rt_ps[:, 0:NE],
                         xt_dt[kt][0][:, tt * 128:(tt + 1) * 128],
                         rw_sl(kt, 0, NE),
                         start=(kt == 0), stop=(kt == KD - 1))
                p.op("vector", "tensor_tensor", [probs_b], [rt_pb, rb_repb],
                     probs_sb[:, :], rt_ps[:, 0:NE], rb_rep[:, :], op=ALU.add)
                p.op("scalar", "activation", [probs_b], [probs_b],
                     probs_sb[:, :], probs_sb[:, :], AF.Sigmoid)
                p.op("vector", "reduce_max", [m1_b], [probs_b],
                     m1_sb[:, :], probs_sb[:, :], axis=AX.X)
                p.op("vector", "tensor_scalar", [mskA_b], [probs_b, m1_b],
                     mskA_sb[:, :], probs_sb[:, :], m1_sb[:, :], None,
                     op0=ALU.is_equal)
                p.op("vector", "scalar_tensor_tensor", [mskA_b],
                     [mskA_b, probs_b],
                     mskA_sb[:, :], mskA_sb[:, :], NEG, probs_sb[:, :],
                     op0=ALU.mult, op1=ALU.add)
                p.op("vector", "reduce_max", [m2_b], [mskA_b],
                     m2_sb[:, :], mskA_sb[:, :], axis=AX.X)
                p.op("vector", "tensor_scalar", [mskA_b], [probs_b, m2_b],
                     mskA_sb[:, :], probs_sb[:, :], m2_sb[:, :], None,
                     op0=ALU.is_ge)
                p.op("vector", "tensor_tensor", [m1_b], [m1_b, m2_b],
                     m1_sb[:, :], m1_sb[:, :], m2_sb[:, :], op=ALU.add)
                p.op("vector", "reciprocal", [m1_b], [m1_b],
                     m1_sb[:, :], m1_sb[:, :])
                p.op("vector", "scalar_tensor_tensor", [wgt_b],
                     [probs_b, m1_b, mskA_b],
                     wgt_sb[:, tt * NE:(tt + 1) * NE], probs_sb[:, :],
                     m1_sb[:, :], mskA_sb[:, :], op0=ALU.mult, op1=ALU.mult)
                if debug:
                    p.dma("sync", dbg["d_wgt"][tt * 128:(tt + 1) * 128, :],
                          wgt_sb[:, tt * NE:(tt + 1) * NE],
                          [db(dbg["d_wgt"])], [wgt_b])

            for e in range(NE + 1):
                for ic in range(NICH):
                    c0 = e * I + ic * ICH
                    p.dma("gpsimd",
                          gch_sb[:, :].rearrange("p (k c) -> p k c", c=ICH),
                          gate_km[:, c0:c0 + ICH].rearrange(
                              "(k p) c -> p k c", p=128),
                          [gch_b], [db(gate_km)])
                    p.dma("gpsimd",
                          uch_sb[:, :].rearrange("p (k c) -> p k c", c=ICH),
                          up_km[:, c0:c0 + ICH].rearrange(
                              "(k p) c -> p k c", p=128),
                          [uch_b], [db(up_km)])
                    p.dma("gpsimd",
                          dch_sb[:, :].rearrange("p (k c) -> p k c", c=D),
                          down_km[c0:c0 + ICH, :].rearrange(
                              "(k p) c -> p k c", p=128),
                          [dch_b], [db(down_km)])
                    for it in range(ICH // 128):
                        for tf in range(NTF):
                            for kt in range(KD):
                                p.op("tensor", "matmul", [mm_pb],
                                     [gch_b, xt_dt[kt][1]],
                                     mm_ps[:, 0:TF],
                                     gch_sl(kt, it * 128, (it + 1) * 128),
                                     xt_dt[kt][0][:, tf * TF:(tf + 1) * TF],
                                     start=(kt == 0), stop=(kt == KD - 1))
                            p.op("scalar", "activation", [gsil_b], [mm_pb],
                                 gsil_sb[:, :], mm_ps[:, 0:TF], AF.Silu)
                            for kt in range(KD):
                                p.op("tensor", "matmul", [mm2_pb],
                                     [uch_b, xt_dt[kt][1]],
                                     mm2_ps[:, 0:TF],
                                     uch_sl(kt, it * 128, (it + 1) * 128),
                                     xt_dt[kt][0][:, tf * TF:(tf + 1) * TF],
                                     start=(kt == 0), stop=(kt == KD - 1))
                            p.op("vector", "tensor_tensor", [gub[it][1]],
                                 [gsil_b, mm2_pb],
                                 gub[it][0][:, tf * TF:(tf + 1) * TF],
                                 gsil_sb[:, :], mm2_ps[:, 0:TF], op=ALU.mult)
                    for tt in range(NTT):
                        for eh in range(D // 512):
                            dps, dpb = ((mm_ps, mm_pb) if (tt * 2 + eh) % 2 == 0
                                        else (mm2_ps, mm2_pb))
                            for it in range(ICH // 128):
                                p.op("tensor", "matmul", [dpb],
                                     [gub[it][1], dch_b],
                                     dps[:, 0:512],
                                     gub[it][0][:, tt * 128:(tt + 1) * 128],
                                     dch_sl(it, eh * 512, (eh + 1) * 512),
                                     start=(it == 0),
                                     stop=(it == ICH // 128 - 1))
                            scal = (wgt_sb[:, tt * NE + e:tt * NE + e + 1]
                                    if e < NE else 1.0)
                            sbuf_ins = [dpb, x1o[tt][1]]
                            if e < NE:
                                sbuf_ins.append(wgt_b)
                            p.op("vector", "scalar_tensor_tensor",
                                 [x1o[tt][1]], sbuf_ins,
                                 x1o[tt][0][:, eh * 512:(eh + 1) * 512],
                                 dps[:, 0:512], scal,
                                 x1o[tt][0][:, eh * 512:(eh + 1) * 512],
                                 op0=ALU.mult, op1=ALU.add)
            if debug:
                for tt in range(NTT):
                    p.dma("sync", dbg["d_x2"][tt * 128:(tt + 1) * 128, :],
                          x1o[tt][0][:, :], [db(dbg["d_x2"])], [x1o[tt][1]])
            p.barrier()

        with contextlib.ExitStack() as s6:
            def sb6(name, shape, dt):
                return mk_sb(s6, name, shape, dt)

            mmf0, mmf0b = mk_ps(s6, "mmf0", [128, 512])
            mmf1, mmf1b = mk_ps(s6, "mmf1", [128, 512])
            wout_sb, wout_b, wout_sl = wload(s6, "woutm", wout_km, D)
            ygk = [sb6(f"ygk{k}", [128, 128], BF16) for k in range(KD)]
            ygsA_sb, ygsA_b = sb6("ygsA", [128, 128], BF16)
            ygsB_sb, ygsB_b = sb6("ygsB", [128, 128], BF16)
            qtmp6_sb, qtmp6_b = sb6("qtmp6", [128, 128], F32)
            sgo = [ (sb6(f"s_f{i}", [128, D], F32),
                     sb6(f"g_f{i}", [128, D], F32),
                     sb6(f"o_f{i}", [128, D], F32)) for i in range(2)]

            p._wait("sync", [tok_b.w])
            with nc.sync.register("rg_f") as rg_f, \
                 nc.sync.register("rg_t") as rg_t, \
                 nc.sync.register("rg_c") as rg_c, \
                 nc.sync.register("rg_u") as rg_u, \
                 nc.sync.register("rg_w") as rg_w:
                nc.sync.load(rg_t, tok_sb[0:1, 0:1])
                nc.sync.reg_mov(rg_f, 1)
                nc.sync.reg_mov(rg_w, 0)
                nc.sync.reg_mov(rg_c, int(S.get("poll_cap", 3000000)))
                with nc.sync.While(rg_f):
                    nc.sync.reg_add(rg_w, rg_w, 32)
                    nc.sync.dma_start(out=fl_sb[0:1, 0:1],
                                      in_=flA[:, :]).then_inc(pl_sem, 16)
                    nc.sync.dma_start(out=fl_sb[0:1, 1:2],
                                      in_=flB[:, :]).then_inc(pl_sem, 16)
                    nc.sync.wait_ge(pl_sem, rg_w)
                    nc.sync.load(rg_f, fl_sb[0:1, 0:1])
                    nc.sync.load(rg_u, fl_sb[0:1, 1:2])
                    nc.sync.reg_sub(rg_f, rg_f, rg_t)
                    nc.sync.reg_sub(rg_u, rg_u, rg_t)
                    nc.sync.reg_alu(rg_f, rg_f, rg_u, ALU.bitwise_or)
                    nc.sync.reg_sub(rg_c, rg_c, 1)
                    with nc.sync.If_eq(rg_c, 0):
                        nc.sync.reg_mov(rg_f, 0)
                nc.sync.sem_inc(p.sem["sync"], 1)
            p.cnt["sync"] += 1
            hs_tok = ("sync", p.cnt["sync"])
            fl_b.w = hs_tok
            ygA_b, ygB_b = db(ygA), db(ygB)

            for tt in range(NTT):
                (s_sb, s_b), (g_sb, g_b), (o_sb, o_b) = sgo[tt % 2]
                gb = 2 * tt * 128
                for kt in range(KD):
                    src = ygA if kt < KDH else ygB
                    srcb = ygA_b if kt < KDH else ygB_b
                    r0 = (kt % KDH) * 128
                    p.dma("sync", ygsA_sb[:, :],
                          src[r0:r0 + 128, gb:gb + 128],
                          [ygsA_b], [srcb], waits=[hs_tok])
                    p.dma("sync", ygsB_sb[:, :],
                          src[r0:r0 + 128, gb + 128:gb + 256],
                          [ygsB_b], [srcb], waits=[hs_tok])
                    par_select(ygk[kt][0][:, :], ygk[kt][1],
                               ygsA_sb[:, :], ygsA_b, ygsB_sb[:, :], ygsB_b,
                               qtmp6_sb[:, :], qtmp6_b)
                load_x_block(tt)
                ln_norm(xt_sb[:, :], xt_b)
                for eh in range(D // 512):
                    mps, mpb = (mmf0, mmf0b) if eh % 2 == 0 else (mmf1, mmf1b)
                    for kt in range(KD):
                        p.op("tensor", "matmul", [mpb], [ygk[kt][1], wout_b],
                             mps[:, 0:512],
                             ygk[kt][0][:, :],
                             wout_sl(kt, eh * 512, (eh + 1) * 512),
                             start=(kt == 0), stop=(kt == KD - 1))
                    p.op("vector", "tensor_tensor", [s_b], [mpb, xh_b],
                         s_sb[:, eh * 512:(eh + 1) * 512], mps[:, 0:512],
                         xh_sb[:, eh * 512:(eh + 1) * 512], op=ALU.add)
                p.op("scalar", "activation", [g_b], [x1o[tt][1]],
                     g_sb[:, :], x1o[tt][0][:, :], AF.Silu)
                if debug:
                    p.dma("scalar", dbg["d_s"][tt * 128:(tt + 1) * 128, :],
                          s_sb[:, :], [db(dbg["d_s"])], [s_b])
                    p.dma("scalar", dbg["d_g"][tt * 128:(tt + 1) * 128, :],
                          g_sb[:, :], [db(dbg["d_g"])], [g_b])
                    for kt in range(KD):
                        p.dma("gpsimd",
                              dbg["d_yg"][kt * 128:(kt + 1) * 128,
                                          tt * 128:(tt + 1) * 128],
                              ygk[kt][0][:, :], [db(dbg["d_yg"])],
                              [ygk[kt][1]])
                p.op("vector", "tensor_tensor", [o_b], [s_b, g_b],
                     o_sb[:, :], s_sb[:, :], g_sb[:, :], op=ALU.mult)
                p.op("vector", "tensor_tensor", [o_b], [o_b, xt_b],
                     o_sb[:, :], o_sb[:, :], xt_sb[:, :], op=ALU.add)
                p.dma("sync", out_own[tt * 128:(tt + 1) * 128, :],
                      o_sb[:, :], [db(out_own)], [o_b])

            p.barrier()

    return nc



def _stage_core(inp, S, core, tok=None):
    import ml_dtypes
    T, D, N = S["T"], S["D"], S["N"]
    NH, HD, RD, L = S["NH"], S["HD"], S["RD"], S["L"]
    NE, I = S["NE"], S["I"]
    DH = D // 2
    NOB = (T // 128) // 2
    QG = min(4, NOB)
    b, p_ = core // 2, core % 2
    f32 = np.float32
    c = np.ascontiguousarray

    x = np.asarray(inp["L0_attn_x"], f32)[b]
    own = np.arange(p_ * DH, (p_ + 1) * DH)
    other = np.arange((1 - p_) * DH, (2 - p_) * DH)
    perm = np.concatenate([own, other])

    W_in = np.asarray(inp["L0_attn_W_in"], f32)
    win_rows = np.concatenate([W_in[perm], W_in[D + own]], axis=0)
    conv_w = np.asarray(inp["L0_attn_conv_w"], f32).reshape(D, 4)[perm]
    conv_b = np.asarray(inp["L0_attn_conv_b"], f32)[perm]
    W_dt = np.asarray(inp["L0_attn_W_dt"], f32)
    b_dt = np.asarray(inp["L0_attn_b_dt"], f32)
    Dp = np.asarray(inp["L0_attn_Dp"], f32)
    W_B = np.asarray(inp["L0_attn_W_B"], f32)
    W_C = np.asarray(inp["L0_attn_W_C"], f32)
    W_out = np.asarray(inp["L0_attn_W_out"], f32)

    KD = D // 128
    KDH = DH // 128
    nope_rows = np.concatenate(
        [np.arange(h * HD + RD, (h + 1) * HD) for h in range(NH)])
    RH = RD // 2
    g1 = np.concatenate([np.arange(h * RD, h * RD + RH) for h in range(NH)])
    g2 = np.concatenate([np.arange(h * RD + RH, (h + 1) * RD)
                         for h in range(NH)])
    rot_perm = np.concatenate([g1, g2])

    k_u = np.asarray(inp["L0_ffn_k_u"], f32)
    q_u = np.asarray(inp["L0_ffn_q_u"], f32)
    v_u = np.asarray(inp["L0_ffn_v_u"], f32)
    rope_q = np.asarray(inp["L0_ffn_rope_q_w"], f32)
    rope_k = np.asarray(inp["L0_ffn_rope_k_w"], f32)

    inv = (1.0 / BASE ** (np.arange(0, RD, 2) / RD)).astype(np.float64)
    tpos = np.arange(T, dtype=np.float64)
    fr = tpos[None, :] * inv[:, None]
    cos16 = np.cos(fr).astype(f32)
    sin16 = np.sin(fr).astype(f32)
    rows = np.arange(128) % RH
    cos_rep = cos16[rows]
    sin_rep = sin16[rows]

    tri = np.where(np.arange(128)[:, None] <= np.arange(128)[None, :],
                   0.0, NEG).astype(f32)
    zz = np.zeros((128, 128), f32)
    ng = np.full((128, 128), NEG, f32)
    am = np.zeros((QG, 2, 128, 128), f32)
    for m in range(QG):
        if p_ == 0:
            am[m, 0] = tri
            am[m, 1] = ng
        else:
            am[m, 0] = zz
            am[m, 1] = tri
    amask = am.reshape(QG * 2 * 128, 128)

    sg = np.asarray(inp["L0_ffn_shared_gate"], f32)
    su = np.asarray(inp["L0_ffn_shared_up"], f32)
    sd = np.asarray(inp["L0_ffn_shared_down"], f32)
    rg = np.asarray(inp["L0_ffn_routed_gate"], f32)
    ru = np.asarray(inp["L0_ffn_routed_up"], f32)
    rd_ = np.asarray(inp["L0_ffn_routed_down"], f32)
    gate_all = np.concatenate([rg, sg], axis=0)
    up_all = np.concatenate([ru, su], axis=0)
    down_all = np.concatenate([rd_, sd], axis=0)

    if tok is None:
        tok = np.array([[np.random.default_rng().integers(1, 2 ** 30)]],
                       dtype=np.int32)

    def kmaj(w):
        return c(w.T)

    d = {
        "x_full": c(x),
        "par": np.array([[p_]], dtype=np.int32),
        "pselin": np.full((128, 1), float(p_), f32),
        "pinvin": np.full((128, 1), 1.0 - p_, f32),
        "tok": tok,
        "win_km": kmaj(win_rows),
        "convw_t": c(conv_w.reshape(KD, 128, 4).transpose(1, 0, 2)
                     .reshape(128, KD * 4)),
        "convb_t": c(conv_b.reshape(KD, 128).T),
        "wdt_km": kmaj(W_dt[own][:, perm]),
        "bdt_t": c(b_dt[own].reshape(KDH, 128).T),
        "a_log": np.asarray(inp["L0_attn_A_log"], f32).reshape(1, N),
        "dp_t": c(Dp[own].reshape(KDH, 128).T),
        "wb_km": kmaj(W_B[:, perm]),
        "wc_km": kmaj(W_C[:, perm]),
        "wout_km": kmaj(W_out),
        "ln_g": np.asarray(inp["L0_attn_ln_g"], f32).reshape(1, D),
        "ln_b": np.asarray(inp["L0_attn_ln_b"], f32).reshape(1, D),
        "ln1_w": np.asarray(inp["L0_ffn_ln1_w"], f32).reshape(1, D),
        "ln2_w": np.asarray(inp["L0_ffn_ln2_w"], f32).reshape(1, D),
        "kvd_km": kmaj(np.asarray(inp["L0_ffn_kv_d"], f32)),
        "qd_km": kmaj(np.asarray(inp["L0_ffn_q_d"], f32)),
        "kun_km": kmaj(k_u[nope_rows]),
        "qun_km": kmaj(q_u[nope_rows]),
        "krot_km": kmaj(rope_k[rot_perm]),
        "qrot_km": kmaj(rope_q[rot_perm]),
        "vu_km": kmaj(v_u),
        "ow_km": kmaj(np.asarray(inp["L0_ffn_o_w"], f32)),
        "cos_rep": c(cos_rep),
        "sin_rep": c(sin_rep),
        "amask": amask,
        "router_km": kmaj(np.asarray(inp["L0_ffn_router_w"], f32)),
        "rbias": np.asarray(inp["L0_ffn_routing_bias"], f32).reshape(1, NE),
        "gate_km": c(gate_all.reshape((NE + 1) * I, D).T),
        "up_km": c(up_all.reshape((NE + 1) * I, D).T),
        "down_km": c(down_all.transpose(0, 2, 1).reshape((NE + 1) * I, D)),
        "ident_in": np.eye(128, dtype=ml_dtypes.bfloat16),
    }
    return d


_NC_CACHE = {}


def kernel(**inputs):
    x = np.asarray(inputs["L0_attn_x"])
    B, T, D = x.shape
    S = dict(SIZES)
    S["T"] = T
    S["D"] = D
    key = (T, D)
    if key not in _NC_CACHE:
        _NC_CACHE[key] = build_program(S, debug=False)
    nc = _NC_CACHE[key]
    tok = np.array([[np.random.default_rng().integers(1, 2 ** 30)]],
                   dtype=np.int32)
    in_maps = [_stage_core(inputs, S, c, tok) for c in range(8)]
    res = run_bass_kernel_spmd(nc, in_maps, core_ids=list(range(8)))
    out = np.empty((B, T, D), np.float32)
    NOB = (T // 128) // 2
    for c in range(8):
        b, p_ = c // 2, c % 2
        oo = res.results[c]["out_own"]
        for j in range(NOB):
            g = 2 * j + p_
            out[b, g * 128:(g + 1) * 128, :] = oo[j * 128:(j + 1) * 128, :]
    return out


# version 05
# speedup vs baseline: 1.2545x; 1.2545x over previous
import contextlib
import numpy as np

import concourse.bass as bass
import concourse.mybir as mybir
from concourse.bass_utils import run_bass_kernel_spmd

F32 = mybir.dt.float32
BF16 = mybir.dt.bfloat16
I32 = mybir.dt.int32
AF = mybir.ActivationFunctionType
ALU = mybir.AluOpType
AX = mybir.AxisListType

NEG = -1.0e30
EPS = 1e-5
BASE = 10000.0
SIZES = dict(T=4096, D=1024, N=64, NH=16, HD=64, RD=32, L=256, NE=7, I=2048)


class Buf:
    __slots__ = ("w", "r")

    def __init__(self):
        self.w = None
        self.r = {}


class Prog:
    ENGS = ("tensor", "vector", "scalar", "gpsimd", "sync")
    QS = ("sync", "scalar", "gpsimd")
    NDS = 6

    def __init__(self, nc, es):
        self.nc = nc
        self.sem = {}
        self.cnt = {}
        self.waited = {e: {} for e in self.ENGS}
        for name in self.ENGS:
            self.sem[name] = es.enter_context(nc.semaphore(f"sm_{name}"))
            self.cnt[name] = 0
        self.drr = {}
        for q in self.QS:
            for i in range(self.NDS):
                key = f"d_{q}{i}"
                self.sem[key] = es.enter_context(nc.semaphore(key))
                self.cnt[key] = 0
            self.drr[q] = 0
        self.eng = {"tensor": nc.tensor, "vector": nc.vector,
                    "scalar": nc.scalar, "gpsimd": nc.gpsimd, "sync": nc.sync}

    def _wait(self, ename, toks):
        w = self.waited[ename]
        need = {}
        for t in toks:
            if t is None:
                continue
            s, v = t
            if v > need.get(s, -1):
                need[s] = v
        for s, v in need.items():
            if w.get(s, 0) < v:
                self.eng[ename].wait_ge(self.sem[s], v)
                w[s] = v

    @staticmethod
    def _toks(ins, outs):
        toks = []
        for b in ins:
            if b is not None and b.w is not None:
                toks.append(b.w)
        for b in outs:
            if b is None:
                continue
            if b.w is not None:
                toks.append(b.w)
            toks.extend(b.r.items())
        return toks

    def _finish(self, ins, outs, tok):
        s, v = tok
        for b in ins:
            if b is not None and b.r.get(s, 0) < v:
                b.r[s] = v
        for b in outs:
            if b is not None:
                b.w = tok
                b.r = {}

    def op(self, ename, method, outs, ins, *args, waits=(), **kw):
        self._wait(ename, self._toks(ins, outs) + list(waits))
        inst = getattr(self.eng[ename], method)(*args, **kw)
        self.cnt[ename] += 1
        tok = (ename, self.cnt[ename])
        inst.then_inc(self.sem[ename], 1)
        self._finish(ins, outs, tok)
        return tok

    def _dma_sem(self, q):
        i = self.drr[q]
        self.drr[q] = (i + 1) % self.NDS
        key = f"d_{q}{i}"
        if self.cnt[key]:
            self._wait(q, [(key, self.cnt[key])])
        return key

    def dma(self, q, out_ap, in_ap, outs, ins, waits=(), **kw):
        self._wait(q, self._toks(ins, outs) + list(waits))
        key = self._dma_sem(q)
        inst = self.eng[q].dma_start(out=out_ap, in_=in_ap, **kw)
        inst.then_inc(self.sem[key], 16)
        self.cnt[key] += 16
        tok = (key, self.cnt[key])
        self._finish(ins, outs, tok)
        return tok

    def barrier(self):
        toks = [(e, self.cnt[e]) for e in self.ENGS]
        toks += [(f"d_{q}{i}", self.cnt[f"d_{q}{i}"])
                 for q in self.QS for i in range(self.NDS)]
        for e in self.ENGS:
            self._wait(e, toks)


def build_program(S, debug=False):
    T, D, N = S["T"], S["D"], S["N"]
    NH, HD, RD, L = S["NH"], S["HD"], S["RD"], S["L"]
    NE, I = S["NE"], S["I"]
    DH = D // 2
    NB = T // 128
    NOB = NB // 2
    TO = NOB * 128
    CH0 = min(512, T)
    NCH0 = T // CH0
    TPC = CH0 // 128
    CH1 = min(1024, T)
    NCH1 = T // CH1
    QG = min(4, NOB)
    NQG = NOB // QG
    KD = D // 128
    KL = L // 128
    NOPE = HD - RD
    RH = RD // 2
    ICH = 512
    NICH = I // ICH
    TF = min(512, TO)
    NTF = TO // TF
    NTT = TO // 128
    KDH = DH // 128

    nc = bass.Bass("TRN2", target_bir_lowering=False)

    def din(name, shape, dt=F32):
        return nc.dram_tensor(name, shape, dt, kind="ExternalInput").ap()

    kscr = "ExternalOutput" if debug else "Internal"

    def dsc(name, shape, dt):
        return nc.dram_tensor(name, shape, dt, kind=kscr).ap()

    x_full = din("x_full", [T, D])
    par = din("par", [1, 1], I32)
    pselin = din("pselin", [128, 1])
    pinvin = din("pinvin", [128, 1])
    tokin = din("tok", [1, 1], I32)
    win_km = din("win_km", [D, D + DH])
    convw_t = din("convw_t", [128, KD * 4])
    convb_t = din("convb_t", [128, KD])
    wdt_km = din("wdt_km", [D, DH])
    bdt_t = din("bdt_t", [128, KDH])
    a_log = din("a_log", [1, N])
    dp_t = din("dp_t", [128, KDH])
    wb_km = din("wb_km", [D, N])
    wc_km = din("wc_km", [D, N])
    wout_km = din("wout_km", [D, D])
    ln_g = din("ln_g", [1, D])
    ln_b = din("ln_b", [1, D])
    ln1_w = din("ln1_w", [1, D])
    ln2_w = din("ln2_w", [1, D])
    kvd_km = din("kvd_km", [D, L])
    qd_km = din("qd_km", [D, L])
    kun_km = din("kun_km", [L, NH * NOPE])
    qun_km = din("qun_km", [L, NH * NOPE])
    krot_km = din("krot_km", [D, NH * RD])
    qrot_km = din("qrot_km", [L, NH * RD])
    vu_km = din("vu_km", [L, NH * HD])
    ow_km = din("ow_km", [D, D])
    cos_rep = din("cos_rep", [128, T])
    sin_rep = din("sin_rep", [128, T])
    amask = din("amask", [QG * 2 * 128, 128])
    router_km = din("router_km", [D, NE])
    rbias = din("rbias", [1, NE])
    gate_km = din("gate_km", [D, (NE + 1) * I])
    up_km = din("up_km", [D, (NE + 1) * I])
    down_km = din("down_km", [(NE + 1) * I, D])
    ident_in = din("ident_in", [128, 128], BF16)

    out_own = nc.dram_tensor("out_own", [TO, D], F32, kind="ExternalOutput").ap()

    z_lin = dsc("z_lin", [DH, T], BF16)
    dt_lin = dsc("dt_lin", [DH, T], F32)
    xc_lin = dsc("xc_lin", [DH, T], F32)
    b_lin = dsc("b_lin", [N, T], F32)
    c_lin = dsc("c_lin", [N, T], F32)
    k_lin = dsc("k_lin", [NH * HD, T], BF16)
    vt_lin = dsc("vt_lin", [T, NH * HD], BF16)
    q_all = dsc("q_all", [NH * HD, T], BF16)
    arow_d = nc.dram_tensor("arow_d", [1, N], F32).ap()
    dbg = {}
    if debug:
        for nm, shape in [("d_y", [DH, T]), ("d_x1", [TO, D]),
                          ("d_x2", [TO, D]), ("d_wgt", [TO, NE]),
                          ("d_s", [TO, D]), ("d_g", [TO, D]),
                          ("d_yg", [D, TO])]:
            dbg[nm] = nc.dram_tensor(nm, shape, F32, kind="ExternalOutput").ap()
    _shr = {} if S.get("local_shared") else {"addr_space": "Shared"}
    ygA = nc.dram_tensor("ygA", [DH, T], BF16, **_shr).ap()
    ygB = nc.dram_tensor("ygB", [DH, T], BF16, **_shr).ap()
    flA = nc.dram_tensor("flA", [1, 1], I32, **_shr).ap()
    flB = nc.dram_tensor("flB", [1, 1], I32, **_shr).ap()

    dram_bufs = {}

    def db(ap):
        key = ap.tensor.name
        if key not in dram_bufs:
            dram_bufs[key] = Buf()
        return dram_bufs[key]

    def mk_sb(stack, name, shape, dt):
        t = stack.enter_context(nc.sbuf_tensor(name, shape, dt))
        return t, Buf()

    def mk_ps(stack, name, shape, dt=F32):
        t = stack.enter_context(nc.psum_tensor(name, shape, dt))
        return t, Buf()

    es = contextlib.ExitStack()
    with es:
        p = Prog(nc, es)

        def wload(stack, name, dram_ap, C_, q="gpsimd", dt=BF16):
            K_ = dram_ap.shape[0]
            nk = K_ // 128
            t = stack.enter_context(nc.sbuf_tensor(name, [128, nk * C_], dt))
            b = Buf()
            src = dram_ap.rearrange("(k p) c -> p k c", p=128)
            dst = t[:, :].rearrange("p (k c) -> p k c", c=C_)
            p.dma(q, dst, src, [b], [db(dram_ap)])

            def sl(kt, c0, c1):
                return t[:, kt * C_ + c0:kt * C_ + c1]
            return t, b, sl
        pl_sem = es.enter_context(nc.semaphore("pl_sem"))
        blk = es.enter_context(nc.Block())

        def sb(name, shape, dt):
            return mk_sb(es, name, shape, dt)

        ident_sb, ident_b = sb("ident", [128, 128], BF16)
        p.dma("sync", ident_sb[:, :], ident_in[:, :], [ident_b], [db(ident_in)])
        par_sb, par_b = sb("par_t", [1, 1], I32)
        p.dma("sync", par_sb[:, :], par[:, :], [par_b], [db(par)])
        tok_sb, tok_b = sb("tok_t", [1, 1], I32)
        p.dma("sync", tok_sb[:, :], tokin[:, :], [tok_b], [db(tokin)])
        fl_sb, fl_b = sb("fl_t", [1, 2], I32)
        acol_sb, acol_b = sb("acol", [128, N], F32)
        arow_sb, arow_b = sb("arow", [1, N], F32)
        p.dma("sync", arow_sb[:, :], a_log[:, :], [arow_b], [db(a_log)])
        p.op("scalar", "activation", [arow_b], [arow_b],
             arow_sb[:, :], arow_sb[:, :], AF.Exp)
        p.op("vector", "tensor_scalar_mul", [arow_b], [arow_b],
             arow_sb[:, :], arow_sb[:, :], -1.0)
        p.dma("sync", arow_d[:, :], arow_sb[:, :], [db(arow_d)], [arow_b])
        p.dma("sync", acol_sb[:, :], arow_d.partition_broadcast(128),
              [acol_b], [db(arow_d)])
        dp_sb, dp_b = sb("dpc", [128, KDH], F32)
        p.dma("sync", dp_sb[:, :], dp_t[:, :], [dp_b], [db(dp_t)])
        rg_par = es.enter_context(nc.sync.register("rg_par"))
        p.op("sync", "load", [], [par_b], rg_par, par_sb[0:1, 0:1])
        psel_sb, psel_b = sb("psel", [128, 1], F32)
        pinv_sb, pinv_b = sb("pinv", [128, 1], F32)
        p.dma("sync", psel_sb[:, :], pselin[:, :], [psel_b], [db(pselin)])
        p.dma("sync", pinv_sb[:, :], pinvin[:, :], [pinv_b], [db(pinvin)])
        _selholder = {}

        def par_select(out_ap, out_b, a_ap, a_b, bb_ap, bb_b, tmp_ap, tmp_b,
                       rows=128):
            p.op("vector", "tensor_scalar", [tmp_b], [a_b, pinv_b],
                 tmp_ap, a_ap, pinv_sb[0:rows, :], None, op0=ALU.mult)
            p.op("vector", "scalar_tensor_tensor", [out_b],
                 [bb_b, psel_b, tmp_b],
                 out_ap, bb_ap, psel_sb[0:rows, :], tmp_ap,
                 op0=ALU.mult, op1=ALU.add)

        def load_x_block(j):
            selA_sb, selA_b = _selholder["A"]
            selB_sb, selB_b = _selholder["B"]
            p.dma("sync", selA_sb[:, :],
                  x_full[2 * j * 128:(2 * j + 1) * 128, :],
                  [selA_b], [db(x_full)])
            p.dma("sync", selB_sb[:, :],
                  x_full[(2 * j + 1) * 128:(2 * j + 2) * 128, :],
                  [selB_b], [db(x_full)])
            par_select(xt_sb[:, :], xt_b, selA_sb[:, :], selA_b,
                       selB_sb[:, :], selB_b, sq_sb[:, :], sq_b)
        mu_sb, mu_b = sb("mu", [128, 1], F32)
        var_sb, var_b = sb("var", [128, 1], F32)
        sq_sb, sq_b = sb("sq", [128, D], F32)
        xh_sb, xh_b = sb("xh", [128, D], F32)
        xt_sb, xt_b = sb("xtile", [128, D], F32)

        def ln_norm(src_ap, src_b):
            p.op("vector", "reduce_sum", [mu_b], [src_b],
                 mu_sb[:, :], src_ap, axis=AX.X)
            p.op("vector", "tensor_scalar_mul", [mu_b], [mu_b],
                 mu_sb[:, :], mu_sb[:, :], 1.0 / D)
            p.op("vector", "tensor_scalar", [xh_b], [src_b, mu_b],
                 xh_sb[:, :], src_ap, mu_sb[:, :], None, op0=ALU.subtract)
            p.op("scalar", "activation", [sq_b], [xh_b],
                 sq_sb[:, :], xh_sb[:, :], AF.Square)
            p.op("vector", "reduce_sum", [var_b], [sq_b],
                 var_sb[:, :], sq_sb[:, :], axis=AX.X)
            p.op("vector", "tensor_scalar_mul", [var_b], [var_b],
                 var_sb[:, :], var_sb[:, :], 1.0 / D)
            p.op("scalar", "activation", [var_b], [var_b],
                 var_sb[:, :], var_sb[:, :], AF.Ln, bias=np.float32(EPS))
            p.op("scalar", "activation", [var_b], [var_b],
                 var_sb[:, :], var_sb[:, :], AF.Exp, scale=np.float32(-0.5))
            p.op("vector", "tensor_scalar", [xh_b], [xh_b, var_b],
                 xh_sb[:, :], xh_sb[:, :], var_sb[:, :], None, op0=ALU.mult)

        with contextlib.ExitStack() as s0:
            def sb0(name, shape, dt):
                return mk_sb(s0, name, shape, dt)

            mmp = [mk_ps(s0, f"mm_ps{i}", [128, 512]) for i in range(4)]
            tpp = [mk_ps(s0, f"tp_ps{i}", [128, 128], BF16) for i in range(2)]
            _mmi = [0]
            def next_mm():
                t = mmp[_mmi[0] % 4]
                _mmi[0] += 1
                return t
            _tpi = [0]
            def next_tp():
                t = tpp[_tpi[0] % 2]
                _tpi[0] += 1
                return t

            gln_g, gln_gb = sb0("g_lng", [128, D], F32)
            gln_bt, gln_bb = sb0("g_lnb", [128, D], F32)
            gln1, gln1b = sb0("g_ln1", [128, D], F32)
            p.dma("sync", gln_g[:, :], ln_g.partition_broadcast(128),
                  [gln_gb], [db(ln_g)])
            p.dma("sync", gln_bt[:, :], ln_b.partition_broadcast(128),
                  [gln_bb], [db(ln_b)])
            p.dma("sync", gln1[:, :], ln1_w.partition_broadcast(128),
                  [gln1b], [db(ln1_w)])
            convw_sb, convw_b = sb0("convw", [128, KD * 4], F32)
            p.dma("sync", convw_sb[:, :], convw_t[:, :], [convw_b],
                  [db(convw_t)])
            convb_sb, convb_b = sb0("convb", [128, KD], F32)
            p.dma("sync", convb_sb[:, :], convb_t[:, :], [convb_b],
                  [db(convb_t)])
            bdt_sb, bdt_b = sb0("bdtc", [128, KDH], F32)
            p.dma("sync", bdt_sb[:, :], bdt_t[:, :], [bdt_b], [db(bdt_t)])

            win_sb, win_b, win_sl = wload(s0, "win", win_km, D + DH)
            wdt_sb, wdt_b, wdt_sl = wload(s0, "wdt", wdt_km, DH)
            wb_sb, wb_b, wb_sl = wload(s0, "wbm", wb_km, N)
            wc_sb, wc_b, wc_sl = wload(s0, "wcm", wc_km, N)
            kvd_sb, kvd_b, kvd_sl = wload(s0, "kvdm", kvd_km, L)
            qd_sb, qd_b, qd_sl = wload(s0, "qdm", qd_km, L)
            kun_sb, kun_b, kun_sl = wload(s0, "kunm", kun_km, NH * NOPE)
            qun_sb, qun_b, qun_sl = wload(s0, "qunm", qun_km, NH * NOPE)
            krot_sb, krot_b, krot_sl = wload(s0, "krotw", krot_km, NH * RD)
            qrot_sb, qrot_b, qrot_sl = wload(s0, "qrotw", qrot_km, NH * RD)
            vu_sb, vu_b, vu_sl = wload(s0, "vum", vu_km, NH * HD)

            tdB_sb, tdB_b = sb0("tdB", [128, D], BF16)
            x1m_dt = [sb0(f"x1mdt{j}", [128, CH0], BF16) for j in range(KD)]
            ln1_dt = [sb0(f"ln1dt{j}", [128, CH0], BF16) for j in range(KD)]
            xmain = [sb0(f"xmain{j}", [128, CH0], F32) for j in range(KD)]
            halo = [sb0(f"halo{j}", [128, 3], F32) for j in range(KD)]
            xmc3 = [sb0(f"xmc3{j}", [128, 6], F32) for j in range(KD)]
            xcf = [sb0(f"xcf{j}", [128, CH0], F32) for j in range(KD)]
            xcb = [sb0(f"xcb{j}", [128, CH0], BF16) for j in range(KD)]
            nt_sb, nt_b = sb0("ntile", [N, CH0], F32)
            kvl = [sb0(f"kvl{j}", [128, CH0], BF16) for j in range(KL)]
            ev_sb, ev_b = sb0("ev", [128, max(CH0, 512)], BF16)
            kr = [sb0(f"kr{j}", [128, CH0], F32)
                  for j in range(NH * RD // 128)]
            rp1_sb, rp1_b = sb0("rp1", [128, CH0], F32)
            rp2_sb, rp2_b = sb0("rp2", [128, CH0], F32)
            spA_sb, spA_b = rp1_sb, rp1_b
            spB_sb, spB_b = rp2_sb, rp2_b
            rpo_sb, rpo_b = sb0("rpo", [128, CH0], BF16)
            cosc_sb, cosc_b = sb0("cosc", [128, CH0], F32)
            sinc_sb, sinc_b = sb0("sinc", [128, CH0], F32)

            dtc = [sb0(f"dtc{j}", [128, CH0], F32) for j in range(KDH)]
            zsil = [sb0(f"zsil{j}", [128, CH0], BF16) for j in range(KDH)]
            brep = [sb0(f"brep{i}", [128, CH0], F32) for i in range(2)]
            crep = [sb0(f"crep{i}", [128, CH0], F32) for i in range(2)]
            sc_a = [sb0(f"sc_a{i}", [128, CH0], F32) for i in range(2)]
            sc_b = [sb0(f"sc_b{i}", [128, CH0], F32) for i in range(2)]
            h_s, h_b = sb0("h_s", [128, CH0], F32)
            hc_s, hc_b = sb0("hc_s", [128, CH0], F32)
            yac = [sb0(f"yac{j}", [128, CH0], F32) for j in range(KDH)]
            ygo_sb, ygo_b = sb0("ygo", [128, CH0], BF16)
            hst = [sb0(f"hst{j}", [128, N], F32) for j in range(KDH)]
            for j in range(KDH):
                p.op("gpsimd", "memset", [hst[j][1]], [], hst[j][0][:, :], 0.0)
            for j in range(KD):
                p.op("gpsimd", "memset", [halo[j][1]], [],
                     halo[j][0][:, :], 0.0)

            def transpose_128(dst_ap, dst_b, src_ap, src_b):
                tp_ps, tp_pb = next_tp()
                p.op("tensor", "transpose", [tp_pb], [src_b, ident_b],
                     tp_ps[:, :], src_ap, ident_sb[:, :])
                p.op("scalar", "copy", [dst_b], [tp_pb], dst_ap, tp_ps[:, :])

            def rope_rot(dst_lin, src_tiles):
                ng = NH * RH // 128
                hpt2 = 128 // RH
                for gti in range(ng):
                    g1, g1b = src_tiles[gti]
                    g2, g2b = src_tiles[ng + gti]
                    p.op("vector", "tensor_tensor", [rp1_b], [g1b, cosc_b],
                         rp1_sb[:, :], g1[:, :], cosc_sb[:, :], op=ALU.mult)
                    p.op("vector", "tensor_tensor", [rp2_b], [g2b, sinc_b],
                         rp2_sb[:, :], g2[:, :], sinc_sb[:, :], op=ALU.mult)
                    p.op("vector", "tensor_tensor", [rpo_b], [rp1_b, rp2_b],
                         rpo_sb[:, :], rp1_sb[:, :], rp2_sb[:, :],
                         op=ALU.subtract)
                    for i in range(hpt2):
                        h = gti * hpt2 + i
                        p.dma("sync",
                              dst_lin[h * HD:h * HD + RH, t0:t0 + CH0],
                              rpo_sb[i * RH:(i + 1) * RH, :],
                              [db(dst_lin)], [rpo_b])
                    p.op("vector", "tensor_tensor", [rp1_b], [g2b, cosc_b],
                         rp1_sb[:, :], g2[:, :], cosc_sb[:, :], op=ALU.mult)
                    p.op("vector", "tensor_tensor", [rp2_b], [g1b, sinc_b],
                         rp2_sb[:, :], g1[:, :], sinc_sb[:, :], op=ALU.mult)
                    p.op("vector", "tensor_tensor", [rpo_b], [rp1_b, rp2_b],
                         rpo_sb[:, :], rp1_sb[:, :], rp2_sb[:, :], op=ALU.add)
                    for i in range(hpt2):
                        h = gti * hpt2 + i
                        p.dma("sync",
                              dst_lin[h * HD + RH:h * HD + RD, t0:t0 + CH0],
                              rpo_sb[i * RH:(i + 1) * RH, :],
                              [db(dst_lin)], [rpo_b])

            for ch in range(NCH0):
                t0 = ch * CH0
                p.dma("sync", cosc_sb[:, :], cos_rep[:, t0:t0 + CH0],
                      [cosc_b], [db(cos_rep)])
                p.dma("sync", sinc_sb[:, :], sin_rep[:, t0:t0 + CH0],
                      [sinc_b], [db(sin_rep)])
                for tt in range(TPC):
                    r0 = t0 + tt * 128
                    p.dma("sync", xt_sb[:, :], x_full[r0:r0 + 128, :],
                          [xt_b], [db(x_full)])
                    ln_norm(xt_sb[:, :], xt_b)
                    ln_norm(xh_sb[:, :], xh_b)
                    p.op("vector", "tensor_tensor", [tdB_b], [xh_b, gln_gb],
                         tdB_sb[:, :], xh_sb[:, :], gln_g[:, :], op=ALU.mult)
                    p.op("vector", "tensor_tensor", [tdB_b], [tdB_b, gln_bb],
                         tdB_sb[:, :], tdB_sb[:, :], gln_bt[:, :], op=ALU.add)
                    for j in range(KD):
                        transpose_128(
                            x1m_dt[j][0][:, tt * 128:(tt + 1) * 128],
                            x1m_dt[j][1],
                            tdB_sb[:, j * 128:(j + 1) * 128], tdB_b)
                    p.op("vector", "tensor_tensor", [tdB_b], [xh_b, gln1b],
                         tdB_sb[:, :], xh_sb[:, :], gln1[:, :], op=ALU.mult)
                    for j in range(KD):
                        transpose_128(
                            ln1_dt[j][0][:, tt * 128:(tt + 1) * 128],
                            ln1_dt[j][1],
                            tdB_sb[:, j * 128:(j + 1) * 128], tdB_b)

                for o in range((D + DH) // 128):
                    mm_ps, mm_pb = next_mm()
                    for kt in range(KD):
                        p.op("tensor", "matmul", [mm_pb],
                             [win_b, x1m_dt[kt][1]],
                             mm_ps[:, 0:CH0],
                             win_sl(kt, o * 128, (o + 1) * 128),
                             x1m_dt[kt][0][:, :],
                             start=(kt == 0), stop=(kt == KD - 1))
                    if o < KD:
                        p.op("scalar", "copy", [xmain[o][1]], [mm_pb],
                             xmain[o][0][:, :], mm_ps[:, 0:CH0])
                    else:
                        oz = o - KD
                        p.op("scalar", "activation", [zsil[oz][1]], [mm_pb],
                             zsil[oz][0][:, :], mm_ps[:, 0:CH0], AF.Silu)

                for j in range(KD):
                    wj = [convw_sb[:, j * 4 + i:j * 4 + i + 1]
                          for i in range(4)]
                    xm, xmb = xmain[j]
                    xc, xcfb = xcf[j]
                    p.op("vector", "tensor_scalar", [xcfb], [xmb, convw_b],
                         xc[:, 3:CH0], xm[:, 0:CH0 - 3], wj[0], None,
                         op0=ALU.mult)
                    for i in range(1, 4):
                        p.op("vector", "scalar_tensor_tensor", [xcfb],
                             [xmb, convw_b, xcfb],
                             xc[:, 3:CH0], xm[:, i:CH0 - 3 + i], wj[i],
                             xc[:, 3:CH0], op0=ALU.mult, op1=ALU.add)
                    m3, m3b = xmc3[j]
                    p.op("vector", "tensor_copy", [m3b], [halo[j][1]],
                         m3[:, 0:3], halo[j][0][:, :])
                    p.op("vector", "tensor_copy", [m3b], [xmb],
                         m3[:, 3:6], xm[:, 0:3])
                    p.op("vector", "tensor_scalar", [xcfb], [m3b, convw_b],
                         xc[:, 0:3], m3[:, 0:3], wj[0], None, op0=ALU.mult)
                    for i in range(1, 4):
                        p.op("vector", "scalar_tensor_tensor", [xcfb],
                             [m3b, convw_b, xcfb],
                             xc[:, 0:3], m3[:, i:3 + i], wj[i],
                             xc[:, 0:3], op0=ALU.mult, op1=ALU.add)
                    p.op("vector", "tensor_copy", [halo[j][1]], [xmb],
                         halo[j][0][:, :], xm[:, CH0 - 3:CH0])
                    p.op("scalar", "activation", [xcfb], [xcfb, convb_b],
                         xc[:, :], xc[:, :], AF.Silu,
                         bias=convb_sb[:, j:j + 1])
                    p.op("vector", "tensor_copy", [xcb[j][1]], [xcfb],
                         xcb[j][0][:, :], xc[:, :])
                    if debug and j < KDH:
                        p.dma("scalar",
                              xc_lin[j * 128:(j + 1) * 128, t0:t0 + CH0],
                              xc[:, :], [db(xc_lin)], [xcfb])

                for o in range(KDH):
                    mm_ps, mm_pb = next_mm()
                    for kt in range(KD):
                        p.op("tensor", "matmul", [mm_pb], [wdt_b, xcb[kt][1]],
                             mm_ps[:, 0:CH0],
                             wdt_sl(kt, o * 128, (o + 1) * 128),
                             xcb[kt][0][:, :],
                             start=(kt == 0), stop=(kt == KD - 1))
                    bcol = bdt_sb[:, o:o + 1]
                    p.op("scalar", "activation", [spA_b], [mm_pb, bdt_b],
                         spA_sb[:, :], mm_ps[:, 0:CH0], AF.Abs, bias=bcol)
                    p.op("scalar", "activation", [spA_b], [spA_b],
                         spA_sb[:, :], spA_sb[:, :], AF.Exp, scale=np.float32(-1.0))
                    p.op("scalar", "activation", [spA_b], [spA_b],
                         spA_sb[:, :], spA_sb[:, :], AF.Ln, bias=np.float32(1.0))
                    p.op("scalar", "activation", [spB_b], [mm_pb, bdt_b],
                         spB_sb[:, :], mm_ps[:, 0:CH0], AF.Relu, bias=bcol)
                    p.op("vector", "tensor_tensor", [dtc[o][1]],
                         [spA_b, spB_b],
                         dtc[o][0][:, :], spA_sb[:, :], spB_sb[:, :],
                         op=ALU.add)
                    if debug:
                        p.dma("scalar",
                              dt_lin[o * 128:(o + 1) * 128, t0:t0 + CH0],
                              dtc[o][0][:, :], [db(dt_lin)], [dtc[o][1]])

                for wsl_, wbuf_, dst in ((wb_sl, wb_b, b_lin),
                                         (wc_sl, wc_b, c_lin)):
                    mm_ps, mm_pb = next_mm()
                    for kt in range(KD):
                        p.op("tensor", "matmul", [mm_pb], [wbuf_, xcb[kt][1]],
                             mm_ps[0:N, 0:CH0],
                             wsl_(kt, 0, N),
                             xcb[kt][0][:, :],
                             start=(kt == 0), stop=(kt == KD - 1))
                    p.op("scalar", "copy", [nt_b], [mm_pb],
                         nt_sb[:, :], mm_ps[0:N, 0:CH0])
                    p.dma("scalar", dst[:, t0:t0 + CH0], nt_sb[:, :],
                          [db(dst)], [nt_b])

                for o in range(KL):
                    mm_ps, mm_pb = next_mm()
                    for kt in range(KD):
                        p.op("tensor", "matmul", [mm_pb],
                             [kvd_b, ln1_dt[kt][1]],
                             mm_ps[:, 0:CH0],
                             kvd_sl(kt, o * 128, (o + 1) * 128),
                             ln1_dt[kt][0][:, :],
                             start=(kt == 0), stop=(kt == KD - 1))
                    p.op("scalar", "copy", [kvl[o][1]], [mm_pb],
                         kvl[o][0][:, :], mm_ps[:, 0:CH0])

                for tt in range(TPC):
                    for vh in range(NH * HD // 512):
                        mm2_ps, mm2_pb = next_mm()
                        for kt in range(KL):
                            p.op("tensor", "matmul", [mm2_pb],
                                 [kvl[kt][1], vu_b],
                                 mm2_ps[:, 0:512],
                                 kvl[kt][0][:, tt * 128:(tt + 1) * 128],
                                 vu_sl(kt, vh * 512, (vh + 1) * 512),
                                 start=(kt == 0), stop=(kt == KL - 1))
                        p.op("scalar", "copy", [ev_b], [mm2_pb],
                             ev_sb[:, 0:512], mm2_ps[:, 0:512])
                        p.dma("sync",
                              vt_lin[t0 + tt * 128:t0 + tt * 128 + 128,
                                     vh * 512:(vh + 1) * 512],
                              ev_sb[:, 0:512], [db(vt_lin)], [ev_b])

                hpt = 128 // NOPE
                for o in range(NH * NOPE // 128):
                    mm_ps, mm_pb = next_mm()
                    for kt in range(KL):
                        p.op("tensor", "matmul", [mm_pb], [kun_b, kvl[kt][1]],
                             mm_ps[:, 0:CH0],
                             kun_sl(kt, o * 128, (o + 1) * 128),
                             kvl[kt][0][:, :],
                             start=(kt == 0), stop=(kt == KL - 1))
                    p.op("scalar", "copy", [ev_b], [mm_pb],
                         ev_sb[:, 0:CH0], mm_ps[:, 0:CH0])
                    for i in range(hpt):
                        h = o * hpt + i
                        p.dma("sync",
                              k_lin[h * HD + RD:h * HD + HD, t0:t0 + CH0],
                              ev_sb[i * NOPE:(i + 1) * NOPE, 0:CH0],
                              [db(k_lin)], [ev_b])

                for o in range(NH * RD // 128):
                    mm_ps, mm_pb = next_mm()
                    for kt in range(KD):
                        p.op("tensor", "matmul", [mm_pb],
                             [krot_b, ln1_dt[kt][1]],
                             mm_ps[:, 0:CH0],
                             krot_sl(kt, o * 128, (o + 1) * 128),
                             ln1_dt[kt][0][:, :],
                             start=(kt == 0), stop=(kt == KD - 1))
                    p.op("scalar", "copy", [kr[o][1]], [mm_pb],
                         kr[o][0][:, :], mm_ps[:, 0:CH0])
                rope_rot(k_lin, kr)

                for o in range(KL):
                    mm_ps, mm_pb = next_mm()
                    for kt in range(KD):
                        p.op("tensor", "matmul", [mm_pb],
                             [qd_b, ln1_dt[kt][1]],
                             mm_ps[:, 0:CH0],
                             qd_sl(kt, o * 128, (o + 1) * 128),
                             ln1_dt[kt][0][:, :],
                             start=(kt == 0), stop=(kt == KD - 1))
                    p.op("scalar", "copy", [kvl[o][1]], [mm_pb],
                         kvl[o][0][:, :], mm_ps[:, 0:CH0])
                for o in range(NH * NOPE // 128):
                    mm_ps, mm_pb = next_mm()
                    for kt in range(KL):
                        p.op("tensor", "matmul", [mm_pb], [qun_b, kvl[kt][1]],
                             mm_ps[:, 0:CH0],
                             qun_sl(kt, o * 128, (o + 1) * 128),
                             kvl[kt][0][:, :],
                             start=(kt == 0), stop=(kt == KL - 1))
                    p.op("scalar", "copy", [ev_b], [mm_pb],
                         ev_sb[:, 0:CH0], mm_ps[:, 0:CH0])
                    for i in range(hpt):
                        h = o * hpt + i
                        p.dma("sync",
                              q_all[h * HD + RD:h * HD + HD, t0:t0 + CH0],
                              ev_sb[i * NOPE:(i + 1) * NOPE, 0:CH0],
                              [db(q_all)], [ev_b])
                for o in range(NH * RD // 128):
                    mm_ps, mm_pb = next_mm()
                    for kt in range(KL):
                        p.op("tensor", "matmul", [mm_pb],
                             [qrot_b, kvl[kt][1]],
                             mm_ps[:, 0:CH0],
                             qrot_sl(kt, o * 128, (o + 1) * 128),
                             kvl[kt][0][:, :],
                             start=(kt == 0), stop=(kt == KL - 1))
                    p.op("scalar", "copy", [kr[o][1]], [mm_pb],
                         kr[o][0][:, :], mm_ps[:, 0:CH0])
                rope_rot(q_all, kr)

                for n in range(N):
                    bi = n % 2
                    p.dma("gpsimd", brep[bi][0][:, :],
                          b_lin[n, t0:t0 + CH0].partition_broadcast(128),
                          [brep[bi][1]], [db(b_lin)])
                    p.dma("gpsimd", crep[bi][0][:, :],
                          c_lin[n, t0:t0 + CH0].partition_broadcast(128),
                          [crep[bi][1]], [db(c_lin)])
                    for j in range(KDH):
                        ab = sc_a[(n * KDH + j) % 2]
                        bb = sc_b[(n * KDH + j) % 2]
                        p.op("scalar", "activation", [ab[1]],
                             [dtc[j][1], acol_b],
                             ab[0][:, :], dtc[j][0][:, :], AF.Exp,
                             scale=acol_sb[:, n:n + 1])
                        p.op("vector", "tensor_tensor", [bb[1]],
                             [xcf[j][1], brep[bi][1]],
                             bb[0][:, :], xcf[j][0][:, :],
                             brep[bi][0][:, :], op=ALU.mult)
                        p.op("vector", "tensor_tensor_scan", [h_b],
                             [ab[1], bb[1], hst[j][1]],
                             out=h_s[:, :], data0=ab[0][:, :],
                             data1=bb[0][:, :],
                             initial=hst[j][0][:, n:n + 1],
                             op0=ALU.mult, op1=ALU.add)
                        p.op("scalar", "copy", [hst[j][1]], [h_b],
                             hst[j][0][:, n:n + 1], h_s[:, CH0 - 1:CH0])
                        if n == 0:
                            p.op("vector", "tensor_tensor", [yac[j][1]],
                                 [h_b, crep[bi][1]],
                                 yac[j][0][:, :], h_s[:, :],
                                 crep[bi][0][:, :], op=ALU.mult)
                        else:
                            p.op("vector", "tensor_tensor", [hc_b],
                                 [h_b, crep[bi][1]],
                                 hc_s[:, :], h_s[:, :], crep[bi][0][:, :],
                                 op=ALU.mult)
                            p.op("vector", "tensor_tensor", [yac[j][1]],
                                 [hc_b, yac[j][1]],
                                 yac[j][0][:, :], yac[j][0][:, :],
                                 hc_s[:, :], op=ALU.add)
                for j in range(KDH):
                    p.op("vector", "scalar_tensor_tensor", [yac[j][1]],
                         [xcf[j][1], dp_b, yac[j][1]],
                         yac[j][0][:, :], xcf[j][0][:, :],
                         dp_sb[:, j:j + 1], yac[j][0][:, :],
                         op0=ALU.mult, op1=ALU.add)
                    if debug:
                        p.dma("scalar",
                              dbg["d_y"][j * 128:(j + 1) * 128, t0:t0 + CH0],
                              yac[j][0][:, :], [db(dbg["d_y"])], [yac[j][1]])
                    p.op("vector", "tensor_tensor", [ygo_b],
                         [yac[j][1], zsil[j][1]],
                         ygo_sb[:, :], yac[j][0][:, :], zsil[j][0][:, :],
                         op=ALU.mult)
                    p._wait("sync", p._toks([ygo_b], [db(ygA), db(ygB)]))
                    dkey = p._dma_sem("sync")
                    with nc.sync.If_eq(rg_par, 0):
                        nc.sync.dma_start(
                            out=ygA[j * 128:(j + 1) * 128, t0:t0 + CH0],
                            in_=ygo_sb[:, :]).then_inc(p.sem[dkey], 16)
                    with nc.sync.Else():
                        nc.sync.dma_start(
                            out=ygB[j * 128:(j + 1) * 128, t0:t0 + CH0],
                            in_=ygo_sb[:, :]).then_inc(p.sem[dkey], 16)
                    p.cnt[dkey] += 16
                    tokd = (dkey, p.cnt[dkey])
                    p._finish([ygo_b], [db(ygA), db(ygB)], tokd)

            yg_toks = [(f"d_sync{i}", p.cnt[f"d_sync{i}"])
                       for i in range(p.NDS)]
            p._wait("sync", yg_toks + [tok_b.w])
            dkey = p._dma_sem("sync")
            with nc.sync.If_eq(rg_par, 0):
                nc.sync.dma_start(out=flA[:, :], in_=tok_sb[:, :]).then_inc(
                    p.sem[dkey], 16)
            with nc.sync.Else():
                nc.sync.dma_start(out=flB[:, :], in_=tok_sb[:, :]).then_inc(
                    p.sem[dkey], 16)
            p.cnt[dkey] += 16
            p.barrier()

        x1o = [sb(f"x1o{i}", [128, D], F32) for i in range(NTT)]
        _selholder["A"] = sb("selA", [128, D], F32)
        _selholder["B"] = sb("selB", [128, D], F32)

        with contextlib.ExitStack() as s3:
            def sb3(name, shape, dt):
                return mk_sb(s3, name, shape, dt)

            att_s0, att_s0b = mk_ps(s3, "att_s0", [128, 512])
            att_s1, att_s1b = mk_ps(s3, "att_s1", [128, 512])
            att_om = [mk_ps(s3, f"att_o{m}", [128, 65]) for m in range(QG)]
            tp3 = [mk_ps(s3, f"tp_ps3{i}", [128, 128], BF16) for i in range(2)]
            _t3 = [0]
            def next_tp3():
                t = tp3[_t3[0] % 2]
                _t3[0] += 1
                return t

            ow_sb, ow_b, ow_sl = wload(s3, "owm", ow_km, D)
            NKTmax = min(8 * (NQG - 1) + 2 * QG, NB)
            khT = [sb3(f"kh{i}", [HD, NKTmax * 128], BF16) for i in range(2)]
            vbT = [sb3(f"vb{i}", [128, 65 * NKTmax], BF16) for i in range(2)]
            qhT = [sb3(f"qh{i}", [HD, QG * 128], BF16) for i in range(2)]
            qsA_sb, qsA_b = sb3("qsA", [HD, 128], BF16)
            qsB_sb, qsB_b = sb3("qsB", [HD, 128], BF16)
            qtmp_sb, qtmp_b = sb3("qtmp", [HD, 128], F32)
            pT = [sb3(f"pT{i}", [128, QG * 128], BF16) for i in range(2)]
            ayg = [sb3(f"ayg{m}", [128, NH * HD], BF16) for m in range(QG)]
            ayT = [sb3(f"ayT{j}", [128, QG * 128], BF16) for j in range(KD)]
            recip_sb, recip_b = sb3("recip", [128, 1], F32)
            msk = [[sb3(f"msk{m}_{j2}", [128, 128], F32) for j2 in range(2)]
                   for m in range(QG)]
            for m in range(QG):
                for j2 in range(2):
                    p.dma("sync", msk[m][j2][0][:, :],
                          amask[(m * 2 + j2) * 128:(m * 2 + j2 + 1) * 128, :],
                          [msk[m][j2][1]], [db(amask)])
            for i in range(2):
                p.op("gpsimd", "memset", [vbT[i][1]], [], vbT[i][0][:, :], 1.0)

            def load_gh(g, h):
                NKT_ = min(8 * g + 2 * QG, NB)
                bi = h % 2
                kh_sb, kh_b = khT[bi]
                vb_sb, vb_b = vbT[bi]
                qh_sb, qh_b = qhT[bi]
                p.dma("sync", kh_sb[:, 0:NKT_ * 128],
                      k_lin[h * HD:(h + 1) * HD, 0:NKT_ * 128],
                      [kh_b], [db(k_lin)])
                for kt in range(NKT_):
                    p.dma("sync", vb_sb[:, kt * 65:kt * 65 + 64],
                          vt_lin[kt * 128:(kt + 1) * 128,
                                 h * HD:h * HD + 64],
                          [vb_b], [db(vt_lin)])
                for m in range(QG):
                    gb = (8 * g + 2 * m) * 128
                    p.dma("sync", qsA_sb[:, :],
                          q_all[h * HD:(h + 1) * HD, gb:gb + 128],
                          [qsA_b], [db(q_all)])
                    p.dma("sync", qsB_sb[:, :],
                          q_all[h * HD:(h + 1) * HD, gb + 128:gb + 256],
                          [qsB_b], [db(q_all)])
                    par_select(qh_sb[:, m * 128:(m + 1) * 128], qh_b,
                               qsA_sb[:, :], qsA_b, qsB_sb[:, :], qsB_b,
                               qtmp_sb[:, :], qtmp_b, rows=HD)

            load_gh(0, 0)
            for g in range(NQG):
                NKT = min(8 * g + 2 * QG, NB)
                for h in range(NH):
                    kh_sb, kh_b = khT[h % 2]
                    vb_sb, vb_b = vbT[h % 2]
                    qh_sb, qh_b = qhT[h % 2]
                    if h + 1 < NH:
                        load_gh(g, h + 1)
                    elif g + 1 < NQG:
                        load_gh(g + 1, 0)
                    QW = QG * 128
                    for kt in range(NKT):
                        sps, spb = (att_s0, att_s0b) if kt % 2 == 0 else \
                                   (att_s1, att_s1b)
                        p.op("tensor", "matmul", [spb], [kh_b, qh_b],
                             sps[:, 0:QW],
                             kh_sb[0:HD, kt * 128:(kt + 1) * 128],
                             qh_sb[0:HD, 0:QW], start=True, stop=True)
                        r = kt - 8 * g
                        if 0 <= r < 2 * QG:
                            m, j2 = r // 2, r % 2
                            p.op("vector", "tensor_tensor", [spb],
                                 [spb, msk[m][j2][1]],
                                 sps[:, m * 128:(m + 1) * 128],
                                 sps[:, m * 128:(m + 1) * 128],
                                 msk[m][j2][0][:, :], op=ALU.add)
                        pb = pT[kt % 2]
                        p.op("scalar", "activation", [pb[1]], [spb],
                             pb[0][:, 0:QW], sps[:, 0:QW], AF.Exp,
                             scale=np.float32(1.0 / np.sqrt(HD)))
                        for m in range(QG):
                            lastm = min(8 * g + 2 * m + 2, NB)
                            if kt < lastm:
                                p.op("tensor", "matmul", [att_om[m][1]],
                                     [pb[1], vb_b],
                                     att_om[m][0][:, 0:65],
                                     pb[0][:, m * 128:(m + 1) * 128],
                                     vb_sb[:, kt * 65:kt * 65 + 65],
                                     start=(kt == 0), stop=(kt == lastm - 1))
                    for m in range(QG):
                        p.op("vector", "reciprocal", [recip_b], [att_om[m][1]],
                             recip_sb[:, :], att_om[m][0][:, 64:65])
                        p.op("vector", "tensor_scalar", [ayg[m][1]],
                             [att_om[m][1], recip_b],
                             ayg[m][0][:, h * HD:h * HD + 64],
                             att_om[m][0][:, 0:64],
                             recip_sb[:, :], None, op0=ALU.mult)

                for m in range(QG):
                    for j in range(KD):
                        tp_ps3, tp_pb3 = next_tp3()
                        p.op("tensor", "transpose", [tp_pb3],
                             [ayg[m][1], ident_b],
                             tp_ps3[:, :],
                             ayg[m][0][:, j * 128:(j + 1) * 128],
                             ident_sb[:, :])
                        p.op("scalar", "copy", [ayT[j][1]], [tp_pb3],
                             ayT[j][0][:, m * 128:(m + 1) * 128],
                             tp_ps3[:, :])
                    load_x_block(QG * g + m)
                    ln_norm(xt_sb[:, :], xt_b)
                    ti = g * QG + m
                    for eh in range(D // 512):
                        mps, mpb = (att_s0, att_s0b) if eh % 2 == 0 else \
                                   (att_s1, att_s1b)
                        for kt in range(KD):
                            p.op("tensor", "matmul", [mpb],
                                 [ayT[kt][1], ow_b],
                                 mps[:, 0:512],
                                 ayT[kt][0][:, m * 128:(m + 1) * 128],
                                 ow_sl(kt, eh * 512, (eh + 1) * 512),
                                 start=(kt == 0), stop=(kt == KD - 1))
                        p.op("vector", "tensor_tensor", [x1o[ti][1]],
                             [mpb, xh_b],
                             x1o[ti][0][:, eh * 512:(eh + 1) * 512],
                             mps[:, 0:512],
                             xh_sb[:, eh * 512:(eh + 1) * 512], op=ALU.add)
                    if debug:
                        p.dma("sync", dbg["d_x1"][ti * 128:(ti + 1) * 128, :],
                              x1o[ti][0][:, :], [db(dbg["d_x1"])],
                              [x1o[ti][1]])
            p.barrier()

        with contextlib.ExitStack() as s5:
            def sb5(name, shape, dt):
                return mk_sb(s5, name, shape, dt)

            mm_ps, mm_pb = mk_ps(s5, "m5a", [128, 512])
            mm2_ps, mm2_pb = mk_ps(s5, "m5b", [128, 512])
            tp_ps5, tp_pb5 = mk_ps(s5, "tp5", [128, 128], BF16)
            rt_ps, rt_pb = mk_ps(s5, "rt5", [128, NE])

            gln2, gln2b = sb5("g_ln2", [128, D], F32)
            p.dma("sync", gln2[:, :], ln2_w.partition_broadcast(128),
                  [gln2b], [db(ln2_w)])
            rb_rep, rb_repb = sb5("rbrep", [128, NE], F32)
            p.dma("sync", rb_rep[:, :], rbias.partition_broadcast(128),
                  [rb_repb], [db(rbias)])
            rw_sb, rw_b, rw_sl = wload(s5, "rwm", router_km, NE)
            tdB5_sb, tdB5_b = sb5("tdB5", [128, D], BF16)
            xt_dt = [sb5(f"xtdt{j}", [128, TO], BF16) for j in range(KD)]
            probs_sb, probs_b = sb5("probs", [128, NE], F32)
            mskA_sb, mskA_b = sb5("mskA", [128, NE], F32)
            m1_sb, m1_b = sb5("m1c", [128, 1], F32)
            m2_sb, m2_b = sb5("m2c", [128, 1], F32)
            wgt_sb, wgt_b = sb5("wgt", [128, NE * NTT], F32)
            gchT = [sb5(f"gch{i}", [128, KD * ICH], BF16) for i in range(2)]
            uchT = [sb5(f"uch{i}", [128, KD * ICH], BF16) for i in range(2)]
            dchT = [sb5(f"dch{i}", [128, (ICH // 128) * D], BF16)
                    for i in range(2)]

            def load_moe_chunk(e, ic):
                bi = (e * NICH + ic) % 2
                c0 = e * I + ic * ICH
                gch_sb, gch_b = gchT[bi]
                uch_sb, uch_b = uchT[bi]
                dch_sb, dch_b = dchT[bi]
                p.dma("gpsimd",
                      gch_sb[:, :].rearrange("p (k c) -> p k c", c=ICH),
                      gate_km[:, c0:c0 + ICH].rearrange(
                          "(k p) c -> p k c", p=128),
                      [gch_b], [db(gate_km)])
                p.dma("gpsimd",
                      uch_sb[:, :].rearrange("p (k c) -> p k c", c=ICH),
                      up_km[:, c0:c0 + ICH].rearrange(
                          "(k p) c -> p k c", p=128),
                      [uch_b], [db(up_km)])
                p.dma("gpsimd",
                      dch_sb[:, :].rearrange("p (k c) -> p k c", c=D),
                      down_km[c0:c0 + ICH, :].rearrange(
                          "(k p) c -> p k c", p=128),
                      [dch_b], [db(down_km)])
            gsil_sb, gsil_b = sb5("gsil", [128, TF], BF16)
            gub = [sb5(f"gub{i}", [128, TO], BF16) for i in range(ICH // 128)]

            for tt in range(NTT):
                ln_norm(x1o[tt][0][:, :], x1o[tt][1])
                p.op("vector", "tensor_tensor", [tdB5_b], [xh_b, gln2b],
                     tdB5_sb[:, :], xh_sb[:, :], gln2[:, :], op=ALU.mult)
                for j in range(KD):
                    p.op("tensor", "transpose", [tp_pb5], [tdB5_b, ident_b],
                         tp_ps5[:, :], tdB5_sb[:, j * 128:(j + 1) * 128],
                         ident_sb[:, :])
                    p.op("scalar", "copy", [xt_dt[j][1]], [tp_pb5],
                         xt_dt[j][0][:, tt * 128:(tt + 1) * 128],
                         tp_ps5[:, :])

            for tt in range(NTT):
                for kt in range(KD):
                    p.op("tensor", "matmul", [rt_pb], [xt_dt[kt][1], rw_b],
                         rt_ps[:, 0:NE],
                         xt_dt[kt][0][:, tt * 128:(tt + 1) * 128],
                         rw_sl(kt, 0, NE),
                         start=(kt == 0), stop=(kt == KD - 1))
                p.op("vector", "tensor_tensor", [probs_b], [rt_pb, rb_repb],
                     probs_sb[:, :], rt_ps[:, 0:NE], rb_rep[:, :], op=ALU.add)
                p.op("scalar", "activation", [probs_b], [probs_b],
                     probs_sb[:, :], probs_sb[:, :], AF.Sigmoid)
                p.op("vector", "reduce_max", [m1_b], [probs_b],
                     m1_sb[:, :], probs_sb[:, :], axis=AX.X)
                p.op("vector", "tensor_scalar", [mskA_b], [probs_b, m1_b],
                     mskA_sb[:, :], probs_sb[:, :], m1_sb[:, :], None,
                     op0=ALU.is_equal)
                p.op("vector", "scalar_tensor_tensor", [mskA_b],
                     [mskA_b, probs_b],
                     mskA_sb[:, :], mskA_sb[:, :], NEG, probs_sb[:, :],
                     op0=ALU.mult, op1=ALU.add)
                p.op("vector", "reduce_max", [m2_b], [mskA_b],
                     m2_sb[:, :], mskA_sb[:, :], axis=AX.X)
                p.op("vector", "tensor_scalar", [mskA_b], [probs_b, m2_b],
                     mskA_sb[:, :], probs_sb[:, :], m2_sb[:, :], None,
                     op0=ALU.is_ge)
                p.op("vector", "tensor_tensor", [m1_b], [m1_b, m2_b],
                     m1_sb[:, :], m1_sb[:, :], m2_sb[:, :], op=ALU.add)
                p.op("vector", "reciprocal", [m1_b], [m1_b],
                     m1_sb[:, :], m1_sb[:, :])
                p.op("vector", "scalar_tensor_tensor", [wgt_b],
                     [probs_b, m1_b, mskA_b],
                     wgt_sb[:, tt * NE:(tt + 1) * NE], probs_sb[:, :],
                     m1_sb[:, :], mskA_sb[:, :], op0=ALU.mult, op1=ALU.mult)
                if debug:
                    p.dma("sync", dbg["d_wgt"][tt * 128:(tt + 1) * 128, :],
                          wgt_sb[:, tt * NE:(tt + 1) * NE],
                          [db(dbg["d_wgt"])], [wgt_b])

            load_moe_chunk(0, 0)
            for e in range(NE + 1):
                for ic in range(NICH):
                    bi = (e * NICH + ic) % 2
                    gch_sb, gch_b = gchT[bi]
                    uch_sb, uch_b = uchT[bi]
                    dch_sb, dch_b = dchT[bi]

                    def gch_sl(kt, c0, c1, _g=gch_sb):
                        return _g[:, kt * ICH + c0:kt * ICH + c1]

                    def uch_sl(kt, c0, c1, _u=uch_sb):
                        return _u[:, kt * ICH + c0:kt * ICH + c1]

                    def dch_sl(kt, c0, c1, _d=dch_sb):
                        return _d[:, kt * D + c0:kt * D + c1]
                    nxt = e * NICH + ic + 1
                    if nxt < (NE + 1) * NICH:
                        load_moe_chunk(nxt // NICH, nxt % NICH)
                    for it in range(ICH // 128):
                        for tf in range(NTF):
                            for kt in range(KD):
                                p.op("tensor", "matmul", [mm_pb],
                                     [gch_b, xt_dt[kt][1]],
                                     mm_ps[:, 0:TF],
                                     gch_sl(kt, it * 128, (it + 1) * 128),
                                     xt_dt[kt][0][:, tf * TF:(tf + 1) * TF],
                                     start=(kt == 0), stop=(kt == KD - 1))
                            p.op("scalar", "activation", [gsil_b], [mm_pb],
                                 gsil_sb[:, :], mm_ps[:, 0:TF], AF.Silu)
                            for kt in range(KD):
                                p.op("tensor", "matmul", [mm2_pb],
                                     [uch_b, xt_dt[kt][1]],
                                     mm2_ps[:, 0:TF],
                                     uch_sl(kt, it * 128, (it + 1) * 128),
                                     xt_dt[kt][0][:, tf * TF:(tf + 1) * TF],
                                     start=(kt == 0), stop=(kt == KD - 1))
                            p.op("vector", "tensor_tensor", [gub[it][1]],
                                 [gsil_b, mm2_pb],
                                 gub[it][0][:, tf * TF:(tf + 1) * TF],
                                 gsil_sb[:, :], mm2_ps[:, 0:TF], op=ALU.mult)
                    for tt in range(NTT):
                        for eh in range(D // 512):
                            dps, dpb = ((mm_ps, mm_pb) if (tt * 2 + eh) % 2 == 0
                                        else (mm2_ps, mm2_pb))
                            for it in range(ICH // 128):
                                p.op("tensor", "matmul", [dpb],
                                     [gub[it][1], dch_b],
                                     dps[:, 0:512],
                                     gub[it][0][:, tt * 128:(tt + 1) * 128],
                                     dch_sl(it, eh * 512, (eh + 1) * 512),
                                     start=(it == 0),
                                     stop=(it == ICH // 128 - 1))
                            scal = (wgt_sb[:, tt * NE + e:tt * NE + e + 1]
                                    if e < NE else 1.0)
                            sbuf_ins = [dpb, x1o[tt][1]]
                            if e < NE:
                                sbuf_ins.append(wgt_b)
                            p.op("vector", "scalar_tensor_tensor",
                                 [x1o[tt][1]], sbuf_ins,
                                 x1o[tt][0][:, eh * 512:(eh + 1) * 512],
                                 dps[:, 0:512], scal,
                                 x1o[tt][0][:, eh * 512:(eh + 1) * 512],
                                 op0=ALU.mult, op1=ALU.add)
            if debug:
                for tt in range(NTT):
                    p.dma("sync", dbg["d_x2"][tt * 128:(tt + 1) * 128, :],
                          x1o[tt][0][:, :], [db(dbg["d_x2"])], [x1o[tt][1]])
            p.barrier()

        with contextlib.ExitStack() as s6:
            def sb6(name, shape, dt):
                return mk_sb(s6, name, shape, dt)

            mmf0, mmf0b = mk_ps(s6, "mmf0", [128, 512])
            mmf1, mmf1b = mk_ps(s6, "mmf1", [128, 512])
            wout_sb, wout_b, wout_sl = wload(s6, "woutm", wout_km, D)
            ygk = [sb6(f"ygk{k}", [128, 128], BF16) for k in range(KD)]
            ygsA_sb, ygsA_b = sb6("ygsA", [128, 128], BF16)
            ygsB_sb, ygsB_b = sb6("ygsB", [128, 128], BF16)
            qtmp6_sb, qtmp6_b = sb6("qtmp6", [128, 128], F32)
            sgo = [ (sb6(f"s_f{i}", [128, D], F32),
                     sb6(f"g_f{i}", [128, D], F32),
                     sb6(f"o_f{i}", [128, D], F32)) for i in range(2)]

            p._wait("sync", [tok_b.w])
            with nc.sync.register("rg_f") as rg_f, \
                 nc.sync.register("rg_t") as rg_t, \
                 nc.sync.register("rg_c") as rg_c, \
                 nc.sync.register("rg_u") as rg_u, \
                 nc.sync.register("rg_w") as rg_w:
                nc.sync.load(rg_t, tok_sb[0:1, 0:1])
                nc.sync.reg_mov(rg_f, 1)
                nc.sync.reg_mov(rg_w, 0)
                nc.sync.reg_mov(rg_c, int(S.get("poll_cap", 3000000)))
                with nc.sync.While(rg_f):
                    nc.sync.reg_add(rg_w, rg_w, 32)
                    nc.sync.dma_start(out=fl_sb[0:1, 0:1],
                                      in_=flA[:, :]).then_inc(pl_sem, 16)
                    nc.sync.dma_start(out=fl_sb[0:1, 1:2],
                                      in_=flB[:, :]).then_inc(pl_sem, 16)
                    nc.sync.wait_ge(pl_sem, rg_w)
                    nc.sync.load(rg_f, fl_sb[0:1, 0:1])
                    nc.sync.load(rg_u, fl_sb[0:1, 1:2])
                    nc.sync.reg_sub(rg_f, rg_f, rg_t)
                    nc.sync.reg_sub(rg_u, rg_u, rg_t)
                    nc.sync.reg_alu(rg_f, rg_f, rg_u, ALU.bitwise_or)
                    nc.sync.reg_sub(rg_c, rg_c, 1)
                    with nc.sync.If_eq(rg_c, 0):
                        nc.sync.reg_mov(rg_f, 0)
                nc.sync.sem_inc(p.sem["sync"], 1)
            p.cnt["sync"] += 1
            hs_tok = ("sync", p.cnt["sync"])
            fl_b.w = hs_tok
            ygA_b, ygB_b = db(ygA), db(ygB)

            for tt in range(NTT):
                (s_sb, s_b), (g_sb, g_b), (o_sb, o_b) = sgo[tt % 2]
                gb = 2 * tt * 128
                for kt in range(KD):
                    src = ygA if kt < KDH else ygB
                    srcb = ygA_b if kt < KDH else ygB_b
                    r0 = (kt % KDH) * 128
                    p.dma("sync", ygsA_sb[:, :],
                          src[r0:r0 + 128, gb:gb + 128],
                          [ygsA_b], [srcb], waits=[hs_tok])
                    p.dma("sync", ygsB_sb[:, :],
                          src[r0:r0 + 128, gb + 128:gb + 256],
                          [ygsB_b], [srcb], waits=[hs_tok])
                    par_select(ygk[kt][0][:, :], ygk[kt][1],
                               ygsA_sb[:, :], ygsA_b, ygsB_sb[:, :], ygsB_b,
                               qtmp6_sb[:, :], qtmp6_b)
                load_x_block(tt)
                ln_norm(xt_sb[:, :], xt_b)
                for eh in range(D // 512):
                    mps, mpb = (mmf0, mmf0b) if eh % 2 == 0 else (mmf1, mmf1b)
                    for kt in range(KD):
                        p.op("tensor", "matmul", [mpb], [ygk[kt][1], wout_b],
                             mps[:, 0:512],
                             ygk[kt][0][:, :],
                             wout_sl(kt, eh * 512, (eh + 1) * 512),
                             start=(kt == 0), stop=(kt == KD - 1))
                    p.op("vector", "tensor_tensor", [s_b], [mpb, xh_b],
                         s_sb[:, eh * 512:(eh + 1) * 512], mps[:, 0:512],
                         xh_sb[:, eh * 512:(eh + 1) * 512], op=ALU.add)
                p.op("scalar", "activation", [g_b], [x1o[tt][1]],
                     g_sb[:, :], x1o[tt][0][:, :], AF.Silu)
                if debug:
                    p.dma("scalar", dbg["d_s"][tt * 128:(tt + 1) * 128, :],
                          s_sb[:, :], [db(dbg["d_s"])], [s_b])
                    p.dma("scalar", dbg["d_g"][tt * 128:(tt + 1) * 128, :],
                          g_sb[:, :], [db(dbg["d_g"])], [g_b])
                    for kt in range(KD):
                        p.dma("gpsimd",
                              dbg["d_yg"][kt * 128:(kt + 1) * 128,
                                          tt * 128:(tt + 1) * 128],
                              ygk[kt][0][:, :], [db(dbg["d_yg"])],
                              [ygk[kt][1]])
                p.op("vector", "tensor_tensor", [o_b], [s_b, g_b],
                     o_sb[:, :], s_sb[:, :], g_sb[:, :], op=ALU.mult)
                p.op("vector", "tensor_tensor", [o_b], [o_b, xt_b],
                     o_sb[:, :], o_sb[:, :], xt_sb[:, :], op=ALU.add)
                p.dma("sync", out_own[tt * 128:(tt + 1) * 128, :],
                      o_sb[:, :], [db(out_own)], [o_b])

            p.barrier()

    return nc



def _stage_core(inp, S, core, tok=None):
    import ml_dtypes
    T, D, N = S["T"], S["D"], S["N"]
    NH, HD, RD, L = S["NH"], S["HD"], S["RD"], S["L"]
    NE, I = S["NE"], S["I"]
    DH = D // 2
    NOB = (T // 128) // 2
    QG = min(4, NOB)
    b, p_ = core // 2, core % 2
    f32 = np.float32
    c = np.ascontiguousarray

    x = np.asarray(inp["L0_attn_x"], f32)[b]
    own = np.arange(p_ * DH, (p_ + 1) * DH)
    other = np.arange((1 - p_) * DH, (2 - p_) * DH)
    perm = np.concatenate([own, other])

    W_in = np.asarray(inp["L0_attn_W_in"], f32)
    win_rows = np.concatenate([W_in[perm], W_in[D + own]], axis=0)
    conv_w = np.asarray(inp["L0_attn_conv_w"], f32).reshape(D, 4)[perm]
    conv_b = np.asarray(inp["L0_attn_conv_b"], f32)[perm]
    W_dt = np.asarray(inp["L0_attn_W_dt"], f32)
    b_dt = np.asarray(inp["L0_attn_b_dt"], f32)
    Dp = np.asarray(inp["L0_attn_Dp"], f32)
    W_B = np.asarray(inp["L0_attn_W_B"], f32)
    W_C = np.asarray(inp["L0_attn_W_C"], f32)
    W_out = np.asarray(inp["L0_attn_W_out"], f32)

    KD = D // 128
    KDH = DH // 128
    nope_rows = np.concatenate(
        [np.arange(h * HD + RD, (h + 1) * HD) for h in range(NH)])
    RH = RD // 2
    g1 = np.concatenate([np.arange(h * RD, h * RD + RH) for h in range(NH)])
    g2 = np.concatenate([np.arange(h * RD + RH, (h + 1) * RD)
                         for h in range(NH)])
    rot_perm = np.concatenate([g1, g2])

    k_u = np.asarray(inp["L0_ffn_k_u"], f32)
    q_u = np.asarray(inp["L0_ffn_q_u"], f32)
    v_u = np.asarray(inp["L0_ffn_v_u"], f32)
    rope_q = np.asarray(inp["L0_ffn_rope_q_w"], f32)
    rope_k = np.asarray(inp["L0_ffn_rope_k_w"], f32)

    inv = (1.0 / BASE ** (np.arange(0, RD, 2) / RD)).astype(np.float64)
    tpos = np.arange(T, dtype=np.float64)
    fr = tpos[None, :] * inv[:, None]
    cos16 = np.cos(fr).astype(f32)
    sin16 = np.sin(fr).astype(f32)
    rows = np.arange(128) % RH
    cos_rep = cos16[rows]
    sin_rep = sin16[rows]

    tri = np.where(np.arange(128)[:, None] <= np.arange(128)[None, :],
                   0.0, NEG).astype(f32)
    zz = np.zeros((128, 128), f32)
    ng = np.full((128, 128), NEG, f32)
    am = np.zeros((QG, 2, 128, 128), f32)
    for m in range(QG):
        if p_ == 0:
            am[m, 0] = tri
            am[m, 1] = ng
        else:
            am[m, 0] = zz
            am[m, 1] = tri
    amask = am.reshape(QG * 2 * 128, 128)

    sg = np.asarray(inp["L0_ffn_shared_gate"], f32)
    su = np.asarray(inp["L0_ffn_shared_up"], f32)
    sd = np.asarray(inp["L0_ffn_shared_down"], f32)
    rg = np.asarray(inp["L0_ffn_routed_gate"], f32)
    ru = np.asarray(inp["L0_ffn_routed_up"], f32)
    rd_ = np.asarray(inp["L0_ffn_routed_down"], f32)
    gate_all = np.concatenate([rg, sg], axis=0)
    up_all = np.concatenate([ru, su], axis=0)
    down_all = np.concatenate([rd_, sd], axis=0)

    if tok is None:
        tok = np.array([[np.random.default_rng().integers(1, 2 ** 30)]],
                       dtype=np.int32)

    def kmaj(w):
        return c(w.T)

    d = {
        "x_full": c(x),
        "par": np.array([[p_]], dtype=np.int32),
        "pselin": np.full((128, 1), float(p_), f32),
        "pinvin": np.full((128, 1), 1.0 - p_, f32),
        "tok": tok,
        "win_km": kmaj(win_rows),
        "convw_t": c(conv_w.reshape(KD, 128, 4).transpose(1, 0, 2)
                     .reshape(128, KD * 4)),
        "convb_t": c(conv_b.reshape(KD, 128).T),
        "wdt_km": kmaj(W_dt[own][:, perm]),
        "bdt_t": c(b_dt[own].reshape(KDH, 128).T),
        "a_log": np.asarray(inp["L0_attn_A_log"], f32).reshape(1, N),
        "dp_t": c(Dp[own].reshape(KDH, 128).T),
        "wb_km": kmaj(W_B[:, perm]),
        "wc_km": kmaj(W_C[:, perm]),
        "wout_km": kmaj(W_out),
        "ln_g": np.asarray(inp["L0_attn_ln_g"], f32).reshape(1, D),
        "ln_b": np.asarray(inp["L0_attn_ln_b"], f32).reshape(1, D),
        "ln1_w": np.asarray(inp["L0_ffn_ln1_w"], f32).reshape(1, D),
        "ln2_w": np.asarray(inp["L0_ffn_ln2_w"], f32).reshape(1, D),
        "kvd_km": kmaj(np.asarray(inp["L0_ffn_kv_d"], f32)),
        "qd_km": kmaj(np.asarray(inp["L0_ffn_q_d"], f32)),
        "kun_km": kmaj(k_u[nope_rows]),
        "qun_km": kmaj(q_u[nope_rows]),
        "krot_km": kmaj(rope_k[rot_perm]),
        "qrot_km": kmaj(rope_q[rot_perm]),
        "vu_km": kmaj(v_u),
        "ow_km": kmaj(np.asarray(inp["L0_ffn_o_w"], f32)),
        "cos_rep": c(cos_rep),
        "sin_rep": c(sin_rep),
        "amask": amask,
        "router_km": kmaj(np.asarray(inp["L0_ffn_router_w"], f32)),
        "rbias": np.asarray(inp["L0_ffn_routing_bias"], f32).reshape(1, NE),
        "gate_km": c(gate_all.reshape((NE + 1) * I, D).T),
        "up_km": c(up_all.reshape((NE + 1) * I, D).T),
        "down_km": c(down_all.transpose(0, 2, 1).reshape((NE + 1) * I, D)),
        "ident_in": np.eye(128, dtype=ml_dtypes.bfloat16),
    }
    return d


_NC_CACHE = {}


def kernel(**inputs):
    x = np.asarray(inputs["L0_attn_x"])
    B, T, D = x.shape
    S = dict(SIZES)
    S["T"] = T
    S["D"] = D
    key = (T, D)
    if key not in _NC_CACHE:
        _NC_CACHE[key] = build_program(S, debug=False)
    nc = _NC_CACHE[key]
    tok = np.array([[np.random.default_rng().integers(1, 2 ** 30)]],
                   dtype=np.int32)
    in_maps = [_stage_core(inputs, S, c, tok) for c in range(8)]
    res = run_bass_kernel_spmd(nc, in_maps, core_ids=list(range(8)))
    out = np.empty((B, T, D), np.float32)
    NOB = (T // 128) // 2
    for c in range(8):
        b, p_ = c // 2, c % 2
        oo = res.results[c]["out_own"]
        for j in range(NOB):
            g = 2 * j + p_
            out[b, g * 128:(g + 1) * 128, :] = oo[j * 128:(j + 1) * 128, :]
    return out


# version 06 (final)
# speedup vs baseline: 1.2678x; 1.0106x over previous
import contextlib
import numpy as np

import concourse.bass as bass
import concourse.mybir as mybir
from concourse.bass_utils import run_bass_kernel_spmd

F32 = mybir.dt.float32
BF16 = mybir.dt.bfloat16
I32 = mybir.dt.int32
AF = mybir.ActivationFunctionType
ALU = mybir.AluOpType
AX = mybir.AxisListType

NEG = -1.0e30
EPS = 1e-5
BASE = 10000.0
SIZES = dict(T=4096, D=1024, N=64, NH=16, HD=64, RD=32, L=256, NE=7, I=2048)


class Buf:
    __slots__ = ("w", "r")

    def __init__(self):
        self.w = None
        self.r = {}


class Prog:
    ENGS = ("tensor", "vector", "scalar", "gpsimd", "sync")
    QS = ("sync", "scalar", "gpsimd")
    NDS = 6

    def __init__(self, nc, es):
        self.nc = nc
        self.sem = {}
        self.cnt = {}
        self.waited = {e: {} for e in self.ENGS}
        for name in self.ENGS:
            self.sem[name] = es.enter_context(nc.semaphore(f"sm_{name}"))
            self.cnt[name] = 0
        self.drr = {}
        for q in self.QS:
            for i in range(self.NDS):
                key = f"d_{q}{i}"
                self.sem[key] = es.enter_context(nc.semaphore(key))
                self.cnt[key] = 0
            self.drr[q] = 0
        self.eng = {"tensor": nc.tensor, "vector": nc.vector,
                    "scalar": nc.scalar, "gpsimd": nc.gpsimd, "sync": nc.sync}

    def _wait(self, ename, toks):
        w = self.waited[ename]
        need = {}
        for t in toks:
            if t is None:
                continue
            s, v = t
            if v > need.get(s, -1):
                need[s] = v
        for s, v in need.items():
            if w.get(s, 0) < v:
                self.eng[ename].wait_ge(self.sem[s], v)
                w[s] = v

    @staticmethod
    def _toks(ins, outs):
        toks = []
        for b in ins:
            if b is not None and b.w is not None:
                toks.append(b.w)
        for b in outs:
            if b is None:
                continue
            if b.w is not None:
                toks.append(b.w)
            toks.extend(b.r.items())
        return toks

    def _finish(self, ins, outs, tok):
        s, v = tok
        for b in ins:
            if b is not None and b.r.get(s, 0) < v:
                b.r[s] = v
        for b in outs:
            if b is not None:
                b.w = tok
                b.r = {}

    def op(self, ename, method, outs, ins, *args, waits=(), **kw):
        self._wait(ename, self._toks(ins, outs) + list(waits))
        inst = getattr(self.eng[ename], method)(*args, **kw)
        self.cnt[ename] += 1
        tok = (ename, self.cnt[ename])
        inst.then_inc(self.sem[ename], 1)
        self._finish(ins, outs, tok)
        return tok

    def _dma_sem(self, q):
        i = self.drr[q]
        self.drr[q] = (i + 1) % self.NDS
        key = f"d_{q}{i}"
        if self.cnt[key]:
            self._wait(q, [(key, self.cnt[key])])
        return key

    def dma(self, q, out_ap, in_ap, outs, ins, waits=(), **kw):
        self._wait(q, self._toks(ins, outs) + list(waits))
        key = self._dma_sem(q)
        inst = self.eng[q].dma_start(out=out_ap, in_=in_ap, **kw)
        inst.then_inc(self.sem[key], 16)
        self.cnt[key] += 16
        tok = (key, self.cnt[key])
        self._finish(ins, outs, tok)
        return tok

    def barrier(self):
        toks = [(e, self.cnt[e]) for e in self.ENGS]
        toks += [(f"d_{q}{i}", self.cnt[f"d_{q}{i}"])
                 for q in self.QS for i in range(self.NDS)]
        for e in self.ENGS:
            self._wait(e, toks)


def build_program(S, debug=False):
    T, D, N = S["T"], S["D"], S["N"]
    NH, HD, RD, L = S["NH"], S["HD"], S["RD"], S["L"]
    NE, I = S["NE"], S["I"]
    DH = D // 2
    NB = T // 128
    NOB = NB // 2
    TO = NOB * 128
    CH0 = min(512, T)
    NCH0 = T // CH0
    TPC = CH0 // 128
    CH1 = min(1024, T)
    NCH1 = T // CH1
    QG = min(4, NOB)
    NQG = NOB // QG
    KD = D // 128
    KL = L // 128
    NOPE = HD - RD
    RH = RD // 2
    ICH = 512
    NICH = I // ICH
    TF = min(512, TO)
    NTF = TO // TF
    NTT = TO // 128
    KDH = DH // 128

    nc = bass.Bass("TRN2", target_bir_lowering=False)

    def din(name, shape, dt=F32):
        return nc.dram_tensor(name, shape, dt, kind="ExternalInput").ap()

    kscr = "ExternalOutput" if debug else "Internal"

    def dsc(name, shape, dt):
        return nc.dram_tensor(name, shape, dt, kind=kscr).ap()

    x_full = din("x_full", [T, D])
    par = din("par", [1, 1], I32)
    pselin = din("pselin", [128, 1])
    pinvin = din("pinvin", [128, 1])
    tokin = din("tok", [1, 1], I32)
    win_km = din("win_km", [D, D + DH])
    convw_t = din("convw_t", [128, KD * 4])
    convb_t = din("convb_t", [128, KD])
    wdt_km = din("wdt_km", [D, DH])
    bdt_t = din("bdt_t", [128, KDH])
    a_log = din("a_log", [1, N])
    dp_t = din("dp_t", [128, KDH])
    wb_km = din("wb_km", [D, N])
    wc_km = din("wc_km", [D, N])
    wout_km = din("wout_km", [D, D])
    ln_g = din("ln_g", [1, D])
    ln_b = din("ln_b", [1, D])
    ln1_w = din("ln1_w", [1, D])
    ln2_w = din("ln2_w", [1, D])
    kvd_km = din("kvd_km", [D, L])
    qd_km = din("qd_km", [D, L])
    kun_km = din("kun_km", [L, NH * NOPE])
    qun_km = din("qun_km", [L, NH * NOPE])
    krot_km = din("krot_km", [D, NH * RD])
    qrot_km = din("qrot_km", [L, NH * RD])
    vu_km = din("vu_km", [L, NH * HD])
    ow_km = din("ow_km", [D, D])
    cos_rep = din("cos_rep", [128, T])
    sin_rep = din("sin_rep", [128, T])
    amask = din("amask", [QG * 2 * 128, 128])
    router_km = din("router_km", [D, NE])
    rbias = din("rbias", [1, NE])
    gate_km = din("gate_km", [D, (NE + 1) * I])
    up_km = din("up_km", [D, (NE + 1) * I])
    down_km = din("down_km", [(NE + 1) * I, D])
    ident_in = din("ident_in", [128, 128], BF16)

    out_own = nc.dram_tensor("out_own", [TO, D], F32, kind="ExternalOutput").ap()

    z_lin = dsc("z_lin", [DH, T], BF16)
    dt_lin = dsc("dt_lin", [DH, T], F32)
    xc_lin = dsc("xc_lin", [DH, T], F32)
    b_lin = dsc("b_lin", [N, T], F32)
    c_lin = dsc("c_lin", [N, T], F32)
    k_lin = dsc("k_lin", [NH * HD, T], BF16)
    vt_lin = dsc("vt_lin", [T, NH * HD], BF16)
    q_all = dsc("q_all", [NH * HD, T], BF16)
    arow_d = nc.dram_tensor("arow_d", [1, N], F32).ap()
    dbg = {}
    if debug:
        for nm, shape in [("d_y", [DH, T]), ("d_x1", [TO, D]),
                          ("d_x2", [TO, D]), ("d_wgt", [TO, NE]),
                          ("d_s", [TO, D]), ("d_g", [TO, D]),
                          ("d_yg", [D, TO])]:
            dbg[nm] = nc.dram_tensor(nm, shape, F32, kind="ExternalOutput").ap()
    _shr = {} if S.get("local_shared") else {"addr_space": "Shared"}
    ygA = nc.dram_tensor("ygA", [DH, T], BF16, **_shr).ap()
    ygB = nc.dram_tensor("ygB", [DH, T], BF16, **_shr).ap()
    flA = nc.dram_tensor("flA", [1, 1], I32, **_shr).ap()
    flB = nc.dram_tensor("flB", [1, 1], I32, **_shr).ap()

    dram_bufs = {}

    def db(ap):
        key = ap.tensor.name
        if key not in dram_bufs:
            dram_bufs[key] = Buf()
        return dram_bufs[key]

    def mk_sb(stack, name, shape, dt):
        t = stack.enter_context(nc.sbuf_tensor(name, shape, dt))
        return t, Buf()

    def mk_ps(stack, name, shape, dt=F32):
        t = stack.enter_context(nc.psum_tensor(name, shape, dt))
        return t, Buf()

    es = contextlib.ExitStack()
    with es:
        p = Prog(nc, es)

        def wload(stack, name, dram_ap, C_, q="gpsimd", dt=BF16):
            K_ = dram_ap.shape[0]
            nk = K_ // 128
            t = stack.enter_context(nc.sbuf_tensor(name, [128, nk * C_], dt))
            b = Buf()
            src = dram_ap.rearrange("(k p) c -> p k c", p=128)
            dst = t[:, :].rearrange("p (k c) -> p k c", c=C_)
            p.dma(q, dst, src, [b], [db(dram_ap)])

            def sl(kt, c0, c1):
                return t[:, kt * C_ + c0:kt * C_ + c1]
            return t, b, sl
        pl_sem = es.enter_context(nc.semaphore("pl_sem"))
        blk = es.enter_context(nc.Block())

        def sb(name, shape, dt):
            return mk_sb(es, name, shape, dt)

        ident_sb, ident_b = sb("ident", [128, 128], BF16)
        p.dma("sync", ident_sb[:, :], ident_in[:, :], [ident_b], [db(ident_in)])
        par_sb, par_b = sb("par_t", [1, 1], I32)
        p.dma("sync", par_sb[:, :], par[:, :], [par_b], [db(par)])
        tok_sb, tok_b = sb("tok_t", [1, 1], I32)
        p.dma("sync", tok_sb[:, :], tokin[:, :], [tok_b], [db(tokin)])
        fl_sb, fl_b = sb("fl_t", [1, 2], I32)
        acol_sb, acol_b = sb("acol", [128, N], F32)
        arow_sb, arow_b = sb("arow", [1, N], F32)
        p.dma("sync", arow_sb[:, :], a_log[:, :], [arow_b], [db(a_log)])
        p.op("scalar", "activation", [arow_b], [arow_b],
             arow_sb[:, :], arow_sb[:, :], AF.Exp)
        p.op("vector", "tensor_scalar_mul", [arow_b], [arow_b],
             arow_sb[:, :], arow_sb[:, :], -1.0)
        p.dma("sync", arow_d[:, :], arow_sb[:, :], [db(arow_d)], [arow_b])
        p.dma("sync", acol_sb[:, :], arow_d.partition_broadcast(128),
              [acol_b], [db(arow_d)])
        dp_sb, dp_b = sb("dpc", [128, KDH], F32)
        p.dma("sync", dp_sb[:, :], dp_t[:, :], [dp_b], [db(dp_t)])
        rg_par = es.enter_context(nc.sync.register("rg_par"))
        p.op("sync", "load", [], [par_b], rg_par, par_sb[0:1, 0:1])
        psel_sb, psel_b = sb("psel", [128, 1], F32)
        pinv_sb, pinv_b = sb("pinv", [128, 1], F32)
        p.dma("sync", psel_sb[:, :], pselin[:, :], [psel_b], [db(pselin)])
        p.dma("sync", pinv_sb[:, :], pinvin[:, :], [pinv_b], [db(pinvin)])
        _selholder = {}

        def par_select(out_ap, out_b, a_ap, a_b, bb_ap, bb_b, tmp_ap, tmp_b,
                       rows=128):
            p.op("vector", "tensor_scalar", [tmp_b], [a_b, pinv_b],
                 tmp_ap, a_ap, pinv_sb[0:rows, :], None, op0=ALU.mult)
            p.op("vector", "scalar_tensor_tensor", [out_b],
                 [bb_b, psel_b, tmp_b],
                 out_ap, bb_ap, psel_sb[0:rows, :], tmp_ap,
                 op0=ALU.mult, op1=ALU.add)

        def load_x_block(j):
            selA_sb, selA_b = _selholder["A"]
            selB_sb, selB_b = _selholder["B"]
            p.dma("sync", selA_sb[:, :],
                  x_full[2 * j * 128:(2 * j + 1) * 128, :],
                  [selA_b], [db(x_full)])
            p.dma("sync", selB_sb[:, :],
                  x_full[(2 * j + 1) * 128:(2 * j + 2) * 128, :],
                  [selB_b], [db(x_full)])
            par_select(xt_sb[:, :], xt_b, selA_sb[:, :], selA_b,
                       selB_sb[:, :], selB_b, sq_sb[:, :], sq_b)
        mu_sb, mu_b = sb("mu", [128, 1], F32)
        var_sb, var_b = sb("var", [128, 1], F32)
        sq_sb, sq_b = sb("sq", [128, D], F32)
        xh_sb, xh_b = sb("xh", [128, D], F32)
        xt_sb, xt_b = sb("xtile", [128, D], F32)

        def ln_norm(src_ap, src_b):
            p.op("vector", "reduce_sum", [mu_b], [src_b],
                 mu_sb[:, :], src_ap, axis=AX.X)
            p.op("vector", "tensor_scalar_mul", [mu_b], [mu_b],
                 mu_sb[:, :], mu_sb[:, :], 1.0 / D)
            p.op("vector", "tensor_scalar", [xh_b], [src_b, mu_b],
                 xh_sb[:, :], src_ap, mu_sb[:, :], None, op0=ALU.subtract)
            p.op("scalar", "activation", [sq_b], [xh_b],
                 sq_sb[:, :], xh_sb[:, :], AF.Square)
            p.op("vector", "reduce_sum", [var_b], [sq_b],
                 var_sb[:, :], sq_sb[:, :], axis=AX.X)
            p.op("vector", "tensor_scalar_mul", [var_b], [var_b],
                 var_sb[:, :], var_sb[:, :], 1.0 / D)
            p.op("scalar", "activation", [var_b], [var_b],
                 var_sb[:, :], var_sb[:, :], AF.Ln, bias=np.float32(EPS))
            p.op("scalar", "activation", [var_b], [var_b],
                 var_sb[:, :], var_sb[:, :], AF.Exp, scale=np.float32(-0.5))
            p.op("vector", "tensor_scalar", [xh_b], [xh_b, var_b],
                 xh_sb[:, :], xh_sb[:, :], var_sb[:, :], None, op0=ALU.mult)

        with contextlib.ExitStack() as s0:
            def sb0(name, shape, dt):
                return mk_sb(s0, name, shape, dt)

            mmp = [mk_ps(s0, f"mm_ps{i}", [128, 512]) for i in range(4)]
            tpp = [mk_ps(s0, f"tp_ps{i}", [128, 128], BF16) for i in range(2)]
            _mmi = [0]
            def next_mm():
                t = mmp[_mmi[0] % 4]
                _mmi[0] += 1
                return t
            _tpi = [0]
            def next_tp():
                t = tpp[_tpi[0] % 2]
                _tpi[0] += 1
                return t

            gln_g, gln_gb = sb0("g_lng", [128, D], F32)
            gln_bt, gln_bb = sb0("g_lnb", [128, D], F32)
            gln1, gln1b = sb0("g_ln1", [128, D], F32)
            p.dma("sync", gln_g[:, :], ln_g.partition_broadcast(128),
                  [gln_gb], [db(ln_g)])
            p.dma("sync", gln_bt[:, :], ln_b.partition_broadcast(128),
                  [gln_bb], [db(ln_b)])
            p.dma("sync", gln1[:, :], ln1_w.partition_broadcast(128),
                  [gln1b], [db(ln1_w)])
            convw_sb, convw_b = sb0("convw", [128, KD * 4], F32)
            p.dma("sync", convw_sb[:, :], convw_t[:, :], [convw_b],
                  [db(convw_t)])
            convb_sb, convb_b = sb0("convb", [128, KD], F32)
            p.dma("sync", convb_sb[:, :], convb_t[:, :], [convb_b],
                  [db(convb_t)])
            bdt_sb, bdt_b = sb0("bdtc", [128, KDH], F32)
            p.dma("sync", bdt_sb[:, :], bdt_t[:, :], [bdt_b], [db(bdt_t)])

            win_sb, win_b, win_sl = wload(s0, "win", win_km, D + DH)
            wdt_sb, wdt_b, wdt_sl = wload(s0, "wdt", wdt_km, DH)
            wb_sb, wb_b, wb_sl = wload(s0, "wbm", wb_km, N)
            wc_sb, wc_b, wc_sl = wload(s0, "wcm", wc_km, N)
            kvd_sb, kvd_b, kvd_sl = wload(s0, "kvdm", kvd_km, L)
            qd_sb, qd_b, qd_sl = wload(s0, "qdm", qd_km, L)
            kun_sb, kun_b, kun_sl = wload(s0, "kunm", kun_km, NH * NOPE)
            qun_sb, qun_b, qun_sl = wload(s0, "qunm", qun_km, NH * NOPE)
            krot_sb, krot_b, krot_sl = wload(s0, "krotw", krot_km, NH * RD)
            qrot_sb, qrot_b, qrot_sl = wload(s0, "qrotw", qrot_km, NH * RD)
            vu_sb, vu_b, vu_sl = wload(s0, "vum", vu_km, NH * HD)

            tdB_sb, tdB_b = sb0("tdB", [128, D], BF16)
            x1m_dt = [sb0(f"x1mdt{j}", [128, CH0], BF16) for j in range(KD)]
            ln1_dt = [sb0(f"ln1dt{j}", [128, CH0], BF16) for j in range(KD)]
            xmain = [sb0(f"xmain{j}", [128, CH0], F32) for j in range(KD)]
            halo = [sb0(f"halo{j}", [128, 3], F32) for j in range(KD)]
            xmc3 = [sb0(f"xmc3{j}", [128, 6], F32) for j in range(KD)]
            xcf = [sb0(f"xcf{j}", [128, CH0], F32) for j in range(KD)]
            xcb = [sb0(f"xcb{j}", [128, CH0], BF16) for j in range(KD)]
            nt_sb, nt_b = sb0("ntile", [N, CH0], F32)
            kvl = [sb0(f"kvl{j}", [128, CH0], BF16) for j in range(KL)]
            ev_sb, ev_b = sb0("ev", [128, max(CH0, 512)], BF16)
            kr = [sb0(f"kr{j}", [128, CH0], F32)
                  for j in range(NH * RD // 128)]
            rp1_sb, rp1_b = sb0("rp1", [128, CH0], F32)
            rp2_sb, rp2_b = sb0("rp2", [128, CH0], F32)
            spA_sb, spA_b = rp1_sb, rp1_b
            spB_sb, spB_b = rp2_sb, rp2_b
            rpo_sb, rpo_b = sb0("rpo", [128, CH0], BF16)
            cosc_sb, cosc_b = sb0("cosc", [128, CH0], F32)
            sinc_sb, sinc_b = sb0("sinc", [128, CH0], F32)

            dtc = [sb0(f"dtc{j}", [128, CH0], F32) for j in range(KDH)]
            zsil = [sb0(f"zsil{j}", [128, CH0], BF16) for j in range(KDH)]
            brep = [sb0(f"brep{i}", [128, CH0], F32) for i in range(2)]
            crep = [sb0(f"crep{i}", [128, CH0], F32) for i in range(2)]
            sc_a = [sb0(f"sc_a{i}", [128, CH0], F32) for i in range(2)]
            sc_b = [sb0(f"sc_b{i}", [128, CH0], F32) for i in range(2)]
            h_s, h_b = sb0("h_s", [128, CH0], F32)
            hc_s, hc_b = sb0("hc_s", [128, CH0], F32)
            yac = [sb0(f"yac{j}", [128, CH0], F32) for j in range(KDH)]
            ygo_sb, ygo_b = sb0("ygo", [128, CH0], BF16)
            hst = [sb0(f"hst{j}", [128, N], F32) for j in range(KDH)]
            for j in range(KDH):
                p.op("gpsimd", "memset", [hst[j][1]], [], hst[j][0][:, :], 0.0)
            for j in range(KD):
                p.op("gpsimd", "memset", [halo[j][1]], [],
                     halo[j][0][:, :], 0.0)

            def transpose_128(dst_ap, dst_b, src_ap, src_b):
                tp_ps, tp_pb = next_tp()
                p.op("tensor", "transpose", [tp_pb], [src_b, ident_b],
                     tp_ps[:, :], src_ap, ident_sb[:, :])
                p.op("scalar", "copy", [dst_b], [tp_pb], dst_ap, tp_ps[:, :])

            def rope_rot(dst_lin, src_tiles):
                ng = NH * RH // 128
                hpt2 = 128 // RH
                for gti in range(ng):
                    g1, g1b = src_tiles[gti]
                    g2, g2b = src_tiles[ng + gti]
                    p.op("vector", "tensor_tensor", [rp1_b], [g1b, cosc_b],
                         rp1_sb[:, :], g1[:, :], cosc_sb[:, :], op=ALU.mult)
                    p.op("vector", "tensor_tensor", [rp2_b], [g2b, sinc_b],
                         rp2_sb[:, :], g2[:, :], sinc_sb[:, :], op=ALU.mult)
                    p.op("vector", "tensor_tensor", [rpo_b], [rp1_b, rp2_b],
                         rpo_sb[:, :], rp1_sb[:, :], rp2_sb[:, :],
                         op=ALU.subtract)
                    for i in range(hpt2):
                        h = gti * hpt2 + i
                        p.dma("sync",
                              dst_lin[h * HD:h * HD + RH, t0:t0 + CH0],
                              rpo_sb[i * RH:(i + 1) * RH, :],
                              [db(dst_lin)], [rpo_b])
                    p.op("vector", "tensor_tensor", [rp1_b], [g2b, cosc_b],
                         rp1_sb[:, :], g2[:, :], cosc_sb[:, :], op=ALU.mult)
                    p.op("vector", "tensor_tensor", [rp2_b], [g1b, sinc_b],
                         rp2_sb[:, :], g1[:, :], sinc_sb[:, :], op=ALU.mult)
                    p.op("vector", "tensor_tensor", [rpo_b], [rp1_b, rp2_b],
                         rpo_sb[:, :], rp1_sb[:, :], rp2_sb[:, :], op=ALU.add)
                    for i in range(hpt2):
                        h = gti * hpt2 + i
                        p.dma("sync",
                              dst_lin[h * HD + RH:h * HD + RD, t0:t0 + CH0],
                              rpo_sb[i * RH:(i + 1) * RH, :],
                              [db(dst_lin)], [rpo_b])

            for ch in range(NCH0):
                t0 = ch * CH0
                p.dma("sync", cosc_sb[:, :], cos_rep[:, t0:t0 + CH0],
                      [cosc_b], [db(cos_rep)])
                p.dma("sync", sinc_sb[:, :], sin_rep[:, t0:t0 + CH0],
                      [sinc_b], [db(sin_rep)])
                for tt in range(TPC):
                    r0 = t0 + tt * 128
                    p.dma("sync", xt_sb[:, :], x_full[r0:r0 + 128, :],
                          [xt_b], [db(x_full)])
                    ln_norm(xt_sb[:, :], xt_b)
                    ln_norm(xh_sb[:, :], xh_b)
                    p.op("vector", "tensor_tensor", [tdB_b], [xh_b, gln_gb],
                         tdB_sb[:, :], xh_sb[:, :], gln_g[:, :], op=ALU.mult)
                    p.op("vector", "tensor_tensor", [tdB_b], [tdB_b, gln_bb],
                         tdB_sb[:, :], tdB_sb[:, :], gln_bt[:, :], op=ALU.add)
                    for j in range(KD):
                        transpose_128(
                            x1m_dt[j][0][:, tt * 128:(tt + 1) * 128],
                            x1m_dt[j][1],
                            tdB_sb[:, j * 128:(j + 1) * 128], tdB_b)
                    p.op("vector", "tensor_tensor", [tdB_b], [xh_b, gln1b],
                         tdB_sb[:, :], xh_sb[:, :], gln1[:, :], op=ALU.mult)
                    for j in range(KD):
                        transpose_128(
                            ln1_dt[j][0][:, tt * 128:(tt + 1) * 128],
                            ln1_dt[j][1],
                            tdB_sb[:, j * 128:(j + 1) * 128], tdB_b)

                for o in range((D + DH) // 128):
                    mm_ps, mm_pb = next_mm()
                    for kt in range(KD):
                        p.op("tensor", "matmul", [mm_pb],
                             [win_b, x1m_dt[kt][1]],
                             mm_ps[:, 0:CH0],
                             win_sl(kt, o * 128, (o + 1) * 128),
                             x1m_dt[kt][0][:, :],
                             start=(kt == 0), stop=(kt == KD - 1))
                    if o < KD:
                        p.op("scalar", "copy", [xmain[o][1]], [mm_pb],
                             xmain[o][0][:, :], mm_ps[:, 0:CH0])
                    else:
                        oz = o - KD
                        p.op("scalar", "activation", [zsil[oz][1]], [mm_pb],
                             zsil[oz][0][:, :], mm_ps[:, 0:CH0], AF.Silu)

                for j in range(KD):
                    wj = [convw_sb[:, j * 4 + i:j * 4 + i + 1]
                          for i in range(4)]
                    xm, xmb = xmain[j]
                    xc, xcfb = xcf[j]
                    p.op("vector", "tensor_scalar", [xcfb], [xmb, convw_b],
                         xc[:, 3:CH0], xm[:, 0:CH0 - 3], wj[0], None,
                         op0=ALU.mult)
                    for i in range(1, 4):
                        p.op("vector", "scalar_tensor_tensor", [xcfb],
                             [xmb, convw_b, xcfb],
                             xc[:, 3:CH0], xm[:, i:CH0 - 3 + i], wj[i],
                             xc[:, 3:CH0], op0=ALU.mult, op1=ALU.add)
                    m3, m3b = xmc3[j]
                    p.op("vector", "tensor_copy", [m3b], [halo[j][1]],
                         m3[:, 0:3], halo[j][0][:, :])
                    p.op("vector", "tensor_copy", [m3b], [xmb],
                         m3[:, 3:6], xm[:, 0:3])
                    p.op("vector", "tensor_scalar", [xcfb], [m3b, convw_b],
                         xc[:, 0:3], m3[:, 0:3], wj[0], None, op0=ALU.mult)
                    for i in range(1, 4):
                        p.op("vector", "scalar_tensor_tensor", [xcfb],
                             [m3b, convw_b, xcfb],
                             xc[:, 0:3], m3[:, i:3 + i], wj[i],
                             xc[:, 0:3], op0=ALU.mult, op1=ALU.add)
                    p.op("vector", "tensor_copy", [halo[j][1]], [xmb],
                         halo[j][0][:, :], xm[:, CH0 - 3:CH0])
                    p.op("scalar", "activation", [xcfb], [xcfb, convb_b],
                         xc[:, :], xc[:, :], AF.Silu,
                         bias=convb_sb[:, j:j + 1])
                    p.op("vector", "tensor_copy", [xcb[j][1]], [xcfb],
                         xcb[j][0][:, :], xc[:, :])
                    if debug and j < KDH:
                        p.dma("scalar",
                              xc_lin[j * 128:(j + 1) * 128, t0:t0 + CH0],
                              xc[:, :], [db(xc_lin)], [xcfb])

                for o in range(KDH):
                    mm_ps, mm_pb = next_mm()
                    for kt in range(KD):
                        p.op("tensor", "matmul", [mm_pb], [wdt_b, xcb[kt][1]],
                             mm_ps[:, 0:CH0],
                             wdt_sl(kt, o * 128, (o + 1) * 128),
                             xcb[kt][0][:, :],
                             start=(kt == 0), stop=(kt == KD - 1))
                    bcol = bdt_sb[:, o:o + 1]
                    p.op("scalar", "activation", [spA_b], [mm_pb, bdt_b],
                         spA_sb[:, :], mm_ps[:, 0:CH0], AF.Abs, bias=bcol)
                    p.op("scalar", "activation", [spA_b], [spA_b],
                         spA_sb[:, :], spA_sb[:, :], AF.Exp, scale=np.float32(-1.0))
                    p.op("scalar", "activation", [spA_b], [spA_b],
                         spA_sb[:, :], spA_sb[:, :], AF.Ln, bias=np.float32(1.0))
                    p.op("scalar", "activation", [spB_b], [mm_pb, bdt_b],
                         spB_sb[:, :], mm_ps[:, 0:CH0], AF.Relu, bias=bcol)
                    p.op("vector", "tensor_tensor", [dtc[o][1]],
                         [spA_b, spB_b],
                         dtc[o][0][:, :], spA_sb[:, :], spB_sb[:, :],
                         op=ALU.add)
                    if debug:
                        p.dma("scalar",
                              dt_lin[o * 128:(o + 1) * 128, t0:t0 + CH0],
                              dtc[o][0][:, :], [db(dt_lin)], [dtc[o][1]])

                for wsl_, wbuf_, dst in ((wb_sl, wb_b, b_lin),
                                         (wc_sl, wc_b, c_lin)):
                    mm_ps, mm_pb = next_mm()
                    for kt in range(KD):
                        p.op("tensor", "matmul", [mm_pb], [wbuf_, xcb[kt][1]],
                             mm_ps[0:N, 0:CH0],
                             wsl_(kt, 0, N),
                             xcb[kt][0][:, :],
                             start=(kt == 0), stop=(kt == KD - 1))
                    p.op("scalar", "copy", [nt_b], [mm_pb],
                         nt_sb[:, :], mm_ps[0:N, 0:CH0])
                    p.dma("scalar", dst[:, t0:t0 + CH0], nt_sb[:, :],
                          [db(dst)], [nt_b])

                for o in range(KL):
                    mm_ps, mm_pb = next_mm()
                    for kt in range(KD):
                        p.op("tensor", "matmul", [mm_pb],
                             [kvd_b, ln1_dt[kt][1]],
                             mm_ps[:, 0:CH0],
                             kvd_sl(kt, o * 128, (o + 1) * 128),
                             ln1_dt[kt][0][:, :],
                             start=(kt == 0), stop=(kt == KD - 1))
                    p.op("scalar", "copy", [kvl[o][1]], [mm_pb],
                         kvl[o][0][:, :], mm_ps[:, 0:CH0])

                for tt in range(TPC):
                    for vh in range(NH * HD // 512):
                        mm2_ps, mm2_pb = next_mm()
                        for kt in range(KL):
                            p.op("tensor", "matmul", [mm2_pb],
                                 [kvl[kt][1], vu_b],
                                 mm2_ps[:, 0:512],
                                 kvl[kt][0][:, tt * 128:(tt + 1) * 128],
                                 vu_sl(kt, vh * 512, (vh + 1) * 512),
                                 start=(kt == 0), stop=(kt == KL - 1))
                        p.op("scalar", "copy", [ev_b], [mm2_pb],
                             ev_sb[:, 0:512], mm2_ps[:, 0:512])
                        p.dma("sync",
                              vt_lin[t0 + tt * 128:t0 + tt * 128 + 128,
                                     vh * 512:(vh + 1) * 512],
                              ev_sb[:, 0:512], [db(vt_lin)], [ev_b])

                hpt = 128 // NOPE
                for o in range(NH * NOPE // 128):
                    mm_ps, mm_pb = next_mm()
                    for kt in range(KL):
                        p.op("tensor", "matmul", [mm_pb], [kun_b, kvl[kt][1]],
                             mm_ps[:, 0:CH0],
                             kun_sl(kt, o * 128, (o + 1) * 128),
                             kvl[kt][0][:, :],
                             start=(kt == 0), stop=(kt == KL - 1))
                    p.op("scalar", "copy", [ev_b], [mm_pb],
                         ev_sb[:, 0:CH0], mm_ps[:, 0:CH0])
                    for i in range(hpt):
                        h = o * hpt + i
                        p.dma("sync",
                              k_lin[h * HD + RD:h * HD + HD, t0:t0 + CH0],
                              ev_sb[i * NOPE:(i + 1) * NOPE, 0:CH0],
                              [db(k_lin)], [ev_b])

                for o in range(NH * RD // 128):
                    mm_ps, mm_pb = next_mm()
                    for kt in range(KD):
                        p.op("tensor", "matmul", [mm_pb],
                             [krot_b, ln1_dt[kt][1]],
                             mm_ps[:, 0:CH0],
                             krot_sl(kt, o * 128, (o + 1) * 128),
                             ln1_dt[kt][0][:, :],
                             start=(kt == 0), stop=(kt == KD - 1))
                    p.op("scalar", "copy", [kr[o][1]], [mm_pb],
                         kr[o][0][:, :], mm_ps[:, 0:CH0])
                rope_rot(k_lin, kr)

                for o in range(KL):
                    mm_ps, mm_pb = next_mm()
                    for kt in range(KD):
                        p.op("tensor", "matmul", [mm_pb],
                             [qd_b, ln1_dt[kt][1]],
                             mm_ps[:, 0:CH0],
                             qd_sl(kt, o * 128, (o + 1) * 128),
                             ln1_dt[kt][0][:, :],
                             start=(kt == 0), stop=(kt == KD - 1))
                    p.op("scalar", "copy", [kvl[o][1]], [mm_pb],
                         kvl[o][0][:, :], mm_ps[:, 0:CH0])
                for o in range(NH * NOPE // 128):
                    mm_ps, mm_pb = next_mm()
                    for kt in range(KL):
                        p.op("tensor", "matmul", [mm_pb], [qun_b, kvl[kt][1]],
                             mm_ps[:, 0:CH0],
                             qun_sl(kt, o * 128, (o + 1) * 128),
                             kvl[kt][0][:, :],
                             start=(kt == 0), stop=(kt == KL - 1))
                    p.op("scalar", "copy", [ev_b], [mm_pb],
                         ev_sb[:, 0:CH0], mm_ps[:, 0:CH0])
                    for i in range(hpt):
                        h = o * hpt + i
                        p.dma("sync",
                              q_all[h * HD + RD:h * HD + HD, t0:t0 + CH0],
                              ev_sb[i * NOPE:(i + 1) * NOPE, 0:CH0],
                              [db(q_all)], [ev_b])
                for o in range(NH * RD // 128):
                    mm_ps, mm_pb = next_mm()
                    for kt in range(KL):
                        p.op("tensor", "matmul", [mm_pb],
                             [qrot_b, kvl[kt][1]],
                             mm_ps[:, 0:CH0],
                             qrot_sl(kt, o * 128, (o + 1) * 128),
                             kvl[kt][0][:, :],
                             start=(kt == 0), stop=(kt == KL - 1))
                    p.op("scalar", "copy", [kr[o][1]], [mm_pb],
                         kr[o][0][:, :], mm_ps[:, 0:CH0])
                rope_rot(q_all, kr)

                for n in range(N):
                    bi = n % 2
                    p.dma("gpsimd", brep[bi][0][:, :],
                          b_lin[n, t0:t0 + CH0].partition_broadcast(128),
                          [brep[bi][1]], [db(b_lin)])
                    p.dma("gpsimd", crep[bi][0][:, :],
                          c_lin[n, t0:t0 + CH0].partition_broadcast(128),
                          [crep[bi][1]], [db(c_lin)])
                    for j in range(KDH):
                        ab = sc_a[(n * KDH + j) % 2]
                        bb = sc_b[(n * KDH + j) % 2]
                        p.op("scalar", "activation", [ab[1]],
                             [dtc[j][1], acol_b],
                             ab[0][:, :], dtc[j][0][:, :], AF.Exp,
                             scale=acol_sb[:, n:n + 1])
                        p.op("vector", "tensor_tensor", [bb[1]],
                             [xcf[j][1], brep[bi][1]],
                             bb[0][:, :], xcf[j][0][:, :],
                             brep[bi][0][:, :], op=ALU.mult)
                        p.op("vector", "tensor_tensor_scan", [h_b],
                             [ab[1], bb[1], hst[j][1]],
                             out=h_s[:, :], data0=ab[0][:, :],
                             data1=bb[0][:, :],
                             initial=hst[j][0][:, n:n + 1],
                             op0=ALU.mult, op1=ALU.add)
                        p.op("scalar", "copy", [hst[j][1]], [h_b],
                             hst[j][0][:, n:n + 1], h_s[:, CH0 - 1:CH0])
                        if n == 0:
                            p.op("vector", "tensor_tensor", [yac[j][1]],
                                 [h_b, crep[bi][1]],
                                 yac[j][0][:, :], h_s[:, :],
                                 crep[bi][0][:, :], op=ALU.mult)
                        else:
                            p.op("vector", "tensor_tensor", [hc_b],
                                 [h_b, crep[bi][1]],
                                 hc_s[:, :], h_s[:, :], crep[bi][0][:, :],
                                 op=ALU.mult)
                            p.op("vector", "tensor_tensor", [yac[j][1]],
                                 [hc_b, yac[j][1]],
                                 yac[j][0][:, :], yac[j][0][:, :],
                                 hc_s[:, :], op=ALU.add)
                for j in range(KDH):
                    p.op("vector", "scalar_tensor_tensor", [yac[j][1]],
                         [xcf[j][1], dp_b, yac[j][1]],
                         yac[j][0][:, :], xcf[j][0][:, :],
                         dp_sb[:, j:j + 1], yac[j][0][:, :],
                         op0=ALU.mult, op1=ALU.add)
                    if debug:
                        p.dma("scalar",
                              dbg["d_y"][j * 128:(j + 1) * 128, t0:t0 + CH0],
                              yac[j][0][:, :], [db(dbg["d_y"])], [yac[j][1]])
                    p.op("vector", "tensor_tensor", [ygo_b],
                         [yac[j][1], zsil[j][1]],
                         ygo_sb[:, :], yac[j][0][:, :], zsil[j][0][:, :],
                         op=ALU.mult)
                    p._wait("sync", p._toks([ygo_b], [db(ygA), db(ygB)]))
                    dkey = p._dma_sem("sync")
                    with nc.sync.If_eq(rg_par, 0):
                        nc.sync.dma_start(
                            out=ygA[j * 128:(j + 1) * 128, t0:t0 + CH0],
                            in_=ygo_sb[:, :]).then_inc(p.sem[dkey], 16)
                    with nc.sync.Else():
                        nc.sync.dma_start(
                            out=ygB[j * 128:(j + 1) * 128, t0:t0 + CH0],
                            in_=ygo_sb[:, :]).then_inc(p.sem[dkey], 16)
                    p.cnt[dkey] += 16
                    tokd = (dkey, p.cnt[dkey])
                    p._finish([ygo_b], [db(ygA), db(ygB)], tokd)

            yg_toks = [(f"d_sync{i}", p.cnt[f"d_sync{i}"])
                       for i in range(p.NDS)]
            p._wait("sync", yg_toks + [tok_b.w])
            dkey = p._dma_sem("sync")
            with nc.sync.If_eq(rg_par, 0):
                nc.sync.dma_start(out=flA[:, :], in_=tok_sb[:, :]).then_inc(
                    p.sem[dkey], 16)
            with nc.sync.Else():
                nc.sync.dma_start(out=flB[:, :], in_=tok_sb[:, :]).then_inc(
                    p.sem[dkey], 16)
            p.cnt[dkey] += 16
            p.barrier()

        x1o = [sb(f"x1o{i}", [128, D], F32) for i in range(NTT)]
        _selholder["A"] = sb("selA", [128, D], F32)
        _selholder["B"] = sb("selB", [128, D], F32)

        with contextlib.ExitStack() as s3:
            def sb3(name, shape, dt):
                return mk_sb(s3, name, shape, dt)

            att_s0, att_s0b = mk_ps(s3, "att_s0", [128, 512])
            att_s1, att_s1b = mk_ps(s3, "att_s1", [128, 512])
            att_om = [mk_ps(s3, f"att_o{m}", [128, 65]) for m in range(QG)]
            tp3 = [mk_ps(s3, f"tp_ps3{i}", [128, 128], BF16) for i in range(2)]
            _t3 = [0]
            def next_tp3():
                t = tp3[_t3[0] % 2]
                _t3[0] += 1
                return t

            ow_sb, ow_b, ow_sl = wload(s3, "owm", ow_km, D)
            NKTmax = min(8 * (NQG - 1) + 2 * QG, NB)
            khT = [sb3(f"kh{i}", [HD, NKTmax * 128], BF16) for i in range(2)]
            vbT = [sb3(f"vb{i}", [128, 65 * NKTmax], BF16) for i in range(2)]
            qhT = [sb3(f"qh{i}", [HD, QG * 128], BF16) for i in range(2)]
            qsA_sb, qsA_b = sb3("qsA", [HD, 128], BF16)
            qsB_sb, qsB_b = sb3("qsB", [HD, 128], BF16)
            qtmp_sb, qtmp_b = sb3("qtmp", [HD, 128], F32)
            pT = [sb3(f"pT{i}", [128, QG * 128], BF16) for i in range(2)]
            ayg = [sb3(f"ayg{m}", [128, NH * HD], BF16) for m in range(QG)]
            ayT = [sb3(f"ayT{j}", [128, QG * 128], BF16) for j in range(KD)]
            recip_sb, recip_b = sb3("recip", [128, 1], F32)
            msk = [[sb3(f"msk{m}_{j2}", [128, 128], F32) for j2 in range(2)]
                   for m in range(QG)]
            for m in range(QG):
                for j2 in range(2):
                    p.dma("sync", msk[m][j2][0][:, :],
                          amask[(m * 2 + j2) * 128:(m * 2 + j2 + 1) * 128, :],
                          [msk[m][j2][1]], [db(amask)])
            for i in range(2):
                p.op("gpsimd", "memset", [vbT[i][1]], [], vbT[i][0][:, :], 1.0)

            def load_gh(g, h):
                NKT_ = min(8 * g + 2 * QG, NB)
                bi = h % 2
                kh_sb, kh_b = khT[bi]
                vb_sb, vb_b = vbT[bi]
                qh_sb, qh_b = qhT[bi]
                p.dma("sync", kh_sb[:, 0:NKT_ * 128],
                      k_lin[h * HD:(h + 1) * HD, 0:NKT_ * 128],
                      [kh_b], [db(k_lin)])
                for kt in range(NKT_):
                    p.dma("sync", vb_sb[:, kt * 65:kt * 65 + 64],
                          vt_lin[kt * 128:(kt + 1) * 128,
                                 h * HD:h * HD + 64],
                          [vb_b], [db(vt_lin)])
                for m in range(QG):
                    gb = (8 * g + 2 * m) * 128
                    p.dma("sync", qsA_sb[:, :],
                          q_all[h * HD:(h + 1) * HD, gb:gb + 128],
                          [qsA_b], [db(q_all)])
                    p.dma("sync", qsB_sb[:, :],
                          q_all[h * HD:(h + 1) * HD, gb + 128:gb + 256],
                          [qsB_b], [db(q_all)])
                    par_select(qh_sb[:, m * 128:(m + 1) * 128], qh_b,
                               qsA_sb[:, :], qsA_b, qsB_sb[:, :], qsB_b,
                               qtmp_sb[:, :], qtmp_b, rows=HD)

            load_gh(0, 0)
            for g in range(NQG):
                NKT = min(8 * g + 2 * QG, NB)
                for h in range(NH):
                    kh_sb, kh_b = khT[h % 2]
                    vb_sb, vb_b = vbT[h % 2]
                    qh_sb, qh_b = qhT[h % 2]
                    if h + 1 < NH:
                        load_gh(g, h + 1)
                    elif g + 1 < NQG:
                        load_gh(g + 1, 0)
                    QW = QG * 128
                    for kt in range(NKT):
                        sps, spb = (att_s0, att_s0b) if kt % 2 == 0 else \
                                   (att_s1, att_s1b)
                        p.op("tensor", "matmul", [spb], [kh_b, qh_b],
                             sps[:, 0:QW],
                             kh_sb[0:HD, kt * 128:(kt + 1) * 128],
                             qh_sb[0:HD, 0:QW], start=True, stop=True)
                        r = kt - 8 * g
                        if 0 <= r < 2 * QG:
                            m, j2 = r // 2, r % 2
                            p.op("vector", "tensor_tensor", [spb],
                                 [spb, msk[m][j2][1]],
                                 sps[:, m * 128:(m + 1) * 128],
                                 sps[:, m * 128:(m + 1) * 128],
                                 msk[m][j2][0][:, :], op=ALU.add)
                        pb = pT[kt % 2]
                        p.op("scalar", "activation", [pb[1]], [spb],
                             pb[0][:, 0:QW], sps[:, 0:QW], AF.Exp,
                             scale=np.float32(1.0 / np.sqrt(HD)))
                        for m in range(QG):
                            lastm = min(8 * g + 2 * m + 2, NB)
                            if kt < lastm:
                                p.op("tensor", "matmul", [att_om[m][1]],
                                     [pb[1], vb_b],
                                     att_om[m][0][:, 0:65],
                                     pb[0][:, m * 128:(m + 1) * 128],
                                     vb_sb[:, kt * 65:kt * 65 + 65],
                                     start=(kt == 0), stop=(kt == lastm - 1))
                    for m in range(QG):
                        p.op("vector", "reciprocal", [recip_b], [att_om[m][1]],
                             recip_sb[:, :], att_om[m][0][:, 64:65])
                        p.op("vector", "tensor_scalar", [ayg[m][1]],
                             [att_om[m][1], recip_b],
                             ayg[m][0][:, h * HD:h * HD + 64],
                             att_om[m][0][:, 0:64],
                             recip_sb[:, :], None, op0=ALU.mult)

                for m in range(QG):
                    for j in range(KD):
                        tp_ps3, tp_pb3 = next_tp3()
                        p.op("tensor", "transpose", [tp_pb3],
                             [ayg[m][1], ident_b],
                             tp_ps3[:, :],
                             ayg[m][0][:, j * 128:(j + 1) * 128],
                             ident_sb[:, :])
                        p.op("scalar", "copy", [ayT[j][1]], [tp_pb3],
                             ayT[j][0][:, m * 128:(m + 1) * 128],
                             tp_ps3[:, :])
                    load_x_block(QG * g + m)
                    ln_norm(xt_sb[:, :], xt_b)
                    ti = g * QG + m
                    for eh in range(D // 512):
                        mps, mpb = (att_s0, att_s0b) if eh % 2 == 0 else \
                                   (att_s1, att_s1b)
                        for kt in range(KD):
                            p.op("tensor", "matmul", [mpb],
                                 [ayT[kt][1], ow_b],
                                 mps[:, 0:512],
                                 ayT[kt][0][:, m * 128:(m + 1) * 128],
                                 ow_sl(kt, eh * 512, (eh + 1) * 512),
                                 start=(kt == 0), stop=(kt == KD - 1))
                        p.op("vector", "tensor_tensor", [x1o[ti][1]],
                             [mpb, xh_b],
                             x1o[ti][0][:, eh * 512:(eh + 1) * 512],
                             mps[:, 0:512],
                             xh_sb[:, eh * 512:(eh + 1) * 512], op=ALU.add)
                    if debug:
                        p.dma("sync", dbg["d_x1"][ti * 128:(ti + 1) * 128, :],
                              x1o[ti][0][:, :], [db(dbg["d_x1"])],
                              [x1o[ti][1]])
            p.barrier()

        with contextlib.ExitStack() as s5:
            def sb5(name, shape, dt):
                return mk_sb(s5, name, shape, dt)

            mm_ps, mm_pb = mk_ps(s5, "m5a", [128, 512])
            mm2_ps, mm2_pb = mk_ps(s5, "m5b", [128, 512])
            tp_ps5, tp_pb5 = mk_ps(s5, "tp5", [128, 128], BF16)
            rt_ps, rt_pb = mk_ps(s5, "rt5", [128, NE])

            gln2, gln2b = sb5("g_ln2", [128, D], F32)
            p.dma("sync", gln2[:, :], ln2_w.partition_broadcast(128),
                  [gln2b], [db(ln2_w)])
            rb_rep, rb_repb = sb5("rbrep", [128, NE], F32)
            p.dma("sync", rb_rep[:, :], rbias.partition_broadcast(128),
                  [rb_repb], [db(rbias)])
            rw_sb, rw_b, rw_sl = wload(s5, "rwm", router_km, NE)
            tdB5_sb, tdB5_b = sb5("tdB5", [128, D], BF16)
            xt_dt = [sb5(f"xtdt{j}", [128, TO], BF16) for j in range(KD)]
            probs_sb, probs_b = sb5("probs", [128, NE], F32)
            mskA_sb, mskA_b = sb5("mskA", [128, NE], F32)
            m1_sb, m1_b = sb5("m1c", [128, 1], F32)
            m2_sb, m2_b = sb5("m2c", [128, 1], F32)
            wgt_sb, wgt_b = sb5("wgt", [128, NE * NTT], F32)
            gchT = [sb5(f"gch{i}", [128, KD * ICH], BF16) for i in range(2)]
            uchT = [sb5(f"uch{i}", [128, KD * ICH], BF16) for i in range(2)]
            dchT = [sb5(f"dch{i}", [128, (ICH // 128) * D], BF16)
                    for i in range(2)]

            def load_moe_chunk(e, ic):
                bi = (e * NICH + ic) % 2
                c0 = e * I + ic * ICH
                gch_sb, gch_b = gchT[bi]
                uch_sb, uch_b = uchT[bi]
                dch_sb, dch_b = dchT[bi]
                p.dma("gpsimd",
                      gch_sb[:, :].rearrange("p (k c) -> p k c", c=ICH),
                      gate_km[:, c0:c0 + ICH].rearrange(
                          "(k p) c -> p k c", p=128),
                      [gch_b], [db(gate_km)])
                p.dma("gpsimd",
                      uch_sb[:, :].rearrange("p (k c) -> p k c", c=ICH),
                      up_km[:, c0:c0 + ICH].rearrange(
                          "(k p) c -> p k c", p=128),
                      [uch_b], [db(up_km)])
                p.dma("gpsimd",
                      dch_sb[:, :].rearrange("p (k c) -> p k c", c=D),
                      down_km[c0:c0 + ICH, :].rearrange(
                          "(k p) c -> p k c", p=128),
                      [dch_b], [db(down_km)])
            gsil_sb, gsil_b = sb5("gsil", [128, TF], BF16)
            gub = [sb5(f"gub{i}", [128, TO], BF16) for i in range(ICH // 128)]

            for tt in range(NTT):
                ln_norm(x1o[tt][0][:, :], x1o[tt][1])
                p.op("vector", "tensor_tensor", [tdB5_b], [xh_b, gln2b],
                     tdB5_sb[:, :], xh_sb[:, :], gln2[:, :], op=ALU.mult)
                for j in range(KD):
                    p.op("tensor", "transpose", [tp_pb5], [tdB5_b, ident_b],
                         tp_ps5[:, :], tdB5_sb[:, j * 128:(j + 1) * 128],
                         ident_sb[:, :])
                    p.op("scalar", "copy", [xt_dt[j][1]], [tp_pb5],
                         xt_dt[j][0][:, tt * 128:(tt + 1) * 128],
                         tp_ps5[:, :])

            for tt in range(NTT):
                for kt in range(KD):
                    p.op("tensor", "matmul", [rt_pb], [xt_dt[kt][1], rw_b],
                         rt_ps[:, 0:NE],
                         xt_dt[kt][0][:, tt * 128:(tt + 1) * 128],
                         rw_sl(kt, 0, NE),
                         start=(kt == 0), stop=(kt == KD - 1))
                p.op("vector", "tensor_tensor", [probs_b], [rt_pb, rb_repb],
                     probs_sb[:, :], rt_ps[:, 0:NE], rb_rep[:, :], op=ALU.add)
                p.op("scalar", "activation", [probs_b], [probs_b],
                     probs_sb[:, :], probs_sb[:, :], AF.Sigmoid)
                p.op("vector", "reduce_max", [m1_b], [probs_b],
                     m1_sb[:, :], probs_sb[:, :], axis=AX.X)
                p.op("vector", "tensor_scalar", [mskA_b], [probs_b, m1_b],
                     mskA_sb[:, :], probs_sb[:, :], m1_sb[:, :], None,
                     op0=ALU.is_equal)
                p.op("vector", "scalar_tensor_tensor", [mskA_b],
                     [mskA_b, probs_b],
                     mskA_sb[:, :], mskA_sb[:, :], NEG, probs_sb[:, :],
                     op0=ALU.mult, op1=ALU.add)
                p.op("vector", "reduce_max", [m2_b], [mskA_b],
                     m2_sb[:, :], mskA_sb[:, :], axis=AX.X)
                p.op("vector", "tensor_scalar", [mskA_b], [probs_b, m2_b],
                     mskA_sb[:, :], probs_sb[:, :], m2_sb[:, :], None,
                     op0=ALU.is_ge)
                p.op("vector", "tensor_tensor", [m1_b], [m1_b, m2_b],
                     m1_sb[:, :], m1_sb[:, :], m2_sb[:, :], op=ALU.add)
                p.op("vector", "reciprocal", [m1_b], [m1_b],
                     m1_sb[:, :], m1_sb[:, :])
                p.op("vector", "scalar_tensor_tensor", [wgt_b],
                     [probs_b, m1_b, mskA_b],
                     wgt_sb[:, tt * NE:(tt + 1) * NE], probs_sb[:, :],
                     m1_sb[:, :], mskA_sb[:, :], op0=ALU.mult, op1=ALU.mult)
                if debug:
                    p.dma("sync", dbg["d_wgt"][tt * 128:(tt + 1) * 128, :],
                          wgt_sb[:, tt * NE:(tt + 1) * NE],
                          [db(dbg["d_wgt"])], [wgt_b])

            load_moe_chunk(0, 0)
            for e in range(NE + 1):
                for ic in range(NICH):
                    bi = (e * NICH + ic) % 2
                    gch_sb, gch_b = gchT[bi]
                    uch_sb, uch_b = uchT[bi]
                    dch_sb, dch_b = dchT[bi]

                    def gch_sl(kt, c0, c1, _g=gch_sb):
                        return _g[:, kt * ICH + c0:kt * ICH + c1]

                    def uch_sl(kt, c0, c1, _u=uch_sb):
                        return _u[:, kt * ICH + c0:kt * ICH + c1]

                    def dch_sl(kt, c0, c1, _d=dch_sb):
                        return _d[:, kt * D + c0:kt * D + c1]
                    nxt = e * NICH + ic + 1
                    if nxt < (NE + 1) * NICH:
                        load_moe_chunk(nxt // NICH, nxt % NICH)
                    for it in range(ICH // 128):
                        for tf in range(NTF):
                            for kt in range(KD):
                                p.op("tensor", "matmul", [mm_pb],
                                     [gch_b, xt_dt[kt][1]],
                                     mm_ps[:, 0:TF],
                                     gch_sl(kt, it * 128, (it + 1) * 128),
                                     xt_dt[kt][0][:, tf * TF:(tf + 1) * TF],
                                     start=(kt == 0), stop=(kt == KD - 1))
                            p.op("scalar", "activation", [gsil_b], [mm_pb],
                                 gsil_sb[:, :], mm_ps[:, 0:TF], AF.Silu)
                            for kt in range(KD):
                                p.op("tensor", "matmul", [mm2_pb],
                                     [uch_b, xt_dt[kt][1]],
                                     mm2_ps[:, 0:TF],
                                     uch_sl(kt, it * 128, (it + 1) * 128),
                                     xt_dt[kt][0][:, tf * TF:(tf + 1) * TF],
                                     start=(kt == 0), stop=(kt == KD - 1))
                            p.op("vector", "tensor_tensor", [gub[it][1]],
                                 [gsil_b, mm2_pb],
                                 gub[it][0][:, tf * TF:(tf + 1) * TF],
                                 gsil_sb[:, :], mm2_ps[:, 0:TF], op=ALU.mult)
                    for tt in range(NTT):
                        for eh in range(D // 512):
                            dps, dpb = ((mm_ps, mm_pb) if (tt * 2 + eh) % 2 == 0
                                        else (mm2_ps, mm2_pb))
                            for it in range(ICH // 128):
                                p.op("tensor", "matmul", [dpb],
                                     [gub[it][1], dch_b],
                                     dps[:, 0:512],
                                     gub[it][0][:, tt * 128:(tt + 1) * 128],
                                     dch_sl(it, eh * 512, (eh + 1) * 512),
                                     start=(it == 0),
                                     stop=(it == ICH // 128 - 1))
                            scal = (wgt_sb[:, tt * NE + e:tt * NE + e + 1]
                                    if e < NE else 1.0)
                            sbuf_ins = [dpb, x1o[tt][1]]
                            if e < NE:
                                sbuf_ins.append(wgt_b)
                            p.op("vector", "scalar_tensor_tensor",
                                 [x1o[tt][1]], sbuf_ins,
                                 x1o[tt][0][:, eh * 512:(eh + 1) * 512],
                                 dps[:, 0:512], scal,
                                 x1o[tt][0][:, eh * 512:(eh + 1) * 512],
                                 op0=ALU.mult, op1=ALU.add)
            if debug:
                for tt in range(NTT):
                    p.dma("sync", dbg["d_x2"][tt * 128:(tt + 1) * 128, :],
                          x1o[tt][0][:, :], [db(dbg["d_x2"])], [x1o[tt][1]])
            p.barrier()

        with contextlib.ExitStack() as s6:
            def sb6(name, shape, dt):
                return mk_sb(s6, name, shape, dt)

            mmf0, mmf0b = mk_ps(s6, "mmf0", [128, 512])
            mmf1, mmf1b = mk_ps(s6, "mmf1", [128, 512])
            wout_sb, wout_b, wout_sl = wload(s6, "woutm", wout_km, D)
            ygk = [sb6(f"ygk{k}", [128, 128], BF16) for k in range(KD)]
            ygbA = [sb6(f"ygbA{i}", [128, KDH * 256], BF16) for i in range(2)]
            ygbB = [sb6(f"ygbB{i}", [128, KDH * 256], BF16) for i in range(2)]
            qtmp6_sb, qtmp6_b = sb6("qtmp6", [128, 128], F32)

            def load_yg(tt):
                bi = tt % 2
                gb = 2 * tt * 128
                p.dma("sync",
                      ygbA[bi][0][:, :].rearrange("p (k c) -> p k c", c=256),
                      ygA[:, gb:gb + 256].rearrange("(k p) c -> p k c",
                                                    p=128),
                      [ygbA[bi][1]], [db(ygA)], waits=[hs_tok])
                p.dma("sync",
                      ygbB[bi][0][:, :].rearrange("p (k c) -> p k c", c=256),
                      ygB[:, gb:gb + 256].rearrange("(k p) c -> p k c",
                                                    p=128),
                      [ygbB[bi][1]], [db(ygB)], waits=[hs_tok])
            sgo = [ (sb6(f"s_f{i}", [128, D], F32),
                     sb6(f"g_f{i}", [128, D], F32),
                     sb6(f"o_f{i}", [128, D], F32)) for i in range(2)]

            p._wait("sync", [tok_b.w])
            with nc.sync.register("rg_f") as rg_f, \
                 nc.sync.register("rg_t") as rg_t, \
                 nc.sync.register("rg_c") as rg_c, \
                 nc.sync.register("rg_u") as rg_u, \
                 nc.sync.register("rg_w") as rg_w:
                nc.sync.load(rg_t, tok_sb[0:1, 0:1])
                nc.sync.reg_mov(rg_f, 1)
                nc.sync.reg_mov(rg_w, 0)
                nc.sync.reg_mov(rg_c, int(S.get("poll_cap", 3000000)))
                with nc.sync.While(rg_f):
                    nc.sync.reg_add(rg_w, rg_w, 32)
                    nc.sync.dma_start(out=fl_sb[0:1, 0:1],
                                      in_=flA[:, :]).then_inc(pl_sem, 16)
                    nc.sync.dma_start(out=fl_sb[0:1, 1:2],
                                      in_=flB[:, :]).then_inc(pl_sem, 16)
                    nc.sync.wait_ge(pl_sem, rg_w)
                    nc.sync.load(rg_f, fl_sb[0:1, 0:1])
                    nc.sync.load(rg_u, fl_sb[0:1, 1:2])
                    nc.sync.reg_sub(rg_f, rg_f, rg_t)
                    nc.sync.reg_sub(rg_u, rg_u, rg_t)
                    nc.sync.reg_alu(rg_f, rg_f, rg_u, ALU.bitwise_or)
                    nc.sync.reg_sub(rg_c, rg_c, 1)
                    with nc.sync.If_eq(rg_c, 0):
                        nc.sync.reg_mov(rg_f, 0)
                nc.sync.sem_inc(p.sem["sync"], 1)
            p.cnt["sync"] += 1
            hs_tok = ("sync", p.cnt["sync"])
            fl_b.w = hs_tok
            ygA_b, ygB_b = db(ygA), db(ygB)

            load_yg(0)
            for tt in range(NTT):
                (s_sb, s_b), (g_sb, g_b), (o_sb, o_b) = sgo[tt % 2]
                bi = tt % 2
                if tt + 1 < NTT:
                    load_yg(tt + 1)
                for kt in range(KD):
                    big, bigb = (ygbA[bi] if kt < KDH else ygbB[bi])
                    r0 = (kt % KDH) * 256
                    par_select(ygk[kt][0][:, :], ygk[kt][1],
                               big[:, r0:r0 + 128], bigb,
                               big[:, r0 + 128:r0 + 256], bigb,
                               qtmp6_sb[:, :], qtmp6_b)
                load_x_block(tt)
                ln_norm(xt_sb[:, :], xt_b)
                for eh in range(D // 512):
                    mps, mpb = (mmf0, mmf0b) if eh % 2 == 0 else (mmf1, mmf1b)
                    for kt in range(KD):
                        p.op("tensor", "matmul", [mpb], [ygk[kt][1], wout_b],
                             mps[:, 0:512],
                             ygk[kt][0][:, :],
                             wout_sl(kt, eh * 512, (eh + 1) * 512),
                             start=(kt == 0), stop=(kt == KD - 1))
                    p.op("vector", "tensor_tensor", [s_b], [mpb, xh_b],
                         s_sb[:, eh * 512:(eh + 1) * 512], mps[:, 0:512],
                         xh_sb[:, eh * 512:(eh + 1) * 512], op=ALU.add)
                p.op("scalar", "activation", [g_b], [x1o[tt][1]],
                     g_sb[:, :], x1o[tt][0][:, :], AF.Silu)
                if debug:
                    p.dma("scalar", dbg["d_s"][tt * 128:(tt + 1) * 128, :],
                          s_sb[:, :], [db(dbg["d_s"])], [s_b])
                    p.dma("scalar", dbg["d_g"][tt * 128:(tt + 1) * 128, :],
                          g_sb[:, :], [db(dbg["d_g"])], [g_b])
                    for kt in range(KD):
                        p.dma("gpsimd",
                              dbg["d_yg"][kt * 128:(kt + 1) * 128,
                                          tt * 128:(tt + 1) * 128],
                              ygk[kt][0][:, :], [db(dbg["d_yg"])],
                              [ygk[kt][1]])
                p.op("vector", "tensor_tensor", [o_b], [s_b, g_b],
                     o_sb[:, :], s_sb[:, :], g_sb[:, :], op=ALU.mult)
                p.op("vector", "tensor_tensor", [o_b], [o_b, xt_b],
                     o_sb[:, :], o_sb[:, :], xt_sb[:, :], op=ALU.add)
                p.dma("sync", out_own[tt * 128:(tt + 1) * 128, :],
                      o_sb[:, :], [db(out_own)], [o_b])

            p.barrier()

    return nc



def _stage_core(inp, S, core, tok=None):
    import ml_dtypes
    T, D, N = S["T"], S["D"], S["N"]
    NH, HD, RD, L = S["NH"], S["HD"], S["RD"], S["L"]
    NE, I = S["NE"], S["I"]
    DH = D // 2
    NOB = (T // 128) // 2
    QG = min(4, NOB)
    b, p_ = core // 2, core % 2
    f32 = np.float32
    c = np.ascontiguousarray

    x = np.asarray(inp["L0_attn_x"], f32)[b]
    own = np.arange(p_ * DH, (p_ + 1) * DH)
    other = np.arange((1 - p_) * DH, (2 - p_) * DH)
    perm = np.concatenate([own, other])

    W_in = np.asarray(inp["L0_attn_W_in"], f32)
    win_rows = np.concatenate([W_in[perm], W_in[D + own]], axis=0)
    conv_w = np.asarray(inp["L0_attn_conv_w"], f32).reshape(D, 4)[perm]
    conv_b = np.asarray(inp["L0_attn_conv_b"], f32)[perm]
    W_dt = np.asarray(inp["L0_attn_W_dt"], f32)
    b_dt = np.asarray(inp["L0_attn_b_dt"], f32)
    Dp = np.asarray(inp["L0_attn_Dp"], f32)
    W_B = np.asarray(inp["L0_attn_W_B"], f32)
    W_C = np.asarray(inp["L0_attn_W_C"], f32)
    W_out = np.asarray(inp["L0_attn_W_out"], f32)

    KD = D // 128
    KDH = DH // 128
    nope_rows = np.concatenate(
        [np.arange(h * HD + RD, (h + 1) * HD) for h in range(NH)])
    RH = RD // 2
    g1 = np.concatenate([np.arange(h * RD, h * RD + RH) for h in range(NH)])
    g2 = np.concatenate([np.arange(h * RD + RH, (h + 1) * RD)
                         for h in range(NH)])
    rot_perm = np.concatenate([g1, g2])

    k_u = np.asarray(inp["L0_ffn_k_u"], f32)
    q_u = np.asarray(inp["L0_ffn_q_u"], f32)
    v_u = np.asarray(inp["L0_ffn_v_u"], f32)
    rope_q = np.asarray(inp["L0_ffn_rope_q_w"], f32)
    rope_k = np.asarray(inp["L0_ffn_rope_k_w"], f32)

    inv = (1.0 / BASE ** (np.arange(0, RD, 2) / RD)).astype(np.float64)
    tpos = np.arange(T, dtype=np.float64)
    fr = tpos[None, :] * inv[:, None]
    cos16 = np.cos(fr).astype(f32)
    sin16 = np.sin(fr).astype(f32)
    rows = np.arange(128) % RH
    cos_rep = cos16[rows]
    sin_rep = sin16[rows]

    tri = np.where(np.arange(128)[:, None] <= np.arange(128)[None, :],
                   0.0, NEG).astype(f32)
    zz = np.zeros((128, 128), f32)
    ng = np.full((128, 128), NEG, f32)
    am = np.zeros((QG, 2, 128, 128), f32)
    for m in range(QG):
        if p_ == 0:
            am[m, 0] = tri
            am[m, 1] = ng
        else:
            am[m, 0] = zz
            am[m, 1] = tri
    amask = am.reshape(QG * 2 * 128, 128)

    sg = np.asarray(inp["L0_ffn_shared_gate"], f32)
    su = np.asarray(inp["L0_ffn_shared_up"], f32)
    sd = np.asarray(inp["L0_ffn_shared_down"], f32)
    rg = np.asarray(inp["L0_ffn_routed_gate"], f32)
    ru = np.asarray(inp["L0_ffn_routed_up"], f32)
    rd_ = np.asarray(inp["L0_ffn_routed_down"], f32)
    gate_all = np.concatenate([rg, sg], axis=0)
    up_all = np.concatenate([ru, su], axis=0)
    down_all = np.concatenate([rd_, sd], axis=0)

    if tok is None:
        tok = np.array([[np.random.default_rng().integers(1, 2 ** 30)]],
                       dtype=np.int32)

    def kmaj(w):
        return c(w.T)

    d = {
        "x_full": c(x),
        "par": np.array([[p_]], dtype=np.int32),
        "pselin": np.full((128, 1), float(p_), f32),
        "pinvin": np.full((128, 1), 1.0 - p_, f32),
        "tok": tok,
        "win_km": kmaj(win_rows),
        "convw_t": c(conv_w.reshape(KD, 128, 4).transpose(1, 0, 2)
                     .reshape(128, KD * 4)),
        "convb_t": c(conv_b.reshape(KD, 128).T),
        "wdt_km": kmaj(W_dt[own][:, perm]),
        "bdt_t": c(b_dt[own].reshape(KDH, 128).T),
        "a_log": np.asarray(inp["L0_attn_A_log"], f32).reshape(1, N),
        "dp_t": c(Dp[own].reshape(KDH, 128).T),
        "wb_km": kmaj(W_B[:, perm]),
        "wc_km": kmaj(W_C[:, perm]),
        "wout_km": kmaj(W_out),
        "ln_g": np.asarray(inp["L0_attn_ln_g"], f32).reshape(1, D),
        "ln_b": np.asarray(inp["L0_attn_ln_b"], f32).reshape(1, D),
        "ln1_w": np.asarray(inp["L0_ffn_ln1_w"], f32).reshape(1, D),
        "ln2_w": np.asarray(inp["L0_ffn_ln2_w"], f32).reshape(1, D),
        "kvd_km": kmaj(np.asarray(inp["L0_ffn_kv_d"], f32)),
        "qd_km": kmaj(np.asarray(inp["L0_ffn_q_d"], f32)),
        "kun_km": kmaj(k_u[nope_rows]),
        "qun_km": kmaj(q_u[nope_rows]),
        "krot_km": kmaj(rope_k[rot_perm]),
        "qrot_km": kmaj(rope_q[rot_perm]),
        "vu_km": kmaj(v_u),
        "ow_km": kmaj(np.asarray(inp["L0_ffn_o_w"], f32)),
        "cos_rep": c(cos_rep),
        "sin_rep": c(sin_rep),
        "amask": amask,
        "router_km": kmaj(np.asarray(inp["L0_ffn_router_w"], f32)),
        "rbias": np.asarray(inp["L0_ffn_routing_bias"], f32).reshape(1, NE),
        "gate_km": c(gate_all.reshape((NE + 1) * I, D).T),
        "up_km": c(up_all.reshape((NE + 1) * I, D).T),
        "down_km": c(down_all.transpose(0, 2, 1).reshape((NE + 1) * I, D)),
        "ident_in": np.eye(128, dtype=ml_dtypes.bfloat16),
    }
    return d


_NC_CACHE = {}


def kernel(**inputs):
    x = np.asarray(inputs["L0_attn_x"])
    B, T, D = x.shape
    S = dict(SIZES)
    S["T"] = T
    S["D"] = D
    key = (T, D)
    if key not in _NC_CACHE:
        _NC_CACHE[key] = build_program(S, debug=False)
    nc = _NC_CACHE[key]
    tok = np.array([[np.random.default_rng().integers(1, 2 ** 30)]],
                   dtype=np.int32)
    in_maps = [_stage_core(inputs, S, c, tok) for c in range(8)]
    res = run_bass_kernel_spmd(nc, in_maps, core_ids=list(range(8)))
    out = np.empty((B, T, D), np.float32)
    NOB = (T // 128) // 2
    for c in range(8):
        b, p_ = c // 2, c % 2
        oo = res.results[c]["out_own"]
        for j in range(NOB):
            g = 2 * j + p_
            out[b, g * 128:(g + 1) * 128, :] = oo[j * 128:(j + 1) * 128, :]
    return out
